# version 06
# baseline (speedup 1.0000x reference)
import contextlib
import math
import numpy as np
import ml_dtypes
import concourse.bass as bass
import concourse.mybir as mybir

F32 = mybir.dt.float32
BF16 = mybir.dt.bfloat16
AF = mybir.ActivationFunctionType
ALU = mybir.AluOpType
AX = mybir.AxisListType

T = 4096
D = 1024
NT = T // 128
DC = D // 128
NEG = -1.0e30


class Buf:
    __slots__ = ("name", "last_w", "readers")

    def __init__(self, name):
        self.name = name
        self.last_w = None
        self.readers = []


class Tok:
    __slots__ = ("key", "val", "snap")

    def __init__(self, key, val, snap):
        self.key = key
        self.val = val
        self.snap = snap


class Prog:
    ENGINES = ("tensor", "vector", "scalar", "gpsimd", "sync")

    def __init__(self, nc, es):
        self.nc = nc
        self.stream = {e: [] for e in self.ENGINES}
        self.count = {e: 0 for e in self.ENGINES}
        self.dma_count = {e: 0 for e in self.ENGINES}
        self.clock = {e: {} for e in self.ENGINES}
        self.sem = {}
        self.dsem = {}
        for e in self.ENGINES:
            self.sem[e] = es.enter_context(nc.semaphore(f"sp_{e}"))
        for e in ("gpsimd", "sync", "scalar", "vector"):
            self.dsem[e] = es.enter_context(nc.semaphore(f"sd_{e}"))
        self.n_instr = 0
        self.n_waits = 0

    def _need(self, eng, deps):
        need = {}
        for t in deps:
            if t is None:
                continue
            for k, v in t.snap.items():
                if need.get(k, -1) < v:
                    need[k] = v
        ck = self.clock[eng]
        out = {k: v for k, v in need.items() if ck.get(k, 0) < v}
        return out

    def _wait(self, eng, key, val):
        kind, src = key
        if kind == "e":
            sem, target = self.sem[src], val
        else:
            sem, target = self.dsem[src], 16 * val
        self.stream[eng].append(lambda E, s=sem, t=target: E.wait_ge(s, t))
        self.n_waits += 1
        ck = self.clock[eng]
        if ck.get(key, 0) < val:
            ck[key] = val

    def emit(self, eng, fn, reads=(), writes=(), deps=(), sync=True, dma=False):
        all_deps = []
        for b in reads:
            if b.last_w is not None:
                all_deps.append(b.last_w)
        for b in writes:
            if b.last_w is not None:
                all_deps.append(b.last_w)
            all_deps.extend(b.readers)
        all_deps.extend(deps)
        need = self._need(eng, all_deps)
        for k in sorted(need):
            self._wait(eng, k, need[k])
        ck = self.clock[eng]
        for t in all_deps:
            for k, v in t.snap.items():
                if ck.get(k, 0) < v:
                    ck[k] = v

        tok = None
        if dma:
            self.dma_count[eng] += 1
            key = ("d", eng)
            val = self.dma_count[eng]
            sem = self.dsem[eng]

            def op(E, f=fn, s=sem):
                ins = f(E)
                ins.then_inc(s, 16)
            self.stream[eng].append(op)
        elif sync:
            self.count[eng] += 1
            key = ("e", eng)
            val = self.count[eng]
            sem = self.sem[eng]

            def op(E, f=fn, s=sem):
                ins = f(E)
                ins.then_inc(s, 1)
            self.stream[eng].append(op)
        else:
            self.stream[eng].append(fn)
        self.n_instr += 1
        if dma or sync:
            snap = dict(ck)
            snap[key] = val
            tok = Tok(key, val, snap)
            for b in reads:
                b.readers.append(tok)
            for b in writes:
                b.last_w = tok
                b.readers = []
        else:
            pass
        return tok

    def fence(self, bufs):
        out = []
        for b in bufs:
            if b.last_w is not None:
                out.append(b.last_w)
            out.extend(b.readers)
        return out

    def apply_fence(self, deps, engines=None):
        for eng in (engines or self.ENGINES):
            need = self._need(eng, deps)
            for k in sorted(need):
                self._wait(eng, k, need[k])
            ck = self.clock[eng]
            for t in deps:
                if t is None:
                    continue
                for k, v in t.snap.items():
                    if ck.get(k, 0) < v:
                        ck[k] = v

    def global_barrier(self):
        targets = {}
        for e in self.ENGINES:
            if self.count[e]:
                targets[("e", e)] = self.count[e]
        for e in ("gpsimd", "sync", "scalar", "vector"):
            if self.dma_count[e]:
                targets[("d", e)] = self.dma_count[e]
        for eng in self.ENGINES:
            ck = self.clock[eng]
            for k in sorted(targets):
                v = targets[k]
                if k == ("e", eng):
                    continue
                if ck.get(k, 0) < v:
                    self._wait(eng, k, v)

    def finalize_into_block(self, block):
        for eng_name in self.ENGINES:
            ops = self.stream[eng_name]
            if not ops:
                continue

            def body(E, ops=ops):
                for f in ops:
                    f(E)
            getattr(block, eng_name)(body)


def ceil_div(a, b):
    return (a + b - 1) // b


def _f(v):
    return np.float32(v) if isinstance(v, float) else v



def host_consts():
    c = {}
    c["ident"] = np.eye(128, dtype=np.float32)
    R = np.zeros((128, 128), dtype=np.float32)
    for e in range(0, 128, 2):
        R[e, e + 1] = -1.0
        R[e + 1, e] = 1.0
    c["rotT"] = R.T.copy()
    pos = np.arange(T, dtype=np.float64)[:, None]
    inv = 1.0 / (10000.0 ** (np.arange(0, D, 2, dtype=np.float64) / D))
    fr = pos * inv
    cos = np.repeat(np.cos(fr), 2, axis=-1).astype(np.float32)
    sin = np.repeat(np.sin(fr), 2, axis=-1).astype(np.float32)
    c["cos_d"] = np.ascontiguousarray(cos.T)
    c["sin_d"] = np.ascontiguousarray(sin.T)
    m = np.zeros((4, 128, 512), dtype=np.float32)
    for p in range(4):
        for b in range(4):
            if b > p:
                m[p, :, b * 128:(b + 1) * 128] = NEG
            elif b == p:
                blk = np.triu(np.full((128, 128), NEG, np.float32), k=1)
                m[p, :, b * 128:(b + 1) * 128] = blk
    c["mask4"] = m
    return c



class CoreProgram:
    def __init__(self, debug_taps=()):
        self.nc = bass.Bass("TRN2", target_bir_lowering=False)
        self.es = contextlib.ExitStack()
        self.prog = Prog(self.nc, self.es)
        self.inputs = {}
        self.dram = {}
        self.debug_taps = set(debug_taps)
        self.sbytes = 0
        self.scope_bufs = {}

    def din(self, name, shape, dtype=F32):
        t = self.nc.dram_tensor(name, list(shape), dtype, kind="ExternalInput")
        self.dram[name] = t.ap()
        return self.dram[name], Buf("d_" + name)

    def dout(self, name, shape, dtype=F32):
        t = self.nc.dram_tensor(name, list(shape), dtype, kind="ExternalOutput")
        self.dram[name] = t.ap()
        return self.dram[name], Buf("d_" + name)

    def dscratch(self, name, shape, dtype=F32):
        t = self.nc.dram_tensor(name, list(shape), dtype, kind="Internal")
        self.dram[name] = t.ap()
        return self.dram[name], Buf("d_" + name)

    def sb(self, name, shape, dtype=F32, cm=None, side=None):
        if side is None:
            side = "left" if cm is None else "right"
        h = (cm or self.es).enter_context(
            self.nc.sbuf_tensor(name, list(shape), dtype, side=side))
        itm = 4 if dtype == F32 else 2
        self.sbytes += int(np.prod(shape[1:])) * itm
        buf = Buf(name)
        self.scope_bufs.setdefault(id(cm or self.es), []).append(buf)
        import os as _os
        if _os.environ.get("MK_DEBUG_ALLOC"):
            print(f"  sb {name} {shape} {dtype} side={side} bytes/part={int(np.prod(shape[1:])) * itm}")
        return h.ap(), buf

    def ps(self, name, shape, dtype=F32, cm=None):
        h = (cm or self.es).enter_context(self.nc.psum_tensor(name, list(shape), dtype))
        return h.ap(), Buf(name)

    def dma(self, q, out_ap, in_ap, reads=(), writes=(), deps=()):
        return self.prog.emit(q, lambda E: E.dma_start(out=out_ap, in_=in_ap),
                              reads=reads, writes=writes, deps=deps, dma=True)

    def mm_group(self, ps_ap_fn, ps_buf, terms, reads, deps=()):
        n = len(terms)
        tok = None
        for i, (lt, rh) in enumerate(terms):
            st = (i == 0)
            sp = (i == n - 1)
            fn = (lambda E, lt=lt, rh=rh, st=st, sp=sp:
                  E.matmul(ps_ap_fn, lt, rh, start=st, stop=sp))
            if i < n - 1:
                self.prog.emit("tensor", fn, reads=reads, writes=(ps_buf,),
                               deps=deps, sync=False)
            else:
                tok = self.prog.emit("tensor", fn, reads=reads, writes=(ps_buf,),
                                     deps=deps, sync=True)
        return tok

    def act(self, out_ap, out_buf, in_ap, in_buf, func=AF.Identity, bias=0.0,
            scale=1.0, accum=None, accum_buf=None, extra_reads=(), deps=()):
        reads = [in_buf] + list(extra_reads)
        writes = [out_buf] + ([accum_buf] if accum_buf is not None else [])
        bias = _f(bias); scale = _f(scale)
        if accum is None:
            fn = lambda E: E.activation(out_ap, in_ap, func, bias=bias, scale=scale)
        else:
            fn = lambda E: E.activation(out_ap, in_ap, func, bias=bias, scale=scale,
                                        accum_out=accum)
        return self.prog.emit("scalar", fn, reads=reads, writes=writes, deps=deps)

    def scopy(self, out_ap, out_buf, in_ap, in_buf, deps=()):
        return self.prog.emit("scalar", lambda E: E.copy(out_ap, in_ap),
                              reads=(in_buf,), writes=(out_buf,), deps=deps)

    def tt(self, out_ap, out_buf, a_ap, a_buf, b_ap, b_buf, op, deps=()):
        return self.prog.emit("vector",
                              lambda E: E.tensor_tensor(out_ap, a_ap, b_ap, op),
                              reads=(a_buf, b_buf), writes=(out_buf,), deps=deps)

    def ts(self, out_ap, out_buf, in_ap, in_buf, s1, s2, op0, op1=None,
           s_bufs=(), deps=()):
        s1 = float(s1) if isinstance(s1, (int, np.floating)) and not hasattr(s1, "ap") else s1
        if op1 is None:
            fn = lambda E: E.tensor_scalar(out_ap, in_ap, s1, None, op0)
        else:
            fn = lambda E: E.tensor_scalar(out_ap, in_ap, s1, s2, op0, op1)
        return self.prog.emit("vector", fn, reads=(in_buf,) + tuple(s_bufs),
                              writes=(out_buf,), deps=deps)

    def stt(self, out_ap, out_buf, in0_ap, in0_buf, scalar_ap, scalar_buf,
            in1_ap, in1_buf, op0, op1, deps=()):
        fn = lambda E: E.scalar_tensor_tensor(out_ap, in0_ap, scalar_ap, in1_ap, op0, op1)
        reads = tuple(b for b in (in0_buf, scalar_buf, in1_buf) if b is not None)
        return self.prog.emit("vector", fn, reads=reads, writes=(out_buf,), deps=deps)

    def vred(self, out_ap, out_buf, in_ap, in_buf, op, deps=()):
        fn = lambda E: E.tensor_reduce(out_ap, in_ap, AX.X, op)
        return self.prog.emit("vector", fn, reads=(in_buf,), writes=(out_buf,), deps=deps)

    def vcopy(self, out_ap, out_buf, in_ap, in_buf, deps=()):
        return self.prog.emit("vector", lambda E: E.tensor_copy(out_ap, in_ap),
                              reads=(in_buf,), writes=(out_buf,), deps=deps)

    def recip(self, out_ap, out_buf, in_ap, in_buf, deps=()):
        return self.prog.emit("vector", lambda E: E.reciprocal(out_ap, in_ap),
                              reads=(in_buf,), writes=(out_buf,), deps=deps)

    def transpose(self, ps_ap, ps_buf, in_ap, in_buf, ident_ap, ident_buf, deps=()):
        fn = lambda E: E.transpose(ps_ap, in_ap, ident_ap)
        return self.prog.emit("tensor", fn, reads=(in_buf, ident_buf),
                              writes=(ps_buf,), deps=deps)

    def load_const_tile(self, name, arr_shape, dtype, q="gpsimd"):
        d_ap, d_buf = self.din(name, arr_shape, F32)
        sb_ap, sb_buf = self.sb("c_" + name, arr_shape, dtype)
        self.dma(q, sb_ap[...], d_ap[...], writes=(sb_buf,))
        return sb_ap, sb_buf

    def ln_pass(self, x_src_ap, x_src_buf, xn_dmaj, xn_dmaj_buf, ident, ident_b,
                pstr, pstr_b, es_ph, tag):
        P = self.prog
        xt, xt_b = [], []
        xc, xc_b = [], []
        xnb, xnb_b = [], []
        for k in range(2):
            a, b = self.sb(f"{tag}_xt{k}", [128, 1024], F32, es_ph); xt.append(a); xt_b.append(b)
            a, b = self.sb(f"{tag}_xc{k}", [128, 1024], F32, es_ph); xc.append(a); xc_b.append(b)
            a, b = self.sb(f"{tag}_xnb{k}", [128, 1024], BF16, es_ph); xnb.append(a); xnb_b.append(b)
        colsL, colsL_b = [], []
        for k in range(2):
            a, b = self.sb(f"{tag}_lncols{k}", [128, 8], F32, es_ph)
            colsL.append(a); colsL_b.append(b)
        for i in range(NT):
            p = i % 2
            cols, cols_b = colsL[p], colsL_b[p]
            self.dma("sync", xt[p][...], x_src_ap[i * 128:(i + 1) * 128, :],
                     reads=(x_src_buf,), writes=(xt_b[p],))
            c0 = 0
            s_ap = cols[:, c0:c0 + 1]
            self.vred(s_ap, cols_b, xt[p][...], xt_b[p], ALU.add)
            nm_ap = cols[:, c0 + 1:c0 + 2]
            self.ts(nm_ap, cols_b, s_ap, cols_b, -1.0 / D, None, ALU.mult)
            self.ts(xc[p][...], xc_b[p], xt[p][...], xt_b[p], nm_ap, None, ALU.add,
                    s_bufs=(cols_b,))
            v_ap = cols[:, c0 + 2:c0 + 3]
            self.act(xt[p][...], xt_b[p], xc[p][...], xc_b[p], AF.Square,
                     accum=v_ap, accum_buf=cols_b)
            ln_ap = cols[:, c0 + 3:c0 + 4]
            self.act(ln_ap, cols_b, v_ap, cols_b, AF.Ln, bias=1.0e-5, scale=1.0 / D)
            self.act(ln_ap, cols_b, ln_ap, cols_b, AF.Exp, scale=-0.5)
            self.ts(xnb[p][...], xnb_b[p], xc[p][...], xc_b[p], ln_ap, None, ALU.mult,
                    s_bufs=(cols_b,))
            for dc in range(DC):
                pb = dc % 2
                self.transpose(pstr[pb][...], pstr_b[pb],
                               xnb[p][:, dc * 128:(dc + 1) * 128], xnb_b[p],
                               ident, ident_b)
                self.vcopy(xn_dmaj[:, dc, i * 128:(i + 1) * 128], xn_dmaj_buf,
                           pstr[pb][...], pstr_b[pb])

    def load_tile(self, W, key, shape, dtype, cm, tag):
        d_ap, d_buf = W[key]
        s_ap, s_buf = self.sb(tag, shape, dtype, cm)
        self.dma("sync", s_ap[...], d_ap[...], reads=(d_buf,), writes=(s_buf,))
        return s_ap, s_buf

    def close_scope(self, es):
        bufs = self.scope_bufs.pop(id(es), [])
        deps = self.prog.fence(bufs)
        es.close()
        self.prog.apply_fence(deps)

    def build_layer0(self, W, x_src_ap, x_src_buf, x_dst_ap, x_dst_buf, consts,
                     debug=False):
        P = self.prog
        ident, ident_b = consts["ident"]
        rotT, rotT_b = consts["rotT"]
        psf = consts["psf"]
        pst = consts["pst"]

        if debug:
            _, self._sdump_buf = self.dout("l0_s_dump", [T, D], F32)
            g_scr_is_out = True
        else:
            g_scr_is_out = False
        esL = contextlib.ExitStack()
        with esL:
            if g_scr_is_out:
                g_scr, g_scr_b = self.dout("l0_g_dump", [T, D], F32)
            else:
                g_scr, g_scr_b = self.dscratch("l0_g_scr", [T, D], F32)
            q_scr, q_scr_b = self.dscratch("l0_q_scr", [DC, 128, T], BF16)
            k_scr, k_scr_b = self.dscratch("l0_k_scr", [DC, 128, T], BF16)

            es_xn = contextlib.ExitStack()
            with es_xn:
                xn, xn_b = self.sb("l0_xn", [128, DC, T], BF16, es_xn)
                es_ln = contextlib.ExitStack()
                with es_ln:
                    self.ln_pass(x_src_ap, x_src_buf, xn, xn_b, ident, ident_b,
                                 [p[0] for p in pst], [p[1] for p in pst], es_ln, "l0ln")
                self.close_scope(es_ln)

                es0 = contextlib.ExitStack()
                es0.__enter__()
                gates, gates_b = self.sb("l0_gates", [128, NT, 8], F32, es0)
                gates_dm, gates_dm_b = self.sb("l0_gates_dm", [8, T], BF16, es0)
                rw, rw_b = W["l0_routerW"]
                rbt, rbt_b = W["l0_router_b_t"]
                gt_tmp, gt_tmp_b = self.sb("l0_gt", [128, 4, 8], F32, es0)
                gt_bf, gt_bf_b = self.sb("l0_gtbf", [128, 8], BF16, es0)
                gcol, gcol_b = self.sb("l0_gcol", [128, 8], F32, es0)
                for i in range(NT):
                    pb = psf[i % 2]
                    terms = [(xn[:, dc, i * 128:(i + 1) * 128], rw[:, dc, :])
                             for dc in range(DC)]
                    self.mm_group(pb[0][:, 0:8], pb[1], terms, reads=(xn_b, rw_b))
                    t0 = gt_tmp[:, (i % 2) * 2, :]
                    t1 = gt_tmp[:, (i % 2) * 2 + 1, :]
                    self.tt(t0, gt_tmp_b, pb[0][:, 0:8], pb[1], rbt[...], rbt_b, ALU.add)
                    c0 = (i % 2) * 4
                    m_ap = gcol[:, c0:c0 + 1]
                    self.vred(m_ap, gcol_b, t0, gt_tmp_b, ALU.max)
                    nm = gcol[:, c0 + 1:c0 + 2]
                    self.ts(nm, gcol_b, m_ap, gcol_b, -1.0, None, ALU.mult)
                    l_ap = gcol[:, c0 + 2:c0 + 3]
                    self.act(t1, gt_tmp_b, t0, gt_tmp_b, AF.Exp, bias=nm, scale=1.0,
                             accum=l_ap, accum_buf=gcol_b, extra_reads=(gcol_b,))
                    r_ap = gcol[:, c0 + 3:c0 + 4]
                    self.recip(r_ap, gcol_b, l_ap, gcol_b)
                    self.ts(t1, gt_tmp_b, t1, gt_tmp_b, r_ap, None, ALU.mult,
                            s_bufs=(gcol_b,))
                    self.vred(m_ap, gcol_b, t1, gt_tmp_b, ALU.max)
                    self.ts(gates[:, i, :], gates_b, t1, gt_tmp_b, m_ap, None, ALU.is_ge,
                            s_bufs=(gcol_b,))
                    self.tt(gates[:, i, :], gates_b, gates[:, i, :], gates_b, t1,
                            gt_tmp_b, ALU.mult)
                    self.vcopy(gt_bf[...], gt_bf_b, gates[:, i, :], gates_b)
                    self.transpose(pst[0][0][0:8, :], pst[0][1], gt_bf[...], gt_bf_b,
                                   ident, ident_b)
                    self.vcopy(gates_dm[:, i * 128:(i + 1) * 128], gates_dm_b,
                               pst[0][0][0:8, :], pst[0][1])

                b2sb, b2sb_b = W["l0_b2"]
                b1sb, b1sb_b = W["l0_b1"]
                g_sb, g_sb_b = self.sb("l0_gq", [128, 8, 1024], F32, es0)
                w1u, w1u_b, w2u, w2u_b, h_sb, h_sb_b = [], [], [], [], [], []
                for k in range(2):
                    a, b = self.sb(f"l0_w1u{k}", [128, 8, 512], BF16, es0)
                    w1u.append(a); w1u_b.append(b)
                    a, b = self.sb(f"l0_w2u{k}", [128, 4, 1024], BF16, es0)
                    w2u.append(a); w2u_b.append(b)
                    a, b = self.sb(f"l0_h{k}", [128, 4, 512], BF16, es0)
                    h_sb.append(a); h_sb_b.append(b)
                gsil, gsil_b = self.sb("l0_gsil", [128, 2, 1024], F32, es0)
                W1d, W1d_b = W["l0_W1_dram"]
                W2d, W2d_b = W["l0_W2_dram"]

                for quarter in range(4):
                    tq0 = quarter * 8
                    for ti in range(8):
                        it = tq0 + ti
                        pb = psf[4 + (ti % 2)]
                        for dh in range(2):
                            self.mm_group(
                                pb[0][...], pb[1],
                                [(gates_dm[:, it * 128:(it + 1) * 128],
                                  b2sb[:, dh * 512:(dh + 1) * 512])],
                                reads=(gates_dm_b, b2sb_b))
                            self.act(g_sb[:, ti, dh * 512:(dh + 1) * 512], g_sb_b,
                                     pb[0][...], pb[1], AF.Identity)
                    for e in range(8):
                        for fb in range(8):
                            u = e * 8 + fb
                            pw = u % 2
                            self.dma("gpsimd", w1u[pw][...],
                                     W1d[e, :, fb * 512:(fb + 1) * 512]
                                     .rearrange("(dc p) f -> p dc f", p=128),
                                     reads=(W1d_b,), writes=(w1u_b[pw],))
                            self.dma("gpsimd", w2u[pw][...],
                                     W2d[e, fb * 512:(fb + 1) * 512, :]
                                     .rearrange("(fc p) d -> p fc d", p=128),
                                     reads=(W2d_b,), writes=(w2u_b[pw],))
                            for tc in range(2):
                                t0c = tq0 * 128 + tc * 512
                                ph = tc % 2
                                for ft in range(4):
                                    pb = psf[ft % 4]
                                    terms = [(w1u[pw][:, dc, ft * 128:(ft + 1) * 128],
                                              xn[:, dc, t0c:t0c + 512])
                                             for dc in range(DC)]
                                    self.mm_group(pb[0][...], pb[1], terms,
                                                  reads=(w1u_b[pw], xn_b))
                                    self.act(h_sb[ph][:, ft, :], h_sb_b[ph],
                                             pb[0][...], pb[1], AF.Relu,
                                             bias=b1sb[:, e, fb * 4 + ft:fb * 4 + ft + 1],
                                             extra_reads=(b1sb_b,))
                            for tc in range(2):
                                t0c = tq0 * 128 + tc * 512
                                ph = tc % 2
                                for ti in range(4):
                                    itt = (t0c // 128) + ti
                                    gq_ti = itt - tq0
                                    for dh in range(2):
                                        pb = psf[4 + ((ti * 2 + dh) % 2)]
                                        terms = [(h_sb[ph][:, fc, ti * 128:(ti + 1) * 128],
                                                  w2u[pw][:, fc, dh * 512:(dh + 1) * 512])
                                                 for fc in range(4)]
                                        self.mm_group(pb[0][...], pb[1], terms,
                                                      reads=(h_sb_b[ph], w2u_b[pw]))
                                        self.stt(g_sb[:, gq_ti, dh * 512:(dh + 1) * 512],
                                                 g_sb_b, pb[0][...], pb[1],
                                                 gates[:, itt, e:e + 1], gates_b,
                                                 g_sb[:, gq_ti, dh * 512:(dh + 1) * 512],
                                                 g_sb_b, ALU.mult, ALU.add)
                    for ti in range(8):
                        it = tq0 + ti
                        self.act(gsil[:, ti % 2, :], gsil_b, g_sb[:, ti, :], g_sb_b,
                                 AF.Silu)
                        self.dma("sync", g_scr[it * 128:(it + 1) * 128, :],
                                 gsil[:, ti % 2, :], reads=(gsil_b,), writes=(g_scr_b,))

                self.close_scope(es0)

                cosd, cosd_b = W["cos_d"]
                sind, sind_b = W["sin_d"]
                esB = contextlib.ExitStack()
                if True:
                    wT, wT_b = [], []
                    for k in range(2):
                        a, b = self.sb(f"l0_wT{k}", [128, DC, D], BF16, esB)
                        wT.append(a); wT_b.append(b)
                    csc, csc_b = self.sb("l0_csc", [128, DC, 256], F32, esB)
                    css, css_b = self.sb("l0_css", [128, DC, 256], F32, esB)
                    tA, tA_b = self.sb("l0_tA", [128, DC, 256], F32, esB)
                    qbf, qbf_b = self.sb("l0_qbf", [128, 2, DC, 256], BF16, esB)
                    tB, tB_b = self.sb("l0_tB", [128, 2, 256], F32, esB)
                    self.dma("gpsimd", wT[0][...],
                             W["l0_wqT"][0].rearrange("(dc p) e -> p dc e", p=128),
                             reads=(W["l0_wqT"][1],), writes=(wT_b[0],))
                    self.dma("gpsimd", wT[1][...],
                             W["l0_wkT"][0].rearrange("(dc p) e -> p dc e", p=128),
                             reads=(W["l0_wkT"][1],), writes=(wT_b[1],))
                    for tc in range(T // 256):
                        t0c = tc * 256
                        for dc in range(DC):
                            self.dma("scalar", csc[:, dc, :],
                                     cosd[dc * 128:(dc + 1) * 128, t0c:t0c + 256],
                                     reads=(cosd_b,), writes=(csc_b,))
                            self.dma("scalar", css[:, dc, :],
                                     sind[dc * 128:(dc + 1) * 128, t0c:t0c + 256],
                                     reads=(sind_b,), writes=(css_b,))
                        for wi, dst_scr, dst_scr_b in ((0, q_scr, q_scr_b),
                                                       (1, k_scr, k_scr_b)):
                            for dc in range(DC):
                                pb = psf[dc % 4]
                                terms = [(wT[wi][:, dci, dc * 128:(dc + 1) * 128],
                                          xn[:, dci, t0c:t0c + 256]) for dci in range(DC)]
                                self.mm_group(pb[0][:, 0:256], pb[1], terms,
                                              reads=(wT_b[wi], xn_b))
                                self.act(tA[:, dc, :], tA_b, pb[0][:, 0:256], pb[1],
                                         AF.Identity)
                                self.vcopy(qbf[:, wi, dc, :], qbf_b, tA[:, dc, :],
                                           tA_b)
                            for dc in range(DC):
                                pr = psf[4 + dc % 2]
                                self.mm_group(pr[0][:, 0:256], pr[1],
                                              [(rotT, qbf[:, wi, dc, :])],
                                              reads=(rotT_b, qbf_b))
                                t1 = tB[:, 0, :]
                                t2 = tB[:, 1, :]
                                self.tt(t1, tB_b, tA[:, dc, :], tA_b,
                                        csc[:, dc, :], csc_b, ALU.mult)
                                self.tt(t2, tB_b, pr[0][:, 0:256], pr[1],
                                        css[:, dc, :], css_b, ALU.mult)
                                qo = qbf[:, wi, dc, :]
                                self.tt(qo, qbf_b, t1, tB_b, t2, tB_b, ALU.add)
                                self.dma("sync", dst_scr[dc, :, t0c:t0c + 256], qo,
                                         reads=(qbf_b,), writes=(dst_scr_b,))
                    v_tm, v_tm_b = self.sb("l0_v", [128, NT, D], BF16, esL, side="left")
                    wd, wd_b = W["l0_wvT"]
                    self.dma("gpsimd", wT[0][...],
                             wd.rearrange("(dc p) e -> p dc e", p=128),
                             reads=(wd_b,), writes=(wT_b[0],))
                    for it in range(NT):
                        for eh in range(2):
                            pb = psf[4 + eh]
                            terms = [(xn[:, dci, it * 128:(it + 1) * 128],
                                      wT[0][:, dci, eh * 512:(eh + 1) * 512])
                                     for dci in range(DC)]
                            self.mm_group(pb[0][...], pb[1], terms, reads=(xn_b, wT_b[0]))
                            self.act(v_tm[:, it, eh * 512:(eh + 1) * 512], v_tm_b,
                                     pb[0][...], pb[1], AF.Identity)
                    pass
                self.close_scope(esB)
            self.close_scope(es_xn)

            esC = contextlib.ExitStack()
            if True:
                kd, kd_b = self.sb("l0_k", [128, DC, T], BF16, esC)
                self.dma("sync", kd[...], k_scr.rearrange("dc p t -> p dc t"),
                         reads=(k_scr_b,), writes=(kd_b,))
                mask, mask_b = self.sb("l0_mask", [128, 4, 512], F32, esC)
                self.dma("sync", mask[...], W["mask4"][0].rearrange("v p f -> p v f"),
                         reads=(W["mask4"][1],), writes=(mask_b,))
                qt, qt_b = [], []
                for k in range(2):
                    a, b = self.sb(f"l0_qt{k}", [128, DC, 128], BF16, esC)
                    qt.append(a); qt_b.append(b)
                PsbL, PsbL_b = [], []
                for k in range(2):
                    a, b = self.sb(f"l0_P{k}", [128, T], BF16, esC)
                    PsbL.append(a); PsbL_b.append(b)
                Pt, Pt_b = [], []
                for k in range(4):
                    a, b = self.sb(f"l0_Pt{k}", [128, 128], BF16, esC)
                    Pt.append(a); Pt_b.append(b)
                yaccL, yaccL_b = [], []
                for k in range(2):
                    a, b = self.sb(f"l0_yacc{k}", [128, D], F32, esC)
                    yaccL.append(a); yaccL_b.append(b)
                xg, xg_b = [], []
                for k in range(4):
                    a, b = self.sb(f"l0_xg{k}", [128, D], F32, esC)
                    xg.append(a); xg_b.append(b)
                xo, xo_b = [], []
                for k in range(2):
                    a, b = self.sb(f"l0_xo{k}", [128, D], F32, esC)
                    xo.append(a); xo_b.append(b)
                mlsL, mlsL_b = [], []
                for k in range(2):
                    a, b = self.sb(f"l0_mls{k}", [128, 24], F32, esC)
                    mlsL.append(a); mlsL_b.append(b)

                def softmax0(it):
                    p = it % 2
                    cols, cols_b = mlsL[p], mlsL_b[p]
                    Psb, Psb_b = PsbL[p], PsbL_b[p]
                    jext = (it + 1) * 128
                    n_ch = ceil_div(jext, 512)
                    m_run = cols[:, 0:1]
                    bias_c = cols[:, 3:4]
                    lfin = cols[:, 4:5]
                    rec = cols[:, 5:6]
                    tmp = cols[:, 6:7]
                    corr = cols[:, 7:8]
                    self.dma("sync", qt[p][...],
                             q_scr[:, :, it * 128:(it + 1) * 128]
                             .rearrange("dc p t -> p dc t"),
                             reads=(q_scr_b,), writes=(qt_b[p],))
                    self.dma("sync", xg[p][...], x_src_ap[it * 128:(it + 1) * 128, :],
                             reads=(x_src_buf,), writes=(xg_b[p],))
                    self.dma("sync", xg[2 + p][...], g_scr[it * 128:(it + 1) * 128, :],
                             reads=(g_scr_b,), writes=(xg_b[2 + p],))
                    n_grp = ceil_div(n_ch, 4)
                    for g0 in range(n_grp):
                        ch0, ch1 = g0 * 4, min(n_ch, g0 * 4 + 4)
                        for ch in range(ch0, ch1):
                            c0 = ch * 512
                            cw = min(512, jext - c0)
                            pb = psf[ch - ch0]
                            terms = [(qt[p][:, dc, :], kd[:, dc, c0:c0 + cw])
                                     for dc in range(DC)]
                            self.mm_group(pb[0][:, 0:cw], pb[1], terms,
                                          reads=(qt_b[p], kd_b))
                            if c0 <= it * 128 < c0 + 512:
                                self.tt(pb[0][:, 0:cw], pb[1], pb[0][:, 0:cw], pb[1],
                                        mask[:, it % 4, 0:cw], mask_b, ALU.add)
                            self.vred(cols[:, 8 + ch - ch0:9 + ch - ch0], cols_b,
                                      pb[0][:, 0:cw], pb[1], ALU.max)
                        ng = ch1 - ch0
                        if ng == 1:
                            m_g = cols[:, 8:9]
                        else:
                            m_g = cols[:, 2:3]
                            self.vred(m_g, cols_b, cols[:, 8:8 + ng], cols_b, ALU.max)
                        if g0 == 0:
                            self.vcopy(m_run, cols_b, m_g, cols_b)
                        else:
                            self.tt(tmp, cols_b, m_run, cols_b, m_g, cols_b, ALU.max)
                            self.tt(corr, cols_b, m_run, cols_b, tmp, cols_b,
                                    ALU.subtract)
                            self.act(corr, cols_b, corr, cols_b, AF.Exp, scale=1.0 / 32.0)
                            self.ts(Psb[:, 0:ch0 * 512], Psb_b, Psb[:, 0:ch0 * 512],
                                    Psb_b, corr, None, ALU.mult, s_bufs=(cols_b,))
                            self.ts(lfin, cols_b, lfin, cols_b, corr, None, ALU.mult,
                                    s_bufs=(cols_b,))
                            self.vcopy(m_run, cols_b, tmp, cols_b)
                        self.ts(bias_c, cols_b, m_run, cols_b, -1.0 / 32.0, None, ALU.mult)
                        for ch in range(ch0, ch1):
                            c0 = ch * 512
                            cw = min(512, jext - c0)
                            pb = psf[ch - ch0]
                            self.act(Psb[:, c0:c0 + cw], Psb_b, pb[0][:, 0:cw], pb[1],
                                     AF.Exp, bias=bias_c, scale=1.0 / 32.0,
                                     accum=cols[:, 16 + ch - ch0:17 + ch - ch0],
                                     accum_buf=cols_b, extra_reads=(cols_b,))
                        lg = cols[:, 1:2]
                        if ng == 1:
                            self.vcopy(lg, cols_b, cols[:, 16:17], cols_b)
                        else:
                            self.vred(lg, cols_b, cols[:, 16:16 + ng], cols_b, ALU.add)
                        if g0 == 0:
                            self.vcopy(lfin, cols_b, lg, cols_b)
                        else:
                            self.tt(lfin, cols_b, lfin, cols_b, lg, cols_b, ALU.add)
                    self.recip(rec, cols_b, lfin, cols_b)
                    self.ts(Psb[:, 0:jext], Psb_b, Psb[:, 0:jext], Psb_b, rec, None,
                            ALU.mult, s_bufs=(cols_b,))

                def pv0(it):
                    p = it % 2
                    Psb, Psb_b = PsbL[p], PsbL_b[p]
                    jext = (it + 1) * 128
                    n_jt = jext // 128
                    for jj in range(n_jt):
                        tp = 2 * p + jj % 2
                        self.transpose(pst[jj % 2][0][...], pst[jj % 2][1],
                                       Psb[:, jj * 128:(jj + 1) * 128], Psb_b,
                                       ident, ident_b)
                        self.vcopy(Pt[tp][...], Pt_b[tp], pst[jj % 2][0][...],
                                   pst[jj % 2][1])
                        for eh in range(2):
                            pb = psf[4 + eh]
                            fn = (lambda E, lt=Pt[tp][...],
                                  rh=v_tm[:, jj, eh * 512:(eh + 1) * 512],
                                  o=pb[0][...], st=(jj == 0), sp=(jj == n_jt - 1):
                                  E.matmul(o, lt, rh, start=st, stop=sp))
                            self.prog.emit("tensor", fn, reads=(Pt_b[tp], v_tm_b),
                                           writes=(pb[1],), sync=(jj == n_jt - 1))
                    for eh in range(2):
                        pb = psf[4 + eh]
                        sl = slice(eh * 512, (eh + 1) * 512)
                        self.tt(xo[p][:, sl], xo_b[p], pb[0][...], pb[1],
                                xg[2 + p][:, sl], xg_b[2 + p], ALU.mult)
                        self.tt(xo[p][:, sl], xo_b[p], xo[p][:, sl], xo_b[p],
                                xg[p][:, sl], xg_b[p], ALU.add)
                    self.dma("sync", x_dst_ap[it * 128:(it + 1) * 128, :], xo[p][...],
                             reads=(xo_b[p],), writes=(x_dst_buf,))

                for it in range(NT):
                    softmax0(it)
                    if it >= 1:
                        pv0(it - 1)
                pv0(NT - 1)
                pass
            self.close_scope(esC)

    def setup_consts(self):
        es = self.es
        consts = {}
        c = host_consts()
        a, b = self.load_const_tile("c_ident", [128, 128], BF16)
        consts["ident"] = (a[...], b)
        self.inputs["c_ident"] = c["ident"]
        a, b = self.load_const_tile("c_rotT", [128, 128], BF16)
        consts["rotT"] = (a[...], b)
        self.inputs["c_rotT"] = c["rotT"]
        a, b = self.load_const_tile("c_identf", [128, 128], F32)
        consts["identf"] = (a[...], b)
        self.inputs["c_identf"] = c["ident"]
        psf = []
        for i in range(6):
            ap, bb = self.ps(f"psf{i}", [128, 512], F32)
            psf.append((ap, bb))
        pstl = []
        for i in range(2):
            ap, bb = self.ps(f"pst{i}", [128, 128], BF16)
            pstl.append((ap, bb))
        consts["psf"] = psf
        consts["pst"] = pstl
        return consts, c

    def final_drain(self):
        P = self.prog
        for q in ("sync", "gpsimd", "scalar", "vector"):
            n = P.dma_count[q]
            if n:
                P.stream["sync"].append(
                    lambda E, s=P.dsem[q], t=16 * n: E.wait_ge(s, t))

    def finalize(self):
        self.final_drain()
        blk = self.es.enter_context(self.nc.Block())
        self.prog.finalize_into_block(blk)
        self.es.close()
        return self.nc


    def ln_pass_dual(self, x_src_ap, x_src_buf, xn_dmaj, xn_dmaj_buf,
                     xn_scr, xn_scr_buf, xn2_scr, xn2_scr_buf,
                     gam_ap, gam_b, bet_ap, bet_b,
                     ident, ident_b, pst, es_ph, tag):
        xt, xt_b, xc, xc_b, xnb, xnb_b, x2b, x2b_b, x2f, x2f_b = [], [], [], [], [], [], [], [], [], []
        for k in range(2):
            a, b = self.sb(f"{tag}_xt{k}", [128, 1024], F32, es_ph); xt.append(a); xt_b.append(b)
            a, b = self.sb(f"{tag}_xc{k}", [128, 1024], F32, es_ph); xc.append(a); xc_b.append(b)
            a, b = self.sb(f"{tag}_xnb{k}", [128, 1024], BF16, es_ph); xnb.append(a); xnb_b.append(b)
            a, b = self.sb(f"{tag}_x2b{k}", [128, 1024], BF16, es_ph); x2b.append(a); x2b_b.append(b)
            a, b = self.sb(f"{tag}_x2f{k}", [128, 1024], F32, es_ph); x2f.append(a); x2f_b.append(b)
        xnf, xnf_b = [], []
        for k in range(2):
            a, b = self.sb(f"{tag}_xnf{k}", [128, 1024], F32, es_ph); xnf.append(a); xnf_b.append(b)
        x2st, x2st_b = [], []
        for k in range(2):
            a, b = self.sb(f"{tag}_x2st{k}", [128, DC, 128], BF16, es_ph); x2st.append(a); x2st_b.append(b)
        colsL, colsL_b = [], []
        for k in range(2):
            a, b = self.sb(f"{tag}_cols{k}", [128, 16], F32, es_ph)
            colsL.append(a); colsL_b.append(b)
        for i in range(NT):
            p = i % 2
            cols, cols_b = colsL[p], colsL_b[p]
            self.dma("sync", xt[p][...], x_src_ap[i * 128:(i + 1) * 128, :],
                     reads=(x_src_buf,), writes=(xt_b[p],))
            c0 = 0
            s_ap = cols[:, c0:c0 + 1]
            self.vred(s_ap, cols_b, xt[p][...], xt_b[p], ALU.add)
            nm_ap = cols[:, c0 + 1:c0 + 2]
            self.ts(nm_ap, cols_b, s_ap, cols_b, -1.0 / D, None, ALU.mult)
            self.ts(xc[p][...], xc_b[p], xt[p][...], xt_b[p], nm_ap, None, ALU.add,
                    s_bufs=(cols_b,))
            v_ap = cols[:, c0 + 2:c0 + 3]
            self.act(xt[p][...], xt_b[p], xc[p][...], xc_b[p], AF.Square,
                     accum=v_ap, accum_buf=cols_b)
            r1 = cols[:, c0 + 3:c0 + 4]
            self.act(r1, cols_b, v_ap, cols_b, AF.Ln, bias=1.0e-5, scale=1.0 / D)
            self.act(r1, cols_b, r1, cols_b, AF.Exp, scale=-0.5)
            w_ap = cols[:, c0 + 4:c0 + 5]
            self.ts(w_ap, cols_b, v_ap, cols_b, r1, None, ALU.mult, s_bufs=(cols_b,))
            self.ts(w_ap, cols_b, w_ap, cols_b, r1, 1.0 / D, ALU.mult, ALU.mult,
                    s_bufs=(cols_b,))
            r2 = cols[:, c0 + 5:c0 + 6]
            self.act(r2, cols_b, w_ap, cols_b, AF.Ln, bias=1.0e-5)
            self.act(r2, cols_b, r2, cols_b, AF.Exp, scale=-0.5)
            r12 = cols[:, c0 + 6:c0 + 7]
            self.ts(r12, cols_b, r2, cols_b, r1, None, ALU.mult, s_bufs=(cols_b,))
            self.ts(xnf[p][...], xnf_b[p], xc[p][...], xc_b[p], r1, None, ALU.mult,
                    s_bufs=(cols_b,))
            self.dma("sync", xn_scr[i * 128:(i + 1) * 128, :], xnf[p][...],
                     reads=(xnf_b[p],), writes=(xn_scr_buf,))
            self.vcopy(xnb[p][...], xnb_b[p], xnf[p][...], xnf_b[p])
            for dc in range(DC):
                pb = dc % 2
                self.transpose(pst[pb][0][...], pst[pb][1],
                               xnb[p][:, dc * 128:(dc + 1) * 128], xnb_b[p],
                               ident, ident_b)
                self.vcopy(xn_dmaj[:, dc, i * 128:(i + 1) * 128], xn_dmaj_buf,
                           pst[pb][0][...], pst[pb][1])
            self.ts(x2f[p][...], x2f_b[p], xc[p][...], xc_b[p], r12, None, ALU.mult,
                    s_bufs=(cols_b,))
            self.tt(x2f[p][...], x2f_b[p], x2f[p][...], x2f_b[p], gam_ap, gam_b, ALU.mult)
            self.tt(x2f[p][...], x2f_b[p], x2f[p][...], x2f_b[p], bet_ap, bet_b, ALU.add)
            self.vcopy(x2b[p][...], x2b_b[p], x2f[p][...], x2f_b[p])
            for dc in range(DC):
                pb = dc % 2
                self.transpose(pst[pb][0][...], pst[pb][1],
                               x2b[p][:, dc * 128:(dc + 1) * 128], x2b_b[p],
                               ident, ident_b)
                self.vcopy(x2st[p][:, dc, :], x2st_b[p], pst[pb][0][...], pst[pb][1])
            self.dma("sync", xn2_scr[:, :, i * 128:(i + 1) * 128]
                     .rearrange("dc p t -> p dc t"), x2st[p][...],
                     reads=(x2st_b[p],), writes=(xn2_scr_buf,))

    def build_layer1(self, W, x_src_ap, x_src_buf, x_dst_ap, x_dst_buf, consts,
                     debug=False):
        P = self.prog
        ident, ident_b = consts["ident"]
        psf = consts["psf"]
        pst = consts["pst"]

        if debug:
            _, self._sdump_buf1 = self.dout("l1_s_dump", [T, D], F32)
        esL = contextlib.ExitStack()
        with esL:
            if debug:
                g_scr, g_scr_b = self.dout("l1_g_dump", [T, D], F32)
            else:
                g_scr, g_scr_b = self.dscratch("l1_g_scr", [T, D], F32)
            xn_scr, xn_scr_b = self.dscratch("l1_xn_scr", [T, D], F32)
            xn2_scr, xn2_scr_b = self.dscratch("l1_xn2_scr", [DC, 128, T], BF16)
            q_scr, q_scr_b = self.dscratch("l1_q_scr", [DC, 128, T], BF16)
            k_scr, k_scr_b = self.dscratch("l1_k_scr", [DC, 128, T], BF16)

            es_xn = contextlib.ExitStack()
            with es_xn:
                xn, xn_b = self.sb("l1_xn", [128, DC, T], BF16, es_xn)
                es_ln = contextlib.ExitStack()
                with es_ln:
                    gam1, gam1_b = self.load_tile(W, "l1_gam_d", [128, 1024], F32, es_ln, "l1gamt")
                    bet1, bet1_b = self.load_tile(W, "l1_bet_d", [128, 1024], F32, es_ln, "l1bett")
                    self.ln_pass_dual(x_src_ap, x_src_buf, xn, xn_b,
                                      xn_scr, xn_scr_b, xn2_scr, xn2_scr_b,
                                      gam1[...], gam1_b, bet1[...], bet1_b,
                                      ident, ident_b, pst, es_ln, "l1ln")
                self.close_scope(es_ln)

                esF = contextlib.ExitStack()
                if True:
                    y_sb, y_sb_b = self.sb("l1_yq", [128, 8, 1024], F32, esF)
                    w1u, w1u_b, w2u, w2u_b, h_sb, h_sb_b = [], [], [], [], [], []
                    for k in range(2):
                        a, b = self.sb(f"l1_w1u{k}", [128, 8, 512], BF16, esF)
                        w1u.append(a); w1u_b.append(b)
                        a, b = self.sb(f"l1_w2u{k}", [128, 4, 1024], BF16, esF)
                        w2u.append(a); w2u_b.append(b)
                        a, b = self.sb(f"l1_h{k}", [128, 4, 512], BF16, esF)
                        h_sb.append(a); h_sb_b.append(b)
                    hrel, hrel_b = self.sb("l1_hrel", [128, 512], F32, esF)
                    mem_sb, mem_sb_b = self.sb("l1_mem", [128, 2, 1024], F32, esF)
                    spk_sb, spk_sb_b = self.sb("l1_spk", [128, 2, 1024], F32, esF)
                    sum_sb, sum_sb_b = self.sb("l1_ssum", [128, 2, 1024], F32, esF)
                    gsil, gsil_b = self.sb("l1_gsil", [128, 2, 1024], F32, esF)
                    WfcT, WfcT_b = W["l1_WfcT_dram"]
                    WprT, WprT_b = W["l1_WprT_dram"]
                    NFB = 8
                    for quarter in range(4):
                        tq0 = quarter * 8
                        for fb in range(NFB):
                            pw = fb % 2
                            self.dma("gpsimd", w1u[pw][...],
                                     WfcT[:, fb * 512:(fb + 1) * 512]
                                     .rearrange("(dc p) f -> p dc f", p=128),
                                     reads=(WfcT_b,), writes=(w1u_b[pw],))
                            self.dma("gpsimd", w2u[pw][...],
                                     WprT[fb * 512:(fb + 1) * 512, :]
                                     .rearrange("(fc p) d -> p fc d", p=128),
                                     reads=(WprT_b,), writes=(w2u_b[pw],))
                            for tc in range(2):
                                t0c = tq0 * 128 + tc * 512
                                ph = tc % 2
                                for ft in range(4):
                                    pb = psf[ft % 4]
                                    terms = [(w1u[pw][:, dc, ft * 128:(ft + 1) * 128],
                                              xn[:, dc, t0c:t0c + 512])
                                             for dc in range(DC)]
                                    self.mm_group(pb[0][...], pb[1], terms,
                                                  reads=(w1u_b[pw], xn_b))
                                    self.ts(hrel[...], hrel_b, pb[0][...], pb[1],
                                            0.0, None, ALU.max)
                                    self.act(h_sb[ph][:, ft, :], h_sb_b[ph],
                                             hrel[...], hrel_b, AF.Square)
                            for tc in range(2):
                                t0c = tq0 * 128 + tc * 512
                                ph = tc % 2
                                for ti in range(4):
                                    itt = (t0c // 128) + ti
                                    yq_ti = itt - tq0
                                    for dh in range(2):
                                        pb = psf[4 + ((ti * 2 + dh) % 2)]
                                        terms = [(h_sb[ph][:, fc, ti * 128:(ti + 1) * 128],
                                                  w2u[pw][:, fc, dh * 512:(dh + 1) * 512])
                                                 for fc in range(4)]
                                        self.mm_group(pb[0][...], pb[1], terms,
                                                      reads=(h_sb_b[ph], w2u_b[pw]))
                                        dst = y_sb[:, yq_ti, dh * 512:(dh + 1) * 512]
                                        if fb == 0:
                                            self.act(dst, y_sb_b, pb[0][...], pb[1],
                                                     AF.Identity)
                                        else:
                                            self.tt(dst, y_sb_b, dst, y_sb_b,
                                                    pb[0][...], pb[1], ALU.add)
                        for ti in range(8):
                            it = tq0 + ti
                            y_ap = y_sb[:, ti, :]
                            mem = mem_sb[:, ti % 2, :]
                            spk = spk_sb[:, ti % 2, :]
                            ssum = sum_sb[:, ti % 2, :]
                            self.ts(spk, spk_sb_b, y_ap, y_sb_b, 1.0, None, ALU.is_gt)
                            self.vcopy(mem, mem_sb_b, y_ap, y_sb_b)
                            self.vcopy(ssum, sum_sb_b, spk, spk_sb_b)
                            for step in range(1, 5):
                                self.ts(mem, mem_sb_b, mem, mem_sb_b, 0.9, None, ALU.mult)
                                self.tt(mem, mem_sb_b, mem, mem_sb_b, y_ap, y_sb_b, ALU.add)
                                self.tt(mem, mem_sb_b, mem, mem_sb_b, spk, spk_sb_b,
                                        ALU.subtract)
                                self.ts(spk, spk_sb_b, mem, mem_sb_b, 1.0, None, ALU.is_gt)
                                self.tt(ssum, sum_sb_b, ssum, sum_sb_b, spk, spk_sb_b,
                                        ALU.add)
                            self.act(gsil[:, ti % 2, :], gsil_b, ssum, sum_sb_b, AF.Silu,
                                     scale=0.2)
                            self.dma("sync", g_scr[it * 128:(it + 1) * 128, :],
                                     gsil[:, ti % 2, :], reads=(gsil_b,), writes=(g_scr_b,))
                self.close_scope(esF)
            self.close_scope(es_xn)

            esQ = contextlib.ExitStack()
            if True:
                v_tm, v_tm_b = self.sb("l1_v", [128, NT, D], BF16, esL, side="left")
                wq_sb, wq_sb_b = self.sb("l1_wq", [128, DC, D], BF16, esQ)
                wk_sb, wk_sb_b = self.sb("l1_wk", [128, DC, D], BF16, esQ)
                wv_sb, wv_sb_b = self.sb("l1_wv", [128, DC, D], BF16, esQ)
                self.dma("gpsimd", wq_sb[...],
                         W["l1_WqT_dram"][0].rearrange("(dc p) e -> p dc e", p=128),
                         reads=(W["l1_WqT_dram"][1],), writes=(wq_sb_b,))
                self.dma("gpsimd", wk_sb[...],
                         W["l1_WkT_dram"][0].rearrange("(dc p) e -> p dc e", p=128),
                         reads=(W["l1_WkT_dram"][1],), writes=(wk_sb_b,))
                self.dma("gpsimd", wv_sb[...],
                         W["l1_WvT_dram"][0].rearrange("(dc p) e -> p dc e", p=128),
                         reads=(W["l1_WvT_dram"][1],), writes=(wv_sb_b,))
                x2c, x2c_b = [], []
                for k in range(2):
                    a, b = self.sb(f"l1_x2c{k}", [128, DC, 512], BF16, esQ)
                    x2c.append(a); x2c_b.append(b)
                qkv_bf, qkv_bf_b = self.sb("l1_qkvbf", [128, 4, 512], BF16, esQ)
                for tc in range(T // 512):
                    t0c = tc * 512
                    p = tc % 2
                    self.dma("sync", x2c[p][...],
                             xn2_scr[:, :, t0c:t0c + 512].rearrange("dc p t -> p dc t"),
                             reads=(xn2_scr_b,), writes=(x2c_b[p],))
                    for which, wsb, wsb_b, dscr, dscr_b in (
                            (0, wq_sb, wq_sb_b, q_scr, q_scr_b),
                            (1, wk_sb, wk_sb_b, k_scr, k_scr_b)):
                        for et in range(DC):
                            pb = psf[(et + which) % 3]
                            terms = [(wsb[:, dci, et * 128:(et + 1) * 128],
                                      x2c[p][:, dci, :]) for dci in range(DC)]
                            self.mm_group(pb[0][...], pb[1], terms,
                                          reads=(wsb_b, x2c_b[p]))
                            bf = qkv_bf[:, which, :]
                            self.act(bf, qkv_bf_b, pb[0][...], pb[1], AF.Identity)
                            self.dma("sync", dscr[et, :, t0c:t0c + 512], bf,
                                     reads=(qkv_bf_b,), writes=(dscr_b,))
                    for ti in range(4):
                        itt = tc * 4 + ti
                        for eh in range(2):
                            pb = psf[3 + eh]
                            terms = [(x2c[p][:, dci, ti * 128:(ti + 1) * 128],
                                      wv_sb[:, dci, eh * 512:(eh + 1) * 512])
                                     for dci in range(DC)]
                            self.mm_group(pb[0][...], pb[1], terms,
                                          reads=(x2c_b[p], wv_sb_b))
                            self.act(v_tm[:, itt, eh * 512:(eh + 1) * 512], v_tm_b,
                                     pb[0][...], pb[1], AF.Identity)
            self.close_scope(esQ)

            esC = contextlib.ExitStack()
            if True:
                kd, kd_b = self.sb("l1_k", [128, DC, T], BF16, esC)
                self.dma("sync", kd[...], k_scr.rearrange("dc p t -> p dc t"),
                         reads=(k_scr_b,), writes=(kd_b,))
                wo_sb, wo_sb_b = self.sb("l1_wo", [128, DC, D], BF16, esC)
                self.dma("gpsimd", wo_sb[...],
                         W["l1_WoT_dram"][0].rearrange("(dc p) e -> p dc e", p=128),
                         reads=(W["l1_WoT_dram"][1],), writes=(wo_sb_b,))
                B0_d, B0_d_b = W["l1_B0_dram"]
                B1_d, B1_d_b = W["l1_B1_dram"]
                far_d, far_d_b = W["l1_far_dram"]
                msk_d, msk_d_b = W["l1_maskdiag_dram"]
                B0p, B0p_b = self.sb("l1_B0p", [128, 16, 128], BF16, esC)
                B1p, B1p_b = self.sb("l1_B1p", [128, 16, 128], BF16, esC)
                farsb, farsb_b = self.sb("l1_farsb", [128, 16], F32, esC)
                msksb, msksb_b = self.sb("l1_mskd", [128, 128], F32, esC)
                self.dma("sync", farsb[...], far_d[...], reads=(far_d_b,),
                         writes=(farsb_b,))
                self.dma("sync", msksb[...], msk_d[...], reads=(msk_d_b,),
                         writes=(msksb_b,))
                es_bt = contextlib.ExitStack()
                if True:
                    btmp, btmp_b = self.sb("l1_btmp", [128, 16, 128], F32, es_bt)
                    self.dma("sync", btmp[...], B0_d.rearrange("h p f -> p h f"),
                             reads=(B0_d_b,), writes=(btmp_b,))
                    for h in range(16):
                        self.ts(btmp[:, h, :], btmp_b, btmp[:, h, :], btmp_b,
                                farsb[:, h:h + 1], 8.0, ALU.subtract, ALU.mult,
                                s_bufs=(farsb_b,))
                        self.tt(B0p[:, h, :], B0p_b, btmp[:, h, :], btmp_b,
                                msksb[...], msksb_b, ALU.add)
                    self.dma("sync", btmp[...], B1_d.rearrange("h p f -> p h f"),
                             reads=(B1_d_b,), writes=(btmp_b,))
                    for h in range(16):
                        self.ts(B1p[:, h, :], B1p_b, btmp[:, h, :], btmp_b,
                                farsb[:, h:h + 1], 8.0, ALU.subtract, ALU.mult,
                                s_bufs=(farsb_b,))
                self.close_scope(es_bt)

                qt, qt_b = [], []
                for k in range(2):
                    a, b = self.sb(f"l1_qt{k}", [128, DC, 128], BF16, esC)
                    qt.append(a); qt_b.append(b)
                PsbL, PsbL_b = [], []
                for k in range(2):
                    a, b = self.sb(f"l1_P{k}", [128, T], BF16, esC)
                    PsbL.append(a); PsbL_b.append(b)
                Pt, Pt_b = [], []
                for k in range(4):
                    a, b = self.sb(f"l1_Pt{k}", [128, 128], BF16, esC)
                    Pt.append(a); Pt_b.append(b)
                ysb, ysb_b = self.sb("l1_ysb", [128, DC, 128], BF16, esC)
                xg, xg_b = [], []
                for k in range(4):
                    a, b = self.sb(f"l1_xg{k}", [128, D], F32, esC)
                    xg.append(a); xg_b.append(b)
                so, so_b = [], []
                for k in range(2):
                    a, b = self.sb(f"l1_so{k}", [128, D], F32, esC)
                    so.append(a); so_b.append(b)
                colsL, colsL_b = [], []
                for k in range(2):
                    a, b = self.sb(f"l1_acols{k}", [128, 32], F32, esC)
                    colsL.append(a); colsL_b.append(b)

                for it in range(NT):
                    p = it % 2
                    self.dma("sync", qt[p][...],
                             q_scr[:, :, it * 128:(it + 1) * 128]
                             .rearrange("dc p t -> p dc t"),
                             reads=(q_scr_b,), writes=(qt_b[p],))
                    self.dma("sync", xg[p][...], x_src_ap[it * 128:(it + 1) * 128, :],
                             reads=(x_src_buf,), writes=(xg_b[p],))
                    self.dma("sync", xg[2 + p][...], xn_scr[it * 128:(it + 1) * 128, :],
                             reads=(xn_scr_b,), writes=(xg_b[2 + p],))
                    jext = (it + 1) * 128
                    n_ch = ceil_div(jext, 512)

                    def softmax_part(h, p=p, it=it, jext=jext, n_ch=n_ch):
                        hp = h % 2
                        cols, cols_b = colsL[hp], colsL_b[hp]
                        Psb, Psb_b = PsbL[hp], PsbL_b[hp]
                        m_run = cols[:, 0:1]
                        bias_c = cols[:, 3:4]
                        lfin = cols[:, 4:5]
                        rec = cols[:, 5:6]
                        tmp = cols[:, 6:7]
                        corr = cols[:, 7:8]
                        dc_h = h // 2
                        pb_h = (h % 2) * 64
                        n_grp = ceil_div(n_ch, 4)
                        for g0 in range(n_grp):
                            ch0, ch1 = g0 * 4, min(n_ch, g0 * 4 + 4)
                            for ch in range(ch0, ch1):
                                c0 = ch * 512
                                cw = min(512, jext - c0)
                                pb = psf[ch - ch0]
                                q_ap = qt[p][pb_h:pb_h + 64, dc_h, :]
                                self.mm_group(pb[0][:, 0:cw], pb[1],
                                              [(q_ap, kd[pb_h:pb_h + 64, dc_h, c0:c0 + cw])],
                                              reads=(qt_b[p], kd_b))
                                jt_lo = c0 // 128
                                if jt_lo <= it < jt_lo + ceil_div(cw, 128):
                                    off = (it - jt_lo) * 128
                                    self.tt(pb[0][:, off:off + 128], pb[1],
                                            pb[0][:, off:off + 128], pb[1],
                                            B0p[:, h, :], B0p_b, ALU.add)
                                if it >= 1 and jt_lo <= it - 1 < jt_lo + ceil_div(cw, 128):
                                    off = (it - 1 - jt_lo) * 128
                                    self.tt(pb[0][:, off:off + 128], pb[1],
                                            pb[0][:, off:off + 128], pb[1],
                                            B1p[:, h, :], B1p_b, ALU.add)
                                self.vred(cols[:, 8 + ch - ch0:9 + ch - ch0], cols_b,
                                          pb[0][:, 0:cw], pb[1], ALU.max)
                            ng = ch1 - ch0
                            if ng == 1:
                                m_g = cols[:, 8:9]
                            else:
                                m_g = cols[:, 2:3]
                                self.vred(m_g, cols_b, cols[:, 8:8 + ng], cols_b, ALU.max)
                            if g0 == 0:
                                self.vcopy(m_run, cols_b, m_g, cols_b)
                            else:
                                self.tt(tmp, cols_b, m_run, cols_b, m_g, cols_b, ALU.max)
                                self.tt(corr, cols_b, m_run, cols_b, tmp, cols_b,
                                        ALU.subtract)
                                self.act(corr, cols_b, corr, cols_b, AF.Exp, scale=0.125)
                                self.ts(Psb[:, 0:ch0 * 512], Psb_b, Psb[:, 0:ch0 * 512],
                                        Psb_b, corr, None, ALU.mult, s_bufs=(cols_b,))
                                self.ts(lfin, cols_b, lfin, cols_b, corr, None, ALU.mult,
                                        s_bufs=(cols_b,))
                                self.vcopy(m_run, cols_b, tmp, cols_b)
                            self.ts(bias_c, cols_b, m_run, cols_b, -0.125, None, ALU.mult)
                            for ch in range(ch0, ch1):
                                c0 = ch * 512
                                cw = min(512, jext - c0)
                                pb = psf[ch - ch0]
                                self.act(Psb[:, c0:c0 + cw], Psb_b, pb[0][:, 0:cw], pb[1],
                                         AF.Exp, bias=bias_c, scale=0.125,
                                         accum=cols[:, 16 + ch - ch0:17 + ch - ch0],
                                         accum_buf=cols_b, extra_reads=(cols_b,))
                            lg = cols[:, 1:2]
                            if ng == 1:
                                self.vcopy(lg, cols_b, cols[:, 16:17], cols_b)
                            else:
                                self.vred(lg, cols_b, cols[:, 16:16 + ng], cols_b, ALU.add)
                            if g0 == 0:
                                self.vcopy(lfin, cols_b, lg, cols_b)
                            else:
                                self.tt(lfin, cols_b, lfin, cols_b, lg, cols_b, ALU.add)
                        self.recip(rec, cols_b, lfin, cols_b)
                        self.ts(Psb[:, 0:jext], Psb_b, Psb[:, 0:jext], Psb_b, rec, None,
                                ALU.mult, s_bufs=(cols_b,))

                    def pv_part(h, jext=jext):
                        hp = h % 2
                        Psb, Psb_b = PsbL[hp], PsbL_b[hp]
                        dc_h = h // 2
                        pb_h = (h % 2) * 64
                        n_jt = jext // 128
                        pby = psf[4 + hp]
                        for jj in range(n_jt):
                            tp2 = 2 * hp + jj % 2
                            self.transpose(pst[jj % 2][0][...], pst[jj % 2][1],
                                           Psb[:, jj * 128:(jj + 1) * 128], Psb_b,
                                           ident, ident_b)
                            self.vcopy(Pt[tp2][...], Pt_b[tp2], pst[jj % 2][0][...],
                                       pst[jj % 2][1])
                            fn = (lambda E, lt=v_tm[:, jj, h * 64:(h + 1) * 64],
                                  rh=Pt[tp2][...], o=pby[0][0:64, 0:128],
                                  st=(jj == 0), sp=(jj == n_jt - 1):
                                  E.matmul(o, lt, rh, start=st, stop=sp))
                            self.prog.emit("tensor", fn, reads=(v_tm_b, Pt_b[tp2]),
                                           writes=(pby[1],), sync=(jj == n_jt - 1))
                        self.act(ysb[pb_h:pb_h + 64, dc_h, :], ysb_b,
                                 pby[0][0:64, 0:128], pby[1], AF.Identity)

                    for h in range(16):
                        softmax_part(h)
                        if h >= 1:
                            pv_part(h - 1)
                    pv_part(15)
                    for dh in range(2):
                        pb = psf[5]
                        terms = [(ysb[:, dci, :], wo_sb[:, dci, dh * 512:(dh + 1) * 512])
                                 for dci in range(DC)]
                        self.mm_group(pb[0][...], pb[1], terms, reads=(ysb_b, wo_sb_b))
                        self.tt(so[p][:, dh * 512:(dh + 1) * 512], so_b[p],
                                pb[0][...], pb[1], xg[2 + p][:, dh * 512:(dh + 1) * 512],
                                xg_b[2 + p], ALU.add)
                    if debug:
                        self.dma("sync", self.dram["l1_s_dump"][it * 128:(it + 1) * 128, :],
                                 so[p][...], reads=(so_b[p],),
                                 writes=(self._sdump_buf1,))
                    self.dma("sync", xg[2 + p][...], g_scr[it * 128:(it + 1) * 128, :],
                             reads=(g_scr_b,), writes=(xg_b[2 + p],))
                    self.tt(so[p][...], so_b[p], so[p][...], so_b[p], xg[2 + p][...],
                            xg_b[2 + p], ALU.mult)
                    self.tt(so[p][...], so_b[p], so[p][...], so_b[p], xg[p][...],
                            xg_b[p], ALU.add)
                    self.dma("sync", x_dst_ap[it * 128:(it + 1) * 128, :], so[p][...],
                             reads=(so_b[p],), writes=(x_dst_buf,))
            self.close_scope(esC)




    def build_layer2(self, W, x_src_ap, x_src_buf, x_dst_ap, x_dst_buf, consts,
                     debug=False):
        P = self.prog
        ident, ident_b = consts["ident"]
        psf = consts["psf"]
        pst = consts["pst"]
        EPS = 1.0e-6

        if debug:
            _, self._sdump_buf2 = self.dout("l2_s_dump", [T, D], F32)
        esL = contextlib.ExitStack()
        with esL:
            if debug:
                g_scr, g_scr_b = self.dout("l2_g_dump", [T, D], F32)
            else:
                g_scr, g_scr_b = self.dscratch("l2_g_scr", [T, D], F32)

            es_xn = contextlib.ExitStack()
            with es_xn:
                xn, xn_b = self.sb("l2_xn", [128, DC, T], BF16, es_xn)
                es_ln = contextlib.ExitStack()
                with es_ln:
                    self.ln_pass(x_src_ap, x_src_buf, xn, xn_b, ident, ident_b,
                                 [p[0] for p in pst], [p[1] for p in pst], es_ln, "l2ln")
                self.close_scope(es_ln)

                esG = contextlib.ExitStack()
                gates, gates_b = self.sb("l2_gates", [128, NT, 8], F32, esL, side="left")
                gates_dm, gates_dm_b = self.sb("l2_gates_dm", [8, T], BF16, esL, side="left")
                if True:
                    rw, rw_b = W["l2_Wr"]
                    rbt, rbt_b = W["l2_br_t"]
                    gt_tmp, gt_tmp_b = self.sb("l2_gt", [128, 4, 8], F32, esG)
                    gt_bf, gt_bf_b = self.sb("l2_gtbf", [128, 8], BF16, esG)
                    gcol, gcol_b = self.sb("l2_gcol", [128, 8], F32, esG)
                    for i in range(NT):
                        pb = psf[i % 2]
                        terms = [(xn[:, dc, i * 128:(i + 1) * 128], rw[:, dc, :])
                                 for dc in range(DC)]
                        self.mm_group(pb[0][:, 0:8], pb[1], terms, reads=(xn_b, rw_b))
                        t0 = gt_tmp[:, (i % 2) * 2, :]
                        t1 = gt_tmp[:, (i % 2) * 2 + 1, :]
                        self.tt(t0, gt_tmp_b, pb[0][:, 0:8], pb[1], rbt[...], rbt_b,
                                ALU.add)
                        c0 = (i % 2) * 4
                        m_ap = gcol[:, c0:c0 + 1]
                        self.vred(m_ap, gcol_b, t0, gt_tmp_b, ALU.max)
                        nm = gcol[:, c0 + 1:c0 + 2]
                        self.ts(nm, gcol_b, m_ap, gcol_b, -1.0, None, ALU.mult)
                        l_ap = gcol[:, c0 + 2:c0 + 3]
                        self.act(t1, gt_tmp_b, t0, gt_tmp_b, AF.Exp, bias=nm, scale=1.0,
                                 accum=l_ap, accum_buf=gcol_b, extra_reads=(gcol_b,))
                        r_ap = gcol[:, c0 + 3:c0 + 4]
                        self.recip(r_ap, gcol_b, l_ap, gcol_b)
                        self.ts(t1, gt_tmp_b, t1, gt_tmp_b, r_ap, None, ALU.mult,
                                s_bufs=(gcol_b,))
                        self.vred(m_ap, gcol_b, t1, gt_tmp_b, ALU.max)
                        self.ts(t0, gt_tmp_b, t1, gt_tmp_b, m_ap, None, ALU.is_equal,
                                s_bufs=(gcol_b,))
                        self.stt(t0, gt_tmp_b, t0, gt_tmp_b, NEG, None, t1,
                                 gt_tmp_b, ALU.mult, ALU.add)
                        self.vred(nm, gcol_b, t0, gt_tmp_b, ALU.max)
                        self.ts(gates[:, i, :], gates_b, t1, gt_tmp_b, nm, None,
                                ALU.is_ge, s_bufs=(gcol_b,))
                        self.tt(gates[:, i, :], gates_b, gates[:, i, :], gates_b, t1,
                                gt_tmp_b, ALU.mult)
                        self.vcopy(gt_bf[...], gt_bf_b, gates[:, i, :], gates_b)
                        self.transpose(pst[0][0][0:8, :], pst[0][1], gt_bf[...],
                                       gt_bf_b, ident, ident_b)
                        self.vcopy(gates_dm[:, i * 128:(i + 1) * 128], gates_dm_b,
                                   pst[0][0][0:8, :], pst[0][1])
                self.close_scope(esG)

                esF = contextlib.ExitStack()
                if True:
                    b2sb, b2sb_b = W["l2_b2"]
                    b1sb, b1sb_b = W["l2_b1"]
                    g_sb, g_sb_b = self.sb("l2_gq", [128, 8, 1024], F32, esF)
                    w1u, w1u_b, w2u, w2u_b, h_sb, h_sb_b = [], [], [], [], [], []
                    for k in range(2):
                        a, b = self.sb(f"l2_w1u{k}", [128, 8, 512], BF16, esF)
                        w1u.append(a); w1u_b.append(b)
                        a, b = self.sb(f"l2_w2u{k}", [128, 4, 1024], BF16, esF)
                        w2u.append(a); w2u_b.append(b)
                        a, b = self.sb(f"l2_h{k}", [128, 4, 512], BF16, esF)
                        h_sb.append(a); h_sb_b.append(b)
                    gsil, gsil_b = self.sb("l2_gsil", [128, 2, 1024], F32, esF)
                    W1d, W1d_b = W["l2_W1_dram"]
                    W2d, W2d_b = W["l2_W2_dram"]
                    for quarter in range(4):
                        tq0 = quarter * 8
                        for ti in range(8):
                            it = tq0 + ti
                            pb = psf[4 + (ti % 2)]
                            for dh in range(2):
                                self.mm_group(
                                    pb[0][...], pb[1],
                                    [(gates_dm[:, it * 128:(it + 1) * 128],
                                      b2sb[:, dh * 512:(dh + 1) * 512])],
                                    reads=(gates_dm_b, b2sb_b))
                                self.act(g_sb[:, ti, dh * 512:(dh + 1) * 512], g_sb_b,
                                         pb[0][...], pb[1], AF.Identity)
                        for e in range(8):
                            for fb in range(4):
                                u = e * 4 + fb
                                pw = u % 2
                                self.dma("gpsimd", w1u[pw][...],
                                         W1d[e, :, fb * 512:(fb + 1) * 512]
                                         .rearrange("(dc p) f -> p dc f", p=128),
                                         reads=(W1d_b,), writes=(w1u_b[pw],))
                                self.dma("gpsimd", w2u[pw][...],
                                         W2d[e, fb * 512:(fb + 1) * 512, :]
                                         .rearrange("(fc p) d -> p fc d", p=128),
                                         reads=(W2d_b,), writes=(w2u_b[pw],))
                                for tc in range(2):
                                    t0c = tq0 * 128 + tc * 512
                                    ph = tc % 2
                                    for ft in range(4):
                                        pb = psf[ft % 4]
                                        terms = [(w1u[pw][:, dc, ft * 128:(ft + 1) * 128],
                                                  xn[:, dc, t0c:t0c + 512])
                                                 for dc in range(DC)]
                                        self.mm_group(pb[0][...], pb[1], terms,
                                                      reads=(w1u_b[pw], xn_b))
                                        self.act(h_sb[ph][:, ft, :], h_sb_b[ph],
                                                 pb[0][...], pb[1], AF.Relu,
                                                 bias=b1sb[:, e, fb * 4 + ft:fb * 4 + ft + 1],
                                                 extra_reads=(b1sb_b,))
                                for tc in range(2):
                                    t0c = tq0 * 128 + tc * 512
                                    ph = tc % 2
                                    for ti in range(4):
                                        itt = (t0c // 128) + ti
                                        gq_ti = itt - tq0
                                        for dh in range(2):
                                            pb = psf[4 + ((ti * 2 + dh) % 2)]
                                            terms = [(h_sb[ph][:, fc, ti * 128:(ti + 1) * 128],
                                                      w2u[pw][:, fc, dh * 512:(dh + 1) * 512])
                                                     for fc in range(4)]
                                            self.mm_group(pb[0][...], pb[1], terms,
                                                          reads=(h_sb_b[ph], w2u_b[pw]))
                                            self.stt(g_sb[:, gq_ti, dh * 512:(dh + 1) * 512],
                                                     g_sb_b, pb[0][...], pb[1],
                                                     gates[:, itt, e:e + 1], gates_b,
                                                     g_sb[:, gq_ti, dh * 512:(dh + 1) * 512],
                                                     g_sb_b, ALU.mult, ALU.add)
                        for ti in range(8):
                            it = tq0 + ti
                            self.act(gsil[:, ti % 2, :], gsil_b, g_sb[:, ti, :], g_sb_b,
                                     AF.Silu)
                            self.dma("sync", g_scr[it * 128:(it + 1) * 128, :],
                                     gsil[:, ti % 2, :], reads=(gsil_b,),
                                     writes=(g_scr_b,))
                self.close_scope(esF)

                esA = contextlib.ExitStack()
                if True:
                    wk_sb, wk_sb_b = self.sb("l2_wk", [128, DC, D], BF16, esA)
                    wv_sb, wv_sb_b = self.sb("l2_wv", [128, DC, D], BF16, esA)
                    proj2, proj2_b = W["l2_proj2"]
                    onec, onec_b = W["l2_ones"]
                    self.dma("gpsimd", wk_sb[...],
                             W["l2_Wk_dram"][0].rearrange("(dc p) e -> p dc e", p=128),
                             reads=(W["l2_Wk_dram"][1],), writes=(wk_sb_b,))
                    self.dma("gpsimd", wv_sb[...],
                             W["l2_Wv_dram"][0].rearrange("(dc p) e -> p dc e", p=128),
                             reads=(W["l2_Wv_dram"][1],), writes=(wv_sb_b,))
                    kch, kch_b = self.sb("l2_kch", [128, DC, 512], BF16, esA)
                    vch, vch_b = self.sb("l2_vch", [128, 4, 1024], BF16, esA)
                    kfch, kfch_b = self.sb("l2_kfch", [128, 4, 1024], BF16, esA)
                    kv_ps = psf[4]
                    ks_ps = psf[5]
                    for tc in range(T // 512):
                        t0c = tc * 512
                        for et in range(DC):
                            pb = psf[et % 2]
                            terms = [(wk_sb[:, dci, et * 128:(et + 1) * 128],
                                      xn[:, dci, t0c:t0c + 512]) for dci in range(DC)]
                            self.mm_group(pb[0][...], pb[1], terms,
                                          reads=(wk_sb_b, xn_b))
                            self.act(kch[:, et, :], kch_b, pb[0][...], pb[1], AF.Identity)
                        for ti in range(4):
                            for eh in range(2):
                                pb = psf[2 + eh]
                                terms = [(xn[:, dci, t0c + ti * 128:t0c + (ti + 1) * 128],
                                          wv_sb[:, dci, eh * 512:(eh + 1) * 512])
                                         for dci in range(DC)]
                                self.mm_group(pb[0][...], pb[1], terms,
                                              reads=(xn_b, wv_sb_b))
                                self.act(vch[:, ti, eh * 512:(eh + 1) * 512], vch_b,
                                         pb[0][...], pb[1], AF.Identity)
                        for h in range(16):
                            b0 = (h % 2) * 64
                            dch = h // 2
                            for ti in range(4):
                                pb = psf[ti % 2]
                                self.mm_group(
                                    pb[0][0:128, 0:64], pb[1],
                                    [(kch[b0:b0 + 64, dch, ti * 128:(ti + 1) * 128],
                                      proj2[b0:b0 + 64, :])],
                                    reads=(kch_b, proj2_b))
                                self.ts(kfch[:, ti, h * 64:(h + 1) * 64], kfch_b,
                                        pb[0][0:128, 0:64], pb[1], EPS, EPS,
                                        ALU.add, ALU.max)
                        for h in range(16):
                            b0 = (h % 2) * 64
                            c0 = (h // 2) * 64
                            for ti in range(4):
                                first = (tc == 0 and ti == 0)
                                fn = (lambda E, lt=kfch[:, ti, h * 64:(h + 1) * 64],
                                      rh=vch[:, ti, h * 64:(h + 1) * 64],
                                      o=kv_ps[0][b0:b0 + 64, c0:c0 + 64], st=first:
                                      E.matmul(o, lt, rh, start=st, stop=True))
                                self.prog.emit("tensor", fn, reads=(kfch_b, vch_b),
                                               writes=(kv_ps[1],), sync=True)
                                fn2 = (lambda E, lt=kfch[:, ti, h * 64:(h + 1) * 64],
                                       rh=onec[...],
                                       o=ks_ps[0][b0:b0 + 64, h:h + 1], st=first:
                                       E.matmul(o, lt, rh, start=st, stop=True))
                                self.prog.emit("tensor", fn2, reads=(kfch_b, onec_b),
                                               writes=(ks_ps[1],), sync=True)
                    kv_sb, kv_sb_b = self.sb("l2_kv", [128, 512], BF16, esA)
                    ks_sb, ks_sb_b = self.sb("l2_ks", [128, 16], BF16, esA)
                    self.act(kv_sb[...], kv_sb_b, kv_ps[0][...], kv_ps[1], AF.Identity)
                    self.act(ks_sb[...], ks_sb_b, ks_ps[0][:, 0:16], ks_ps[1], AF.Identity)

                    wq_sb, wq_sb_b = wk_sb, wk_sb_b
                    self.dma("gpsimd", wq_sb[...],
                             W["l2_Wq_dram"][0].rearrange("(dc p) e -> p dc e", p=128),
                             reads=(W["l2_Wq_dram"][1],), writes=(wq_sb_b,))
                    wo_sb, wo_sb_b = wv_sb, wv_sb_b
                    self.dma("gpsimd", wo_sb[...],
                             W["l2_Wo_dram"][0].rearrange("(ec p) o -> p ec o", p=128),
                             reads=(W["l2_Wo_dram"][1],), writes=(wo_sb_b,))
                    bot_ap, bot_b = self.load_tile(W, "l2_bo_d", [128, 1024], F32, esA, "l2bott")
                    bot = bot_ap[...]
                    qch, qch_b = kch, kch_b
                    qfd, qfd_b = self.sb("l2_qfd", [128, 2, 512], BF16, esA)
                    otm, otm_b = vch, vch_b
                    odm, odm_b = self.sb("l2_odm", [128, DC, 128], BF16, esA)
                    dcol, dcol_b = self.sb("l2_dcol", [128, 8], F32, esA)
                    xg, xg_b = [], []
                    for k in range(4):
                        a, b = self.sb(f"l2_xg{k}", [128, D], F32, esA)
                        xg.append(a); xg_b.append(b)
                    so, so_b = [], []
                    for k in range(2):
                        a, b = self.sb(f"l2_so{k}", [128, D], F32, esA)
                        so.append(a); so_b.append(b)
                    for tc in range(T // 512):
                        t0c = tc * 512
                        for et in range(DC):
                            pb = psf[et % 2]
                            terms = [(wq_sb[:, dci, et * 128:(et + 1) * 128],
                                      xn[:, dci, t0c:t0c + 512]) for dci in range(DC)]
                            self.mm_group(pb[0][...], pb[1], terms,
                                          reads=(wq_sb_b, xn_b))
                            self.act(qch[:, et, :], qch_b, pb[0][...], pb[1], AF.Identity)
                        for h in range(16):
                            b0 = (h % 2) * 64
                            dch = h // 2
                            pb = psf[2 + h % 2]
                            self.mm_group(
                                pb[0][b0:b0 + 64, 0:512], pb[1],
                                [(proj2[b0:b0 + 64, :], qch[b0:b0 + 64, dch, :])],
                                reads=(proj2_b, qch_b))
                            self.ts(qfd[:, h % 2, :][b0:b0 + 64, :], qfd_b,
                                    pb[0][b0:b0 + 64, 0:512], pb[1], EPS, EPS,
                                    ALU.add, ALU.max)
                            for ti in range(4):
                                pbn = psf[4]
                                self.mm_group(
                                    pbn[0][0:128, 0:64], pbn[1],
                                    [(qfd[:, h % 2, ti * 128:(ti + 1) * 128][b0:b0 + 64, :],
                                      kv_sb[b0:b0 + 64, (h // 2) * 64:(h // 2) * 64 + 64])],
                                    reads=(qfd_b, kv_sb_b))
                                pbd = psf[5]
                                self.mm_group(
                                    pbd[0][0:128, 0:1], pbd[1],
                                    [(qfd[:, h % 2, ti * 128:(ti + 1) * 128][b0:b0 + 64, :],
                                      ks_sb[b0:b0 + 64, h:h + 1])],
                                    reads=(qfd_b, ks_sb_b))
                                dc_ap = dcol[:, ti * 2:ti * 2 + 1]
                                self.ts(dc_ap, dcol_b, pbd[0][0:128, 0:1], pbd[1],
                                        EPS, None, ALU.add)
                                rc_ap = dcol[:, ti * 2 + 1:ti * 2 + 2]
                                self.recip(rc_ap, dcol_b, dc_ap, dcol_b)
                                self.ts(otm[:, ti, h * 64:(h + 1) * 64], otm_b,
                                        pbn[0][0:128, 0:64], pbn[1], rc_ap, None,
                                        ALU.mult, s_bufs=(dcol_b,))
                        for ti in range(4):
                            it = tc * 4 + ti
                            p = it % 2
                            self.dma("sync", xg[p][...],
                                     x_src_ap[it * 128:(it + 1) * 128, :],
                                     reads=(x_src_buf,), writes=(xg_b[p],))
                            self.dma("sync", xg[2 + p][...],
                                     g_scr[it * 128:(it + 1) * 128, :],
                                     reads=(g_scr_b,), writes=(xg_b[2 + p],))
                            for dc in range(DC):
                                pb2 = pst[dc % 2]
                                self.transpose(pb2[0][...], pb2[1],
                                               otm[:, ti, dc * 128:(dc + 1) * 128],
                                               otm_b, ident, ident_b)
                                self.vcopy(odm[:, dc, :], odm_b, pb2[0][...], pb2[1])
                            for dh in range(2):
                                pb = psf[dh % 2]
                                terms = [(odm[:, dci, :],
                                          wo_sb[:, dci, dh * 512:(dh + 1) * 512])
                                         for dci in range(DC)]
                                self.mm_group(pb[0][...], pb[1], terms,
                                              reads=(odm_b, wo_sb_b))
                                self.tt(so[p][:, dh * 512:(dh + 1) * 512], so_b[p],
                                        pb[0][...], pb[1],
                                        bot[:, dh * 512:(dh + 1) * 512], bot_b, ALU.add)
                            if debug:
                                self.dma("sync",
                                         self.dram["l2_s_dump"][it * 128:(it + 1) * 128, :],
                                         so[p][...], reads=(so_b[p],),
                                         writes=(self._sdump_buf2,))
                            self.tt(so[p][...], so_b[p], so[p][...], so_b[p],
                                    xg[2 + p][...], xg_b[2 + p], ALU.mult)
                            self.tt(so[p][...], so_b[p], so[p][...], so_b[p],
                                    xg[p][...], xg_b[p], ALU.add)
                            self.dma("sync", x_dst_ap[it * 128:(it + 1) * 128, :],
                                     so[p][...], reads=(so_b[p],), writes=(x_dst_buf,))
                self.close_scope(esA)
            self.close_scope(es_xn)




    def ln_pass_affine(self, x_src_ap, x_src_buf, out_dmaj, out_dmaj_buf,
                      gam_ap, gam_b, bet_ap, bet_b, ident, ident_b, pst, es_ph, tag):
        xt, xt_b, xc, xc_b, xnb, xnb_b = [], [], [], [], [], []
        for k in range(2):
            a, b = self.sb(f"{tag}_xt{k}", [128, 1024], F32, es_ph); xt.append(a); xt_b.append(b)
            a, b = self.sb(f"{tag}_xc{k}", [128, 1024], F32, es_ph); xc.append(a); xc_b.append(b)
            a, b = self.sb(f"{tag}_xnb{k}", [128, 1024], BF16, es_ph); xnb.append(a); xnb_b.append(b)
        colsL, colsL_b = [], []
        for k in range(2):
            a, b = self.sb(f"{tag}_cols{k}", [128, 8], F32, es_ph)
            colsL.append(a); colsL_b.append(b)
        for i in range(NT):
            p = i % 2
            cols, cols_b = colsL[p], colsL_b[p]
            self.dma("sync", xt[p][...], x_src_ap[i * 128:(i + 1) * 128, :],
                     reads=(x_src_buf,), writes=(xt_b[p],))
            c0 = 0
            s_ap = cols[:, c0:c0 + 1]
            self.vred(s_ap, cols_b, xt[p][...], xt_b[p], ALU.add)
            nm_ap = cols[:, c0 + 1:c0 + 2]
            self.ts(nm_ap, cols_b, s_ap, cols_b, -1.0 / D, None, ALU.mult)
            self.ts(xc[p][...], xc_b[p], xt[p][...], xt_b[p], nm_ap, None, ALU.add,
                    s_bufs=(cols_b,))
            v_ap = cols[:, c0 + 2:c0 + 3]
            self.act(xt[p][...], xt_b[p], xc[p][...], xc_b[p], AF.Square,
                     accum=v_ap, accum_buf=cols_b)
            r1 = cols[:, c0 + 3:c0 + 4]
            self.act(r1, cols_b, v_ap, cols_b, AF.Ln, bias=1.0e-5, scale=1.0 / D)
            self.act(r1, cols_b, r1, cols_b, AF.Exp, scale=-0.5)
            self.ts(xc[p][...], xc_b[p], xc[p][...], xc_b[p], r1, None, ALU.mult,
                    s_bufs=(cols_b,))
            self.tt(xc[p][...], xc_b[p], xc[p][...], xc_b[p], gam_ap, gam_b, ALU.mult)
            self.tt(xc[p][...], xc_b[p], xc[p][...], xc_b[p], bet_ap, bet_b, ALU.add)
            self.vcopy(xnb[p][...], xnb_b[p], xc[p][...], xc_b[p])
            for dc in range(DC):
                pb = dc % 2
                self.transpose(pst[pb][0][...], pst[pb][1],
                               xnb[p][:, dc * 128:(dc + 1) * 128], xnb_b[p],
                               ident, ident_b)
                self.vcopy(out_dmaj[:, dc, i * 128:(i + 1) * 128], out_dmaj_buf,
                           pst[pb][0][...], pst[pb][1])

    def build_layer3(self, W, x_src_ap, x_src_buf, x_dst_ap, x_dst_buf, consts,
                     debug=False):
        P = self.prog
        ident, ident_b = consts["ident"]
        identf, identf_b = consts["identf"]
        psf = consts["psf"]
        pst = consts["pst"]

        if debug:
            _, self._sdump_buf3 = self.dout("l3_s_dump", [T, D], F32)
        esL = contextlib.ExitStack()
        with esL:
            if debug:
                g_scr, g_scr_b = self.dout("l3_g_dump", [T, D], F32)
            else:
                g_scr, g_scr_b = self.dscratch("l3_g_scr", [T, D], F32)
            xn_scr, xn_scr_b = self.dscratch("l3_xn_scr", [T, D], F32)
            h1_scr, h1_scr_b = self.dscratch("l3_h1_scr", [DC, 128, T + 8], BF16)
            x2_scr, x2_scr_b = self.dscratch("l3_x2_scr", [T, D], F32)
            x3_scr, x3_scr_b = self.dscratch("l3_x3_scr", [T, D], F32)

            es_xn = contextlib.ExitStack()
            with es_xn:
                xn, xn_b = self.sb("l3_xn", [128, DC, T], BF16, es_xn)
                es_ln = contextlib.ExitStack()
                with es_ln:
                    zpad, zpad_b = self.sb("l3_zpad", [128, DC, 8], BF16, es_ln)
                    self.prog.emit("vector", lambda E: E.memset(zpad[...], 0.0),
                                   writes=(zpad_b,))
                    self.dma("sync", h1_scr[:, :, 0:8].rearrange("dc p t -> p dc t"),
                             zpad[...], reads=(zpad_b,), writes=(h1_scr_b,))
                    cg, cg_b = self.load_tile(W, "l3_cg_d", [128, 1024], F32, es_ln, "l3cgt")
                    cb, cb_b = self.load_tile(W, "l3_cb_d", [128, 1024], F32, es_ln, "l3cbt")
                    self.ln_pass_dual(x_src_ap, x_src_buf, xn, xn_b,
                                      xn_scr, xn_scr_b,
                                      h1_scr[:, :, 8:8 + T], h1_scr_b,
                                      cg[...], cg_b, cb[...], cb_b,
                                      ident, ident_b, pst, es_ln, "l3ln")
                self.close_scope(es_ln)

                esG = contextlib.ExitStack()
                gates, gates_b = self.sb("l3_gates", [128, NT, 8], F32, esL, side="left")
                sigc, sigc_b = self.sb("l3_sig", [128, NT], F32, esL, side="left")
                if True:
                    rw, rw_b = W["l3_rw"]
                    sgw, sgw_b = W["l3_sgw"]
                    gt_tmp, gt_tmp_b = self.sb("l3_gt", [128, 4, 8], F32, esG)
                    gcol, gcol_b = self.sb("l3_gcol", [128, 8], F32, esG)
                    for i in range(NT):
                        pb = psf[i % 2]
                        terms = [(xn[:, dc, i * 128:(i + 1) * 128], rw[:, dc, :])
                                 for dc in range(DC)]
                        self.mm_group(pb[0][:, 0:8], pb[1], terms, reads=(xn_b, rw_b))
                        t0 = gt_tmp[:, (i % 2) * 2, :]
                        t1 = gt_tmp[:, (i % 2) * 2 + 1, :]
                        c0 = (i % 2) * 4
                        m_ap = gcol[:, c0:c0 + 1]
                        self.vred(m_ap, gcol_b, pb[0][:, 0:8], pb[1], ALU.max)
                        nm = gcol[:, c0 + 1:c0 + 2]
                        self.ts(nm, gcol_b, m_ap, gcol_b, -1.0, None, ALU.mult)
                        l_ap = gcol[:, c0 + 2:c0 + 3]
                        self.act(t1, gt_tmp_b, pb[0][:, 0:8], pb[1], AF.Exp, bias=nm,
                                 scale=1.0, accum=l_ap, accum_buf=gcol_b,
                                 extra_reads=(gcol_b,))
                        r_ap = gcol[:, c0 + 3:c0 + 4]
                        self.recip(r_ap, gcol_b, l_ap, gcol_b)
                        self.ts(t1, gt_tmp_b, t1, gt_tmp_b, r_ap, None, ALU.mult,
                                s_bufs=(gcol_b,))
                        self.vred(m_ap, gcol_b, t1, gt_tmp_b, ALU.max)
                        self.ts(t0, gt_tmp_b, t1, gt_tmp_b, m_ap, None, ALU.is_equal,
                                s_bufs=(gcol_b,))
                        self.stt(t0, gt_tmp_b, t0, gt_tmp_b, NEG, None, t1,
                                 gt_tmp_b, ALU.mult, ALU.add)
                        self.vred(nm, gcol_b, t0, gt_tmp_b, ALU.max)
                        self.ts(gates[:, i, :], gates_b, t1, gt_tmp_b, nm, None,
                                ALU.is_ge, s_bufs=(gcol_b,))
                        self.tt(gates[:, i, :], gates_b, gates[:, i, :], gates_b, t1,
                                gt_tmp_b, ALU.mult)
                        pbs = psf[2]
                        terms = [(xn[:, dc, i * 128:(i + 1) * 128], sgw[:, dc:dc + 1])
                                 for dc in range(DC)]
                        self.mm_group(pbs[0][:, 0:1], pbs[1], terms, reads=(xn_b, sgw_b))
                        self.act(sigc[:, i:i + 1], sigc_b, pbs[0][:, 0:1], pbs[1],
                                 AF.Sigmoid)
                self.close_scope(esG)

                esF = contextlib.ExitStack()
                if True:
                    g_sb, g_sb_b = self.sb("l3_gq", [128, 8, 1024], F32, esF)
                    w1u, w1u_b, w2u, w2u_b, h_sb, h_sb_b = [], [], [], [], [], []
                    for k in range(2):
                        a, b = self.sb(f"l3_w1u{k}", [128, 8, 512], BF16, esF)
                        w1u.append(a); w1u_b.append(b)
                        a, b = self.sb(f"l3_w2u{k}", [128, 4, 1024], BF16, esF)
                        w2u.append(a); w2u_b.append(b)
                        a, b = self.sb(f"l3_h{k}", [128, 4, 512], BF16, esF)
                        h_sb.append(a); h_sb_b.append(b)
                    sgu_u, sgu_u_b = [], []
                    for k in range(2):
                        a, b = self.sb(f"l3_sgu{k}", [128, 8, 1024], BF16, esF)
                        sgu_u.append(a); sgu_u_b.append(b)
                    sdw, sdw_b = self.sb("l3_sdw", [128, 8, 1024], BF16, esF)
                    self.dma("gpsimd", sdw[...],
                             W["l3_sdown_dram"][0].rearrange("(fc p) d -> p fc d", p=128),
                             reads=(W["l3_sdown_dram"][1],), writes=(sdw_b,))
                    for k in range(2):
                        self.dma("gpsimd", sgu_u[k][...],
                                 W["l3_sgu_dram"][0][:, k, :, :]
                                 .rearrange("(dc p) g f -> p dc (g f)", p=128),
                                 reads=(W["l3_sgu_dram"][1],), writes=(sgu_u_b[k],))
                    sh1, sh1_b = self.sb("l3_sh1", [128, 2, 512], BF16, esF)
                    sh1d, sh1d_b = self.sb("l3_sh1d", [128, 8, 128], BF16, esF)
                    shg, shg_b = self.sb("l3_shg", [128, 512], F32, esF)
                    W1d, W1d_b = W["l3_w1_dram"]
                    W2d, W2d_b = W["l3_w2_dram"]
                    for quarter in range(4):
                        tq0 = quarter * 8
                        for ti in range(8):
                            it = tq0 + ti
                            tt0 = it * 128
                            for u in range(2):
                                for half in range(2):
                                    pb = psf[u * 2 + half]
                                    terms = [(xn[:, dc, tt0:tt0 + 128],
                                              sgu_u[u][:, dc, half * 512:(half + 1) * 512])
                                             for dc in range(DC)]
                                    self.mm_group(pb[0][...], pb[1], terms,
                                                  reads=(xn_b, sgu_u_b[u]))
                                self.act(shg[...], shg_b, psf[u * 2][0][...],
                                         psf[u * 2][1], AF.Silu)
                                self.tt(sh1[:, u, :], sh1_b, shg[...], shg_b,
                                        psf[u * 2 + 1][0][...], psf[u * 2 + 1][1],
                                        ALU.mult)
                            for fc in range(DC):
                                pb2 = pst[fc % 2]
                                u, off = fc // 4, (fc % 4) * 128
                                self.transpose(pb2[0][...], pb2[1],
                                               sh1[:, u, off:off + 128], sh1_b,
                                               ident, ident_b)
                                self.vcopy(sh1d[:, fc, :], sh1d_b, pb2[0][...], pb2[1])
                            for dh in range(2):
                                pb = psf[4 + dh]
                                terms = [(sh1d[:, fci, :],
                                          sdw[:, fci, dh * 512:(dh + 1) * 512])
                                         for fci in range(DC)]
                                self.mm_group(pb[0][...], pb[1], terms,
                                              reads=(sh1d_b, sdw_b))
                                self.ts(g_sb[:, ti, dh * 512:(dh + 1) * 512], g_sb_b,
                                        pb[0][...], pb[1], sigc[:, it:it + 1], None,
                                        ALU.mult, s_bufs=(sigc_b,))
                        for e in range(8):
                            for fb in range(2):
                                u = e * 2 + fb
                                pw = u % 2
                                self.dma("gpsimd", w1u[pw][...],
                                         W1d[e, :, fb * 512:(fb + 1) * 512]
                                         .rearrange("(dc p) f -> p dc f", p=128),
                                         reads=(W1d_b,), writes=(w1u_b[pw],))
                                self.dma("gpsimd", w2u[pw][...],
                                         W2d[e, fb * 512:(fb + 1) * 512, :]
                                         .rearrange("(fc p) d -> p fc d", p=128),
                                         reads=(W2d_b,), writes=(w2u_b[pw],))
                                for tc in range(2):
                                    t0c = tq0 * 128 + tc * 512
                                    ph = tc % 2
                                    for ft in range(4):
                                        pb = psf[ft % 4]
                                        terms = [(w1u[pw][:, dc, ft * 128:(ft + 1) * 128],
                                                  xn[:, dc, t0c:t0c + 512])
                                                 for dc in range(DC)]
                                        self.mm_group(pb[0][...], pb[1], terms,
                                                      reads=(w1u_b[pw], xn_b))
                                        self.act(h_sb[ph][:, ft, :], h_sb_b[ph],
                                                 pb[0][...], pb[1], AF.Silu)
                                for tc in range(2):
                                    t0c = tq0 * 128 + tc * 512
                                    ph = tc % 2
                                    for ti in range(4):
                                        itt = (t0c // 128) + ti
                                        gq_ti = itt - tq0
                                        for dh in range(2):
                                            pb = psf[4 + ((ti * 2 + dh) % 2)]
                                            terms = [(h_sb[ph][:, fc, ti * 128:(ti + 1) * 128],
                                                      w2u[pw][:, fc, dh * 512:(dh + 1) * 512])
                                                     for fc in range(4)]
                                            self.mm_group(pb[0][...], pb[1], terms,
                                                          reads=(h_sb_b[ph], w2u_b[pw]))
                                            self.stt(g_sb[:, gq_ti, dh * 512:(dh + 1) * 512],
                                                     g_sb_b, pb[0][...], pb[1],
                                                     gates[:, itt, e:e + 1], gates_b,
                                                     g_sb[:, gq_ti, dh * 512:(dh + 1) * 512],
                                                     g_sb_b, ALU.mult, ALU.add)
                        for ti in range(8):
                            it = tq0 + ti
                            self.act(shg[...], shg_b, g_sb[:, ti, 0:512], g_sb_b, AF.Silu)
                            self.dma("sync", g_scr[it * 128:(it + 1) * 128, 0:512],
                                     shg[...], reads=(shg_b,), writes=(g_scr_b,))
                            self.act(shg[...], shg_b, g_sb[:, ti, 512:1024], g_sb_b,
                                     AF.Silu)
                            self.dma("sync", g_scr[it * 128:(it + 1) * 128, 512:1024],
                                     shg[...], reads=(shg_b,), writes=(g_scr_b,))
                self.close_scope(esF)
            self.close_scope(es_xn)

            esCV = contextlib.ExitStack()
            if True:
                wcv, wcv_b = self.sb("l3_wcv", [128, 64, 128], BF16, esCV)
                self.dma("gpsimd", wcv[...],
                         W["l3_convw_dram"][0].rearrange("k o c p f -> p (k o c) f"),
                         reads=(W["l3_convw_dram"][1],), writes=(wcv_b,))
                cbs, cbs_b = W["l3_cbias"]
                h1c, h1c_b = [], []
                for k in range(2):
                    a, b = self.sb(f"l3_h1c{k}", [128, DC, 520], BF16, esCV)
                    h1c.append(a); h1c_b.append(b)
                cvd, cvd_b = self.sb("l3_cvd", [128, DC, 512], F32, esCV)
                x2t, x2t_b = [], []
                for k in range(2):
                    a, b = self.sb(f"l3_x2t{k}", [128, 1024], F32, esCV)
                    x2t.append(a); x2t_b.append(b)
                for tc in range(T // 512):
                    t0c = tc * 512
                    p = tc % 2
                    self.dma("sync", h1c[p][:, :, 0:519],
                             h1_scr[:, :, t0c + 1:t0c + 520]
                             .rearrange("dc p t -> p dc t"),
                             reads=(h1_scr_b,), writes=(h1c_b[p],))
                    for oc8 in range(8):
                        g = oc8 // 2
                        pb = psf[oc8 % 4]
                        terms = []
                        for k in range(4):
                            for cic in range(2):
                                widx = k * 16 + oc8 * 2 + cic
                                terms.append((wcv[:, widx, :],
                                              h1c[p][:, g * 2 + cic, 4 + k:4 + k + 512]))
                        self.mm_group(pb[0][...], pb[1], terms, reads=(wcv_b, h1c_b[p]))
                        self.act(cvd[:, oc8, :], cvd_b, pb[0][...], pb[1], AF.Identity,
                                 bias=cbs[:, oc8:oc8 + 1], extra_reads=(cbs_b,))
                    for ti in range(4):
                        it = tc * 4 + ti
                        self.dma("sync", x2t[p][...],
                                 xn_scr[it * 128:(it + 1) * 128, :],
                                 reads=(xn_scr_b,), writes=(x2t_b[p],))
                        for oc8 in range(8):
                            pb2 = psf[4 + oc8 % 2]
                            self.transpose(pb2[0][:, 0:128], pb2[1],
                                           cvd[:, oc8, ti * 128:(ti + 1) * 128], cvd_b,
                                           identf, identf_b)
                            self.tt(x2t[p][:, oc8 * 128:(oc8 + 1) * 128], x2t_b[p],
                                    x2t[p][:, oc8 * 128:(oc8 + 1) * 128], x2t_b[p],
                                    pb2[0][:, 0:128], pb2[1], ALU.add)
                        self.dma("sync", x2_scr[it * 128:(it + 1) * 128, :],
                                 x2t[p][...], reads=(x2t_b[p],), writes=(x2_scr_b,))
            self.close_scope(esCV)

            es4 = contextlib.ExitStack()
            if True:
                h2, h2_b = self.sb("l3_h2", [128, DC, T], BF16, es4)
                es_ln2 = contextlib.ExitStack()
                with es_ln2:
                    ag, ag_b = self.load_tile(W, "l3_ag_d", [128, 1024], F32, es_ln2, "l3agt")
                    ab, ab_b = self.load_tile(W, "l3_ab_d", [128, 1024], F32, es_ln2, "l3abt")
                    self.ln_pass_affine(x2_scr, x2_scr_b, h2, h2_b,
                                        ag[...], ag_b, ab[...], ab_b,
                                        ident, ident_b, pst, es_ln2, "l3ln2")
                self.close_scope(es_ln2)
                ones_sc, ones_sc_b = self.sb("l3_ones", [128, T], F32, es4)
                self.prog.emit("vector", lambda E: E.memset(ones_sc[...], 1.0),
                               writes=(ones_sc_b,))
                isc, isc_b = self.sb("l3_isc", [128, T], F32, es4)
                self.dma("sync", isc[...], W["l3_iscale_dram"][0][...],
                         reads=(W["l3_iscale_dram"][1],), writes=(isc_b,))
                qb_, qb_b = W["l3_bq"]
                kb_, kb_b = W["l3_bk"]
                vb_, vb_b = W["l3_bv"]
                wq3, wq3_b = self.sb("l3_wq3", [128, DC, 128], BF16, es4)
                wk3, wk3_b = self.sb("l3_wk3", [128, DC, 128], BF16, es4)
                wv3, wv3_b = self.sb("l3_wv3", [128, DC, 128], BF16, es4)
                qs, qs_b = self.sb("l3_qs", [128, T], F32, es4)
                ks, ks_b = self.sb("l3_ks", [128, T], F32, es4)
                vs, vs_b = self.sb("l3_vs", [128, T], F32, es4)
                tmps, tmps_b = self.sb("l3_tmps", [128, T], F32, es4)
                x2cb, x2cb_b = self.sb("l3_x2cb", [128, 32, 128], F32, es4)
                for dc in range(DC):
                    self.dma("gpsimd", wq3[...],
                             W["l3_WqT_dram"][0][:, dc * 128:(dc + 1) * 128]
                             .rearrange("(dci p) e -> p dci e", p=128),
                             reads=(W["l3_WqT_dram"][1],), writes=(wq3_b,))
                    self.dma("gpsimd", wk3[...],
                             W["l3_WkT_dram"][0][:, dc * 128:(dc + 1) * 128]
                             .rearrange("(dci p) e -> p dci e", p=128),
                             reads=(W["l3_WkT_dram"][1],), writes=(wk3_b,))
                    self.dma("gpsimd", wv3[...],
                             W["l3_WvT_dram"][0][:, dc * 128:(dc + 1) * 128]
                             .rearrange("(dci p) e -> p dci e", p=128),
                             reads=(W["l3_WvT_dram"][1],), writes=(wv3_b,))
                    for tc in range(DC):
                        t0c = tc * 512
                        for which, wsb, wsb_b, dst, dst_b, bias_t, bias_tb in (
                                (0, wq3, wq3_b, qs, qs_b, qb_, qb_b),
                                (1, wk3, wk3_b, ks, ks_b, kb_, kb_b),
                                (2, wv3, wv3_b, vs, vs_b, vb_, vb_b)):
                            pb = psf[which % 4]
                            terms = [(wsb[:, dci, :], h2[:, dci, t0c:t0c + 512])
                                     for dci in range(DC)]
                            self.mm_group(pb[0][...], pb[1], terms, reads=(wsb_b, h2_b))
                            self.act(dst[:, t0c:t0c + 512], dst_b, pb[0][...], pb[1],
                                     AF.Identity, bias=bias_t[:, dc:dc + 1],
                                     extra_reads=(bias_tb,))
                    self.prog.emit("vector",
                                   lambda E, o=tmps[...], d0=ones_sc[...], d1=qs[...]:
                                   E.tensor_tensor_scan(o, d0, d1, 0.0, ALU.mult, ALU.add),
                                   reads=(ones_sc_b, qs_b), writes=(tmps_b,))
                    self.tt(tmps[...], tmps_b, tmps[...], tmps_b, isc[...], isc_b,
                            ALU.mult)
                    self.tt(tmps[...], tmps_b, tmps[...], tmps_b, ks[...], ks_b,
                            ALU.mult)
                    self.prog.emit("vector",
                                   lambda E, o=qs[...], d0=ones_sc[...], d1=tmps[...]:
                                   E.tensor_tensor_scan(o, d0, d1, 0.0, ALU.mult, ALU.add),
                                   reads=(ones_sc_b, tmps_b), writes=(qs_b,))
                    self.tt(qs[...], qs_b, qs[...], qs_b, isc[...], isc_b, ALU.mult)
                    self.tt(qs[...], qs_b, qs[...], qs_b, vs[...], vs_b, ALU.mult)
                    self.dma("sync", x2cb[...],
                             x2_scr[:, dc * 128:(dc + 1) * 128]
                             .rearrange("(tt p) o -> p tt o", p=128),
                             reads=(x2_scr_b,), writes=(x2cb_b,))
                    for ti in range(NT):
                        pb2 = psf[4 + ti % 2]
                        self.transpose(pb2[0][:, 0:128], pb2[1],
                                       qs[:, ti * 128:(ti + 1) * 128], qs_b,
                                       identf, identf_b)
                        self.tt(x2cb[:, ti, :], x2cb_b, x2cb[:, ti, :], x2cb_b,
                                pb2[0][:, 0:128], pb2[1], ALU.add)
                    self.dma("sync",
                             x3_scr[:, dc * 128:(dc + 1) * 128]
                             .rearrange("(tt p) o -> p tt o", p=128),
                             x2cb[...], reads=(x2cb_b,), writes=(x3_scr_b,))
            self.close_scope(es4)

            es5 = contextlib.ExitStack()
            if True:
                h3, h3_b = self.sb("l3_h3", [128, DC, T], BF16, es5)
                es_ln3 = contextlib.ExitStack()
                with es_ln3:
                    bg, bg_b = self.load_tile(W, "l3_bg_d", [128, 1024], F32, es_ln3, "l3bgt")
                    bb2, bb2_b = self.load_tile(W, "l3_bb_d", [128, 1024], F32, es_ln3, "l3bbt")
                    self.ln_pass_affine(x3_scr, x3_scr_b, h3, h3_b,
                                        bg[...], bg_b, bb2[...], bb2_b,
                                        ident, ident_b, pst, es_ln3, "l3ln3")
                self.close_scope(es_ln3)
                y_sb, y_sb_b = self.sb("l3_yq", [128, 8, 1024], F32, es5)
                w1u, w1u_b, w2u, w2u_b, h_sb, h_sb_b = [], [], [], [], [], []
                for k in range(2):
                    a, b = self.sb(f"l3_bw1u{k}", [128, 8, 512], BF16, es5)
                    w1u.append(a); w1u_b.append(b)
                    a, b = self.sb(f"l3_bw2u{k}", [128, 4, 1024], BF16, es5)
                    w2u.append(a); w2u_b.append(b)
                    a, b = self.sb(f"l3_bh{k}", [128, 4, 512], BF16, es5)
                    h_sb.append(a); h_sb_b.append(b)
                but, but_b = self.load_tile(W, "l3_bu_d", [128, 1024], F32, es5, "l3but")
                but = but[...]
                bbr, bbr_b = W["l3_bbias"]
                WbT, WbT_b = W["l3_WbT_dram"]
                WuT, WuT_b = W["l3_WuT_dram"]
                xg, xg_b = [], []
                for k in range(6):
                    a, b = self.sb(f"l3_xg{k}", [128, D], F32, es5)
                    xg.append(a); xg_b.append(b)
                so, so_b = [], []
                for k in range(2):
                    a, b = self.sb(f"l3_so{k}", [128, D], F32, es5)
                    so.append(a); so_b.append(b)
                NFB = 8
                for quarter in range(4):
                    tq0 = quarter * 8
                    for fb in range(NFB):
                        pw = fb % 2
                        self.dma("gpsimd", w1u[pw][...],
                                 WbT[:, fb * 512:(fb + 1) * 512]
                                 .rearrange("(dc p) f -> p dc f", p=128),
                                 reads=(WbT_b,), writes=(w1u_b[pw],))
                        self.dma("gpsimd", w2u[pw][...],
                                 WuT[fb * 512:(fb + 1) * 512, :]
                                 .rearrange("(fc p) d -> p fc d", p=128),
                                 reads=(WuT_b,), writes=(w2u_b[pw],))
                        for tc in range(2):
                            t0c = tq0 * 128 + tc * 512
                            ph = tc % 2
                            for ft in range(4):
                                pb = psf[ft % 4]
                                terms = [(w1u[pw][:, dc, ft * 128:(ft + 1) * 128],
                                          h3[:, dc, t0c:t0c + 512])
                                         for dc in range(DC)]
                                self.mm_group(pb[0][...], pb[1], terms,
                                              reads=(w1u_b[pw], h3_b))
                                self.act(h_sb[ph][:, ft, :], h_sb_b[ph],
                                         pb[0][...], pb[1], AF.Gelu,
                                         bias=bbr[:, fb * 4 + ft:fb * 4 + ft + 1],
                                         extra_reads=(bbr_b,))
                        for tc in range(2):
                            t0c = tq0 * 128 + tc * 512
                            ph = tc % 2
                            for ti in range(4):
                                itt = (t0c // 128) + ti
                                yq_ti = itt - tq0
                                for dh in range(2):
                                    pb = psf[4 + ((ti * 2 + dh) % 2)]
                                    terms = [(h_sb[ph][:, fc, ti * 128:(ti + 1) * 128],
                                              w2u[pw][:, fc, dh * 512:(dh + 1) * 512])
                                             for fc in range(4)]
                                    self.mm_group(pb[0][...], pb[1], terms,
                                                  reads=(h_sb_b[ph], w2u_b[pw]))
                                    dst = y_sb[:, yq_ti, dh * 512:(dh + 1) * 512]
                                    if fb == 0:
                                        self.act(dst, y_sb_b, pb[0][...], pb[1],
                                                 AF.Identity)
                                    else:
                                        self.tt(dst, y_sb_b, dst, y_sb_b,
                                                pb[0][...], pb[1], ALU.add)
                    for ti in range(8):
                        it = tq0 + ti
                        p = it % 2
                        self.dma("sync", xg[p][...],
                                 x_src_ap[it * 128:(it + 1) * 128, :],
                                 reads=(x_src_buf,), writes=(xg_b[p],))
                        self.dma("sync", xg[2 + p][...],
                                 g_scr[it * 128:(it + 1) * 128, :],
                                 reads=(g_scr_b,), writes=(xg_b[2 + p],))
                        self.dma("sync", xg[4 + p][...],
                                 x3_scr[it * 128:(it + 1) * 128, :],
                                 reads=(x3_scr_b,), writes=(xg_b[4 + p],))
                        self.tt(so[p][...], so_b[p], y_sb[:, ti, :], y_sb_b,
                                but[...], but_b, ALU.add)
                        self.tt(so[p][...], so_b[p], so[p][...], so_b[p],
                                xg[4 + p][...], xg_b[4 + p], ALU.add)
                        if debug:
                            self.dma("sync",
                                     self.dram["l3_s_dump"][it * 128:(it + 1) * 128, :],
                                     so[p][...], reads=(so_b[p],),
                                     writes=(self._sdump_buf3,))
                        self.tt(so[p][...], so_b[p], so[p][...], so_b[p],
                                xg[2 + p][...], xg_b[2 + p], ALU.mult)
                        self.tt(so[p][...], so_b[p], so[p][...], so_b[p],
                                xg[p][...], xg_b[p], ALU.add)
                        self.dma("sync", x_dst_ap[it * 128:(it + 1) * 128, :],
                                 so[p][...], reads=(so_b[p],), writes=(x_dst_buf,))
            self.close_scope(es5)





def prep_l0_host(inp, cp, consts_np):
    W = {}
    nc = cp.nc

    def dram(name, arr, dtype=F32):
        ap, buf = cp.din(name, list(arr.shape), dtype)
        cp.inputs[name] = np.ascontiguousarray(arr, dtype=np.float32)
        return ap, buf

    def sbuf_const(name, arr, sb_dtype):
        ap, buf = dram(name, arr)
        s_ap, s_buf = cp.sb("s_" + name, list(arr.shape), sb_dtype)
        cp.dma("gpsimd", s_ap[...], ap[...], reads=(buf,), writes=(s_buf,))
        return s_ap, s_buf

    W["l0_wqT"] = dram("l0_wqT", inp["L0_attn_wq"].T)
    W["l0_wkT"] = dram("l0_wkT", inp["L0_attn_wk"].T)
    W["l0_wvT"] = dram("l0_wvT", inp["L0_attn_wv"].T)
    W["l0_W1_dram"] = dram("l0_W1", inp["L0_ffn_W1"])
    W["l0_W2_dram"] = dram("l0_W2", inp["L0_ffn_W2"])
    W["cos_d"] = dram("c_cos_d", consts_np["cos_d"])
    W["sin_d"] = dram("c_sin_d", consts_np["sin_d"])
    W["mask4"] = dram("c_mask4", consts_np["mask4"])
    rw = inp["L0_ffn_router_W"].reshape(8, 128, 8).transpose(1, 0, 2)
    W["l0_routerW"] = sbuf_const("l0_rw", rw, BF16)
    rbt = np.tile(inp["L0_ffn_router_b"][None, :], (128, 1))
    W["l0_router_b_t"] = sbuf_const("l0_rbt", rbt, F32)
    b1r = inp["L0_ffn_b1"].reshape(8, 32, 128).transpose(2, 0, 1)
    W["l0_b1"] = sbuf_const("l0_b1r", b1r, F32)
    W["l0_b2"] = sbuf_const("l0_b2", inp["L0_ffn_b2"], BF16)
    return W

def prep_l1_host(inp, cp):
    W = {}

    def dram(name, arr, dtype=F32):
        ap, buf = cp.din(name, list(arr.shape), dtype)
        cp.inputs[name] = np.ascontiguousarray(arr, dtype=np.float32)
        return ap, buf

    def sbuf_const(name, arr, sb_dtype):
        ap, buf = dram(name, arr)
        s_ap, s_buf = cp.sb("s_" + name, list(arr.shape), sb_dtype)
        cp.dma("gpsimd", s_ap[...], ap[...], reads=(buf,), writes=(s_buf,))
        return s_ap, s_buf

    W["l1_WqT_dram"] = dram("l1_WqT", inp["L1_attn_Wq"].T)
    W["l1_WkT_dram"] = dram("l1_WkT", inp["L1_attn_Wk"].T)
    W["l1_WvT_dram"] = dram("l1_WvT", inp["L1_attn_Wv"].T)
    W["l1_WoT_dram"] = dram("l1_WoT", inp["L1_attn_Wo"].T)
    W["l1_WfcT_dram"] = dram("l1_WfcT", inp["L1_ffn_W_fc"].T)
    W["l1_WprT_dram"] = dram("l1_WprT", inp["L1_ffn_W_proj"].T)
    W["l1_gam_d"] = dram("l1_gam", np.tile(inp["L1_attn_ln_gamma"][None, :], (128, 1)))
    W["l1_bet_d"] = dram("l1_bet", np.tile(inp["L1_attn_ln_beta"][None, :], (128, 1)))
    rel = inp["L1_attn_rel"]
    ii = np.arange(128)[:, None]
    jj = np.arange(128)[None, :]
    idx0 = np.clip(0 + ii - jj, 0, 128)
    idx1 = np.clip(128 + ii - jj, 0, 128)
    B0 = rel[:, idx0]
    B1 = rel[:, idx1]
    W["l1_B0_dram"] = dram("l1_B0", B0)
    W["l1_B1_dram"] = dram("l1_B1", B1)
    far = np.tile(rel[:, 128][None, :], (128, 1))
    W["l1_far_dram"] = dram("l1_far", far)
    mskd = np.triu(np.full((128, 128), NEG, np.float32), k=1)
    W["l1_maskdiag_dram"] = dram("l1_maskdiag", mskd)
    return W


def prep_l2_host(inp, cp):
    W = {}

    def dram(name, arr, dtype=F32):
        ap, buf = cp.din(name, list(arr.shape), dtype)
        cp.inputs[name] = np.ascontiguousarray(arr, dtype=np.float32)
        return ap, buf

    def sbuf_const(name, arr, sb_dtype):
        ap, buf = dram(name, arr)
        s_ap, s_buf = cp.sb("s_" + name, list(arr.shape), sb_dtype)
        cp.dma("gpsimd", s_ap[...], ap[...], reads=(buf,), writes=(s_buf,))
        return s_ap, s_buf

    W["l2_Wq_dram"] = dram("l2_Wq", inp["L2_attn_Wq"])
    W["l2_Wk_dram"] = dram("l2_Wk", inp["L2_attn_Wk"])
    W["l2_Wv_dram"] = dram("l2_Wv", inp["L2_attn_Wv"])
    W["l2_Wo_dram"] = dram("l2_Wo", inp["L2_attn_Wo"])
    W["l2_W1_dram"] = dram("l2_W1", inp["L2_ffn_W1"])
    W["l2_W2_dram"] = dram("l2_W2", inp["L2_ffn_W2"])
    W["l2_proj2"] = sbuf_const("l2_proj2",
                               np.concatenate([inp["L2_attn_proj"]] * 2, axis=0), BF16)
    W["l2_ones"] = sbuf_const("l2_onesc", np.ones((128, 1), np.float32), BF16)
    rw = inp["L2_ffn_Wr"].reshape(8, 128, 8).transpose(1, 0, 2)
    W["l2_Wr"] = sbuf_const("l2_rw", rw, BF16)
    W["l2_br_t"] = sbuf_const("l2_rbt", np.tile(inp["L2_ffn_br"][None, :], (128, 1)), F32)
    b1r = inp["L2_ffn_b1"].reshape(8, 16, 128).transpose(2, 0, 1)
    W["l2_b1"] = sbuf_const("l2_b1r", b1r, F32)
    W["l2_b2"] = sbuf_const("l2_b2", inp["L2_ffn_b2"], BF16)
    W["l2_bo_d"] = dram("l2_bot", np.tile(inp["L2_attn_bo"][None, :], (128, 1)))
    return W


def prep_l3_host(inp, cp):
    W = {}

    def dram(name, arr, dtype=F32):
        ap, buf = cp.din(name, list(arr.shape), dtype)
        cp.inputs[name] = np.ascontiguousarray(arr, dtype=np.float32)
        return ap, buf

    def sbuf_const(name, arr, sb_dtype):
        ap, buf = dram(name, arr)
        s_ap, s_buf = cp.sb("s_" + name, list(arr.shape), sb_dtype)
        cp.dma("gpsimd", s_ap[...], ap[...], reads=(buf,), writes=(s_buf,))
        return s_ap, s_buf

    W["l3_WqT_dram"] = dram("l3_WqT", inp["L3_attn_Wq"].T)
    W["l3_WkT_dram"] = dram("l3_WkT", inp["L3_attn_Wk"].T)
    W["l3_WvT_dram"] = dram("l3_WvT", inp["L3_attn_Wv"].T)
    W["l3_WbT_dram"] = dram("l3_WbT", inp["L3_attn_Wb"].T)
    W["l3_WuT_dram"] = dram("l3_WuT", inp["L3_attn_Wu"].T)
    W["l3_w1_dram"] = dram("l3_w1", inp["L3_ffn_w1"])
    W["l3_w2_dram"] = dram("l3_w2", inp["L3_ffn_w2"])
    W["l3_sdown_dram"] = dram("l3_sdown", inp["L3_ffn_shared_down"])
    sgu = inp["L3_ffn_shared_gate_up"]
    sgu_u = np.stack([
        np.stack([sgu[:, u * 512:(u + 1) * 512], sgu[:, 1024 + u * 512:1024 + (u + 1) * 512]],
                 axis=1)
        for u in range(2)], axis=1)
    W["l3_sgu_dram"] = dram("l3_sgu", sgu_u)
    cw = inp["L3_attn_conv_w"]
    blocks = np.zeros((4, 8, 2, 128, 128), np.float32)
    for k in range(4):
        for oc8 in range(8):
            g = oc8 // 2
            for cic in range(2):
                blk = cw[oc8 * 128:(oc8 + 1) * 128, cic * 128:(cic + 1) * 128, k]
                blocks[k, oc8, cic] = blk.T
    W["l3_convw_dram"] = dram("l3_convw", blocks)
    W["l3_cg_d"] = dram("l3_cg", np.tile(inp["L3_attn_ln_conv_g"][None, :], (128, 1)))
    W["l3_cb_d"] = dram("l3_cb", np.tile(inp["L3_attn_ln_conv_b"][None, :], (128, 1)))
    W["l3_ag_d"] = dram("l3_ag", np.tile(inp["L3_attn_ln_attn_g"][None, :], (128, 1)))
    W["l3_ab_d"] = dram("l3_ab", np.tile(inp["L3_attn_ln_attn_b"][None, :], (128, 1)))
    W["l3_bg_d"] = dram("l3_bg", np.tile(inp["L3_attn_ln_boom_g"][None, :], (128, 1)))
    W["l3_bb_d"] = dram("l3_bb", np.tile(inp["L3_attn_ln_boom_b"][None, :], (128, 1)))
    W["l3_bu_d"] = dram("l3_bu", np.tile(inp["L3_attn_bu"][None, :], (128, 1)))
    W["l3_cbias"] = sbuf_const("l3_cbv", inp["L3_attn_conv_b"].reshape(8, 128).T, F32)
    W["l3_bq"] = sbuf_const("l3_bqv", inp["L3_attn_bq"].reshape(8, 128).T, F32)
    W["l3_bk"] = sbuf_const("l3_bkv", inp["L3_attn_bk"].reshape(8, 128).T, F32)
    W["l3_bv"] = sbuf_const("l3_bvv", inp["L3_attn_bv"].reshape(8, 128).T, F32)
    W["l3_bbias"] = sbuf_const("l3_bbv", inp["L3_attn_bb"].reshape(32, 128).T, F32)
    rw = inp["L3_ffn_router_w"].reshape(8, 128, 8).transpose(1, 0, 2)
    W["l3_rw"] = sbuf_const("l3_rwv", rw, BF16)
    W["l3_sgw"] = sbuf_const("l3_sgwv", inp["L3_ffn_shared_gate_w"].reshape(8, 128).T, BF16)
    isc = (1.0 / np.arange(1, T + 1, dtype=np.float64)).astype(np.float32)
    W["l3_iscale_dram"] = dram("l3_iscale", np.tile(isc[None, :], (128, 1)))
    return W



from concourse.bass_utils import run_bass_kernel_spmd


def _build_full(inp):
    cp = CoreProgram()
    consts, c_np = cp.setup_consts()
    W0 = prep_l0_host(inp, cp, c_np)
    W1 = prep_l1_host(inp, cp)
    W2 = prep_l2_host(inp, cp)
    W3 = prep_l3_host(inp, cp)
    x_in_ap, x_in_b = cp.din("x_in", [T, D], F32)
    xA_ap, xA_b = cp.dscratch("xA", [T, D], F32)
    xB_ap, xB_b = cp.dscratch("xB", [T, D], F32)
    xout_ap, xout_b = cp.dout("x_out", [T, D], F32)

    cp.build_layer0(W0, x_in_ap, x_in_b, xA_ap, xA_b, consts)
    cp.prog.global_barrier()
    cp.build_layer1(W1, xA_ap, xA_b, xB_ap, xB_b, consts)
    cp.prog.global_barrier()
    cp.build_layer2(W2, xB_ap, xB_b, xA_ap, xA_b, consts)
    cp.prog.global_barrier()
    cp.build_layer3(W3, xA_ap, xA_b, xout_ap, xout_b, consts)
    nc = cp.finalize()
    return nc, cp


def kernel(**inputs):
    inp = {k: np.asarray(v) for k, v in inputs.items()}
    nc, cp = _build_full(inp)
    x_full = np.ascontiguousarray(inp["L0_attn_x"], dtype=np.float32)
    in_maps = []
    for core in range(8):
        m = dict(cp.inputs)
        m["x_in"] = np.ascontiguousarray(x_full[core])
        in_maps.append(m)
    res = run_bass_kernel_spmd(nc, in_maps, core_ids=list(range(8)))
    out = np.stack([np.asarray(r["x_out"]) for r in res.results]).astype(np.float32)
    return out


# version 08 (final)
# speedup vs baseline: 1.0743x; 1.0743x over previous
import contextlib
import math
import numpy as np
import ml_dtypes
import concourse.bass as bass
import concourse.mybir as mybir

F32 = mybir.dt.float32
BF16 = mybir.dt.bfloat16
AF = mybir.ActivationFunctionType
ALU = mybir.AluOpType
AX = mybir.AxisListType

T = 4096
D = 1024
NT = T // 128
DC = D // 128
NEG = -1.0e30


class Buf:
    __slots__ = ("name", "last_w", "readers")

    def __init__(self, name):
        self.name = name
        self.last_w = None
        self.readers = []


class Tok:
    __slots__ = ("key", "val", "snap")

    def __init__(self, key, val, snap):
        self.key = key
        self.val = val
        self.snap = snap


class Prog:
    ENGINES = ("tensor", "vector", "scalar", "gpsimd", "sync")

    def __init__(self, nc, es):
        self.nc = nc
        self.stream = {e: [] for e in self.ENGINES}
        self.count = {e: 0 for e in self.ENGINES}
        self.dma_count = {e: 0 for e in self.ENGINES}
        self.clock = {e: {} for e in self.ENGINES}
        self.sem = {}
        self.dsem = {}
        for e in self.ENGINES:
            self.sem[e] = es.enter_context(nc.semaphore(f"sp_{e}"))
        for e in ("gpsimd", "sync", "scalar", "vector"):
            self.dsem[e] = es.enter_context(nc.semaphore(f"sd_{e}"))
        self.n_instr = 0
        self.n_waits = 0

    def _need(self, eng, deps):
        need = {}
        for t in deps:
            if t is None:
                continue
            for k, v in t.snap.items():
                if need.get(k, -1) < v:
                    need[k] = v
        ck = self.clock[eng]
        out = {k: v for k, v in need.items() if ck.get(k, 0) < v}
        return out

    def _wait(self, eng, key, val):
        kind, src = key
        if kind == "e":
            sem, target = self.sem[src], val
        else:
            sem, target = self.dsem[src], 16 * val
        self.stream[eng].append(lambda E, s=sem, t=target: E.wait_ge(s, t))
        self.n_waits += 1
        ck = self.clock[eng]
        if ck.get(key, 0) < val:
            ck[key] = val

    def emit(self, eng, fn, reads=(), writes=(), deps=(), sync=True, dma=False):
        all_deps = []
        for b in reads:
            if b.last_w is not None:
                all_deps.append(b.last_w)
        for b in writes:
            if b.last_w is not None:
                all_deps.append(b.last_w)
            all_deps.extend(b.readers)
        all_deps.extend(deps)
        need = self._need(eng, all_deps)
        for k in sorted(need):
            self._wait(eng, k, need[k])
        ck = self.clock[eng]
        for t in all_deps:
            for k, v in t.snap.items():
                if ck.get(k, 0) < v:
                    ck[k] = v

        tok = None
        if dma:
            self.dma_count[eng] += 1
            key = ("d", eng)
            val = self.dma_count[eng]
            sem = self.dsem[eng]

            def op(E, f=fn, s=sem):
                ins = f(E)
                ins.then_inc(s, 16)
            self.stream[eng].append(op)
        elif sync:
            self.count[eng] += 1
            key = ("e", eng)
            val = self.count[eng]
            sem = self.sem[eng]

            def op(E, f=fn, s=sem):
                ins = f(E)
                ins.then_inc(s, 1)
            self.stream[eng].append(op)
        else:
            self.stream[eng].append(fn)
        self.n_instr += 1
        if dma or sync:
            snap = dict(ck)
            snap[key] = val
            tok = Tok(key, val, snap)
            for b in reads:
                b.readers.append(tok)
            for b in writes:
                b.last_w = tok
                b.readers = []
        else:
            pass
        return tok

    def fence(self, bufs):
        out = []
        for b in bufs:
            if b.last_w is not None:
                out.append(b.last_w)
            out.extend(b.readers)
        return out

    def apply_fence(self, deps, engines=None):
        for eng in (engines or self.ENGINES):
            need = self._need(eng, deps)
            for k in sorted(need):
                self._wait(eng, k, need[k])
            ck = self.clock[eng]
            for t in deps:
                if t is None:
                    continue
                for k, v in t.snap.items():
                    if ck.get(k, 0) < v:
                        ck[k] = v

    def global_barrier(self):
        targets = {}
        for e in self.ENGINES:
            if self.count[e]:
                targets[("e", e)] = self.count[e]
        for e in ("gpsimd", "sync", "scalar", "vector"):
            if self.dma_count[e]:
                targets[("d", e)] = self.dma_count[e]
        for eng in self.ENGINES:
            ck = self.clock[eng]
            for k in sorted(targets):
                v = targets[k]
                if k == ("e", eng):
                    continue
                if ck.get(k, 0) < v:
                    self._wait(eng, k, v)

    def finalize_into_block(self, block):
        for eng_name in self.ENGINES:
            ops = self.stream[eng_name]
            if not ops:
                continue

            def body(E, ops=ops):
                for f in ops:
                    f(E)
            getattr(block, eng_name)(body)


def ceil_div(a, b):
    return (a + b - 1) // b


def _f(v):
    return np.float32(v) if isinstance(v, float) else v



def host_consts():
    c = {}
    c["ident"] = np.eye(128, dtype=np.float32)
    R = np.zeros((128, 128), dtype=np.float32)
    for e in range(0, 128, 2):
        R[e, e + 1] = -1.0
        R[e + 1, e] = 1.0
    c["rotT"] = R.T.copy()
    pos = np.arange(T, dtype=np.float64)[:, None]
    inv = 1.0 / (10000.0 ** (np.arange(0, D, 2, dtype=np.float64) / D))
    fr = pos * inv
    cos = np.repeat(np.cos(fr), 2, axis=-1).astype(np.float32)
    sin = np.repeat(np.sin(fr), 2, axis=-1).astype(np.float32)
    c["cos_d"] = np.ascontiguousarray(cos.T)
    c["sin_d"] = np.ascontiguousarray(sin.T)
    m = np.zeros((4, 128, 512), dtype=np.float32)
    for p in range(4):
        for b in range(4):
            if b > p:
                m[p, :, b * 128:(b + 1) * 128] = NEG
            elif b == p:
                blk = np.triu(np.full((128, 128), NEG, np.float32), k=1)
                m[p, :, b * 128:(b + 1) * 128] = blk
    c["mask4"] = m
    return c



class CoreProgram:
    def __init__(self, debug_taps=()):
        self.nc = bass.Bass("TRN2", target_bir_lowering=False)
        self.es = contextlib.ExitStack()
        self.prog = Prog(self.nc, self.es)
        self.inputs = {}
        self.dram = {}
        self.debug_taps = set(debug_taps)
        self.sbytes = 0
        self.scope_bufs = {}

    def din(self, name, shape, dtype=F32):
        t = self.nc.dram_tensor(name, list(shape), dtype, kind="ExternalInput")
        self.dram[name] = t.ap()
        return self.dram[name], Buf("d_" + name)

    def dout(self, name, shape, dtype=F32):
        t = self.nc.dram_tensor(name, list(shape), dtype, kind="ExternalOutput")
        self.dram[name] = t.ap()
        return self.dram[name], Buf("d_" + name)

    def dscratch(self, name, shape, dtype=F32):
        t = self.nc.dram_tensor(name, list(shape), dtype, kind="Internal")
        self.dram[name] = t.ap()
        return self.dram[name], Buf("d_" + name)

    def sb(self, name, shape, dtype=F32, cm=None, side=None):
        if side is None:
            side = "left" if cm is None else "right"
        h = (cm or self.es).enter_context(
            self.nc.sbuf_tensor(name, list(shape), dtype, side=side))
        itm = 4 if dtype == F32 else 2
        self.sbytes += int(np.prod(shape[1:])) * itm
        buf = Buf(name)
        self.scope_bufs.setdefault(id(cm or self.es), []).append(buf)
        import os as _os
        if _os.environ.get("MK_DEBUG_ALLOC"):
            print(f"  sb {name} {shape} {dtype} side={side} bytes/part={int(np.prod(shape[1:])) * itm}")
        return h.ap(), buf

    def ps(self, name, shape, dtype=F32, cm=None):
        h = (cm or self.es).enter_context(self.nc.psum_tensor(name, list(shape), dtype))
        return h.ap(), Buf(name)

    def dma(self, q, out_ap, in_ap, reads=(), writes=(), deps=()):
        return self.prog.emit(q, lambda E: E.dma_start(out=out_ap, in_=in_ap),
                              reads=reads, writes=writes, deps=deps, dma=True)

    def mm_group(self, ps_ap_fn, ps_buf, terms, reads, deps=()):
        n = len(terms)
        tok = None
        for i, (lt, rh) in enumerate(terms):
            st = (i == 0)
            sp = (i == n - 1)
            fn = (lambda E, lt=lt, rh=rh, st=st, sp=sp:
                  E.matmul(ps_ap_fn, lt, rh, start=st, stop=sp))
            if i < n - 1:
                self.prog.emit("tensor", fn, reads=reads, writes=(ps_buf,),
                               deps=deps, sync=False)
            else:
                tok = self.prog.emit("tensor", fn, reads=reads, writes=(ps_buf,),
                                     deps=deps, sync=True)
        return tok

    def act(self, out_ap, out_buf, in_ap, in_buf, func=AF.Identity, bias=0.0,
            scale=1.0, accum=None, accum_buf=None, extra_reads=(), deps=()):
        reads = [in_buf] + list(extra_reads)
        writes = [out_buf] + ([accum_buf] if accum_buf is not None else [])
        bias = _f(bias); scale = _f(scale)
        if accum is None:
            fn = lambda E: E.activation(out_ap, in_ap, func, bias=bias, scale=scale)
        else:
            fn = lambda E: E.activation(out_ap, in_ap, func, bias=bias, scale=scale,
                                        accum_out=accum)
        return self.prog.emit("scalar", fn, reads=reads, writes=writes, deps=deps)

    def scopy(self, out_ap, out_buf, in_ap, in_buf, deps=()):
        return self.prog.emit("scalar", lambda E: E.copy(out_ap, in_ap),
                              reads=(in_buf,), writes=(out_buf,), deps=deps)

    def tt(self, out_ap, out_buf, a_ap, a_buf, b_ap, b_buf, op, deps=()):
        return self.prog.emit("vector",
                              lambda E: E.tensor_tensor(out_ap, a_ap, b_ap, op),
                              reads=(a_buf, b_buf), writes=(out_buf,), deps=deps)

    def ts(self, out_ap, out_buf, in_ap, in_buf, s1, s2, op0, op1=None,
           s_bufs=(), deps=()):
        s1 = float(s1) if isinstance(s1, (int, np.floating)) and not hasattr(s1, "ap") else s1
        if op1 is None:
            fn = lambda E: E.tensor_scalar(out_ap, in_ap, s1, None, op0)
        else:
            fn = lambda E: E.tensor_scalar(out_ap, in_ap, s1, s2, op0, op1)
        return self.prog.emit("vector", fn, reads=(in_buf,) + tuple(s_bufs),
                              writes=(out_buf,), deps=deps)

    def stt(self, out_ap, out_buf, in0_ap, in0_buf, scalar_ap, scalar_buf,
            in1_ap, in1_buf, op0, op1, deps=()):
        fn = lambda E: E.scalar_tensor_tensor(out_ap, in0_ap, scalar_ap, in1_ap, op0, op1)
        reads = tuple(b for b in (in0_buf, scalar_buf, in1_buf) if b is not None)
        return self.prog.emit("vector", fn, reads=reads, writes=(out_buf,), deps=deps)

    def vred(self, out_ap, out_buf, in_ap, in_buf, op, deps=()):
        fn = lambda E: E.tensor_reduce(out_ap, in_ap, AX.X, op)
        return self.prog.emit("vector", fn, reads=(in_buf,), writes=(out_buf,), deps=deps)

    def vcopy(self, out_ap, out_buf, in_ap, in_buf, deps=()):
        return self.prog.emit("vector", lambda E: E.tensor_copy(out_ap, in_ap),
                              reads=(in_buf,), writes=(out_buf,), deps=deps)

    def recip(self, out_ap, out_buf, in_ap, in_buf, deps=()):
        return self.prog.emit("vector", lambda E: E.reciprocal(out_ap, in_ap),
                              reads=(in_buf,), writes=(out_buf,), deps=deps)

    def transpose(self, ps_ap, ps_buf, in_ap, in_buf, ident_ap, ident_buf, deps=()):
        fn = lambda E: E.transpose(ps_ap, in_ap, ident_ap)
        return self.prog.emit("tensor", fn, reads=(in_buf, ident_buf),
                              writes=(ps_buf,), deps=deps)

    def load_const_tile(self, name, arr_shape, dtype, q="gpsimd"):
        d_ap, d_buf = self.din(name, arr_shape, F32)
        sb_ap, sb_buf = self.sb("c_" + name, arr_shape, dtype)
        self.dma(q, sb_ap[...], d_ap[...], writes=(sb_buf,))
        return sb_ap, sb_buf

    def ln_pass(self, x_src_ap, x_src_buf, xn_dmaj, xn_dmaj_buf, ident, ident_b,
                pstr, pstr_b, es_ph, tag):
        P = self.prog
        xt, xt_b = [], []
        xc, xc_b = [], []
        xnb, xnb_b = [], []
        for k in range(2):
            a, b = self.sb(f"{tag}_xt{k}", [128, 1024], F32, es_ph); xt.append(a); xt_b.append(b)
            a, b = self.sb(f"{tag}_xc{k}", [128, 1024], F32, es_ph); xc.append(a); xc_b.append(b)
            a, b = self.sb(f"{tag}_xnb{k}", [128, 1024], BF16, es_ph); xnb.append(a); xnb_b.append(b)
        colsL, colsL_b = [], []
        for k in range(2):
            a, b = self.sb(f"{tag}_lncols{k}", [128, 8], F32, es_ph)
            colsL.append(a); colsL_b.append(b)
        for i in range(NT):
            p = i % 2
            cols, cols_b = colsL[p], colsL_b[p]
            self.dma("sync", xt[p][...], x_src_ap[i * 128:(i + 1) * 128, :],
                     reads=(x_src_buf,), writes=(xt_b[p],))
            c0 = 0
            s_ap = cols[:, c0:c0 + 1]
            self.vred(s_ap, cols_b, xt[p][...], xt_b[p], ALU.add)
            nm_ap = cols[:, c0 + 1:c0 + 2]
            self.ts(nm_ap, cols_b, s_ap, cols_b, -1.0 / D, None, ALU.mult)
            self.ts(xc[p][...], xc_b[p], xt[p][...], xt_b[p], nm_ap, None, ALU.add,
                    s_bufs=(cols_b,))
            v_ap = cols[:, c0 + 2:c0 + 3]
            self.act(xt[p][...], xt_b[p], xc[p][...], xc_b[p], AF.Square,
                     accum=v_ap, accum_buf=cols_b)
            ln_ap = cols[:, c0 + 3:c0 + 4]
            self.act(ln_ap, cols_b, v_ap, cols_b, AF.Ln, bias=1.0e-5, scale=1.0 / D)
            self.act(ln_ap, cols_b, ln_ap, cols_b, AF.Exp, scale=-0.5)
            self.ts(xnb[p][...], xnb_b[p], xc[p][...], xc_b[p], ln_ap, None, ALU.mult,
                    s_bufs=(cols_b,))
            for dc in range(DC):
                pb = dc % 2
                self.transpose(pstr[pb][...], pstr_b[pb],
                               xnb[p][:, dc * 128:(dc + 1) * 128], xnb_b[p],
                               ident, ident_b)
                self.vcopy(xn_dmaj[:, dc, i * 128:(i + 1) * 128], xn_dmaj_buf,
                           pstr[pb][...], pstr_b[pb])

    def load_tile(self, W, key, shape, dtype, cm, tag):
        d_ap, d_buf = W[key]
        s_ap, s_buf = self.sb(tag, shape, dtype, cm)
        q = "sync" if dtype == F32 else "gpsimd"
        self.dma(q, s_ap[...], d_ap[...], reads=(d_buf,), writes=(s_buf,))
        return s_ap, s_buf

    def close_scope(self, es):
        bufs = self.scope_bufs.pop(id(es), [])
        deps = self.prog.fence(bufs)
        es.close()
        self.prog.apply_fence(deps)

    def build_layer0(self, W, x_src_ap, x_src_buf, x_dst_ap, x_dst_buf, consts,
                     debug=False):
        P = self.prog
        ident, ident_b = consts["ident"]
        rotT, rotT_b = consts["rotT"]
        psf = consts["psf"]
        pst = consts["pst"]

        if debug:
            _, self._sdump_buf = self.dout("l0_s_dump", [T, D], F32)
            g_scr_is_out = True
        else:
            g_scr_is_out = False
        esL = contextlib.ExitStack()
        with esL:
            if g_scr_is_out:
                g_scr, g_scr_b = self.dout("l0_g_dump", [T, D], F32)
            else:
                g_scr, g_scr_b = self.dscratch("l0_g_scr", [T, D], F32)
            q_scr, q_scr_b = self.dscratch("l0_q_scr", [DC, 128, T], BF16)
            k_scr, k_scr_b = self.dscratch("l0_k_scr", [DC, 128, T], BF16)

            es_xn = contextlib.ExitStack()
            with es_xn:
                xn, xn_b = self.sb("l0_xn", [128, DC, T], BF16, es_xn)
                es_ln = contextlib.ExitStack()
                with es_ln:
                    self.ln_pass(x_src_ap, x_src_buf, xn, xn_b, ident, ident_b,
                                 [p[0] for p in pst], [p[1] for p in pst], es_ln, "l0ln")
                self.close_scope(es_ln)

                es0 = contextlib.ExitStack()
                es0.__enter__()
                gates, gates_b = self.sb("l0_gates", [128, NT, 8], F32, es0)
                gates_dm, gates_dm_b = self.sb("l0_gates_dm", [8, T], BF16, es0)
                rw_ap, rw_b = self.load_tile(W, "l0_rw_d", [128, 8, 8], BF16, es0, "l0rwt")
                rw = rw_ap
                rbt_ap, rbt_b = self.load_tile(W, "l0_rbt_d", [128, 8], F32, es0, "l0rbtt")
                rbt = rbt_ap
                gt_tmp, gt_tmp_b = self.sb("l0_gt", [128, 4, 8], F32, es0)
                gt_bf, gt_bf_b = self.sb("l0_gtbf", [128, 8], BF16, es0)
                gcol, gcol_b = self.sb("l0_gcol", [128, 8], F32, es0)
                for i in range(NT):
                    pb = psf[i % 2]
                    terms = [(xn[:, dc, i * 128:(i + 1) * 128], rw[:, dc, :])
                             for dc in range(DC)]
                    self.mm_group(pb[0][:, 0:8], pb[1], terms, reads=(xn_b, rw_b))
                    t0 = gt_tmp[:, (i % 2) * 2, :]
                    t1 = gt_tmp[:, (i % 2) * 2 + 1, :]
                    self.tt(t0, gt_tmp_b, pb[0][:, 0:8], pb[1], rbt[...], rbt_b, ALU.add)
                    c0 = (i % 2) * 4
                    m_ap = gcol[:, c0:c0 + 1]
                    self.vred(m_ap, gcol_b, t0, gt_tmp_b, ALU.max)
                    nm = gcol[:, c0 + 1:c0 + 2]
                    self.ts(nm, gcol_b, m_ap, gcol_b, -1.0, None, ALU.mult)
                    l_ap = gcol[:, c0 + 2:c0 + 3]
                    self.act(t1, gt_tmp_b, t0, gt_tmp_b, AF.Exp, bias=nm, scale=1.0,
                             accum=l_ap, accum_buf=gcol_b, extra_reads=(gcol_b,))
                    r_ap = gcol[:, c0 + 3:c0 + 4]
                    self.recip(r_ap, gcol_b, l_ap, gcol_b)
                    self.ts(t1, gt_tmp_b, t1, gt_tmp_b, r_ap, None, ALU.mult,
                            s_bufs=(gcol_b,))
                    self.vred(m_ap, gcol_b, t1, gt_tmp_b, ALU.max)
                    self.ts(gates[:, i, :], gates_b, t1, gt_tmp_b, m_ap, None, ALU.is_ge,
                            s_bufs=(gcol_b,))
                    self.tt(gates[:, i, :], gates_b, gates[:, i, :], gates_b, t1,
                            gt_tmp_b, ALU.mult)
                    self.vcopy(gt_bf[...], gt_bf_b, gates[:, i, :], gates_b)
                    self.transpose(pst[0][0][0:8, :], pst[0][1], gt_bf[...], gt_bf_b,
                                   ident, ident_b)
                    self.vcopy(gates_dm[:, i * 128:(i + 1) * 128], gates_dm_b,
                               pst[0][0][0:8, :], pst[0][1])

                b2sb, b2sb_b = self.load_tile(W, "l0_b2_d", [8, 1024], BF16, es0, "l0b2t")
                b1sb, b1sb_b = self.load_tile(W, "l0_b1_d", [128, 8, 32], F32, es0, "l0b1t")
                g_sb, g_sb_b = self.sb("l0_gq", [128, 8, 1024], F32, es0)
                w1u, w1u_b, w2u, w2u_b, h_sb, h_sb_b = [], [], [], [], [], []
                for k in range(2):
                    a, b = self.sb(f"l0_w1u{k}", [128, 8, 512], BF16, es0)
                    w1u.append(a); w1u_b.append(b)
                    a, b = self.sb(f"l0_w2u{k}", [128, 4, 1024], BF16, es0)
                    w2u.append(a); w2u_b.append(b)
                    a, b = self.sb(f"l0_h{k}", [128, 4, 512], BF16, es0)
                    h_sb.append(a); h_sb_b.append(b)
                gsil, gsil_b = self.sb("l0_gsil", [128, 2, 1024], F32, es0)
                W1d, W1d_b = W["l0_W1_dram"]
                W2d, W2d_b = W["l0_W2_dram"]

                for quarter in range(4):
                    tq0 = quarter * 8
                    for ti in range(8):
                        it = tq0 + ti
                        pb = psf[4 + (ti % 2)]
                        for dh in range(2):
                            self.mm_group(
                                pb[0][...], pb[1],
                                [(gates_dm[:, it * 128:(it + 1) * 128],
                                  b2sb[:, dh * 512:(dh + 1) * 512])],
                                reads=(gates_dm_b, b2sb_b))
                            self.act(g_sb[:, ti, dh * 512:(dh + 1) * 512], g_sb_b,
                                     pb[0][...], pb[1], AF.Identity)
                    for e in range(8):
                        for fb in range(8):
                            u = e * 8 + fb
                            pw = u % 2
                            self.dma("gpsimd", w1u[pw][...],
                                     W1d[e, :, fb * 512:(fb + 1) * 512]
                                     .rearrange("(dc p) f -> p dc f", p=128),
                                     reads=(W1d_b,), writes=(w1u_b[pw],))
                            self.dma("gpsimd", w2u[pw][...],
                                     W2d[e, fb * 512:(fb + 1) * 512, :]
                                     .rearrange("(fc p) d -> p fc d", p=128),
                                     reads=(W2d_b,), writes=(w2u_b[pw],))
                            for tc in range(2):
                                t0c = tq0 * 128 + tc * 512
                                ph = tc % 2
                                for ft in range(4):
                                    pb = psf[ft % 4]
                                    terms = [(w1u[pw][:, dc, ft * 128:(ft + 1) * 128],
                                              xn[:, dc, t0c:t0c + 512])
                                             for dc in range(DC)]
                                    self.mm_group(pb[0][...], pb[1], terms,
                                                  reads=(w1u_b[pw], xn_b))
                                    self.act(h_sb[ph][:, ft, :], h_sb_b[ph],
                                             pb[0][...], pb[1], AF.Relu,
                                             bias=b1sb[:, e, fb * 4 + ft:fb * 4 + ft + 1],
                                             extra_reads=(b1sb_b,))
                            for tc in range(2):
                                t0c = tq0 * 128 + tc * 512
                                ph = tc % 2
                                for ti in range(4):
                                    itt = (t0c // 128) + ti
                                    gq_ti = itt - tq0
                                    for dh in range(2):
                                        pb = psf[4 + ((ti * 2 + dh) % 2)]
                                        terms = [(h_sb[ph][:, fc, ti * 128:(ti + 1) * 128],
                                                  w2u[pw][:, fc, dh * 512:(dh + 1) * 512])
                                                 for fc in range(4)]
                                        self.mm_group(pb[0][...], pb[1], terms,
                                                      reads=(h_sb_b[ph], w2u_b[pw]))
                                        self.stt(g_sb[:, gq_ti, dh * 512:(dh + 1) * 512],
                                                 g_sb_b, pb[0][...], pb[1],
                                                 gates[:, itt, e:e + 1], gates_b,
                                                 g_sb[:, gq_ti, dh * 512:(dh + 1) * 512],
                                                 g_sb_b, ALU.mult, ALU.add)
                    for ti in range(8):
                        it = tq0 + ti
                        self.act(gsil[:, ti % 2, :], gsil_b, g_sb[:, ti, :], g_sb_b,
                                 AF.Silu)
                        self.dma("sync", g_scr[it * 128:(it + 1) * 128, :],
                                 gsil[:, ti % 2, :], reads=(gsil_b,), writes=(g_scr_b,))

                self.close_scope(es0)

                cosd, cosd_b = W["cos_d"]
                sind, sind_b = W["sin_d"]
                esB = contextlib.ExitStack()
                if True:
                    wT, wT_b = [], []
                    for k in range(2):
                        a, b = self.sb(f"l0_wT{k}", [128, DC, D], BF16, esB)
                        wT.append(a); wT_b.append(b)
                    csc, csc_b = self.sb("l0_csc", [128, DC, 256], F32, esB)
                    css, css_b = self.sb("l0_css", [128, DC, 256], F32, esB)
                    tA, tA_b = self.sb("l0_tA", [128, DC, 256], F32, esB)
                    qbf, qbf_b = self.sb("l0_qbf", [128, 2, DC, 256], BF16, esB)
                    tB, tB_b = self.sb("l0_tB", [128, 2, 256], F32, esB)
                    self.dma("gpsimd", wT[0][...],
                             W["l0_wqT"][0].rearrange("(dc p) e -> p dc e", p=128),
                             reads=(W["l0_wqT"][1],), writes=(wT_b[0],))
                    self.dma("gpsimd", wT[1][...],
                             W["l0_wkT"][0].rearrange("(dc p) e -> p dc e", p=128),
                             reads=(W["l0_wkT"][1],), writes=(wT_b[1],))
                    for tc in range(T // 256):
                        t0c = tc * 256
                        for dc in range(DC):
                            self.dma("scalar", csc[:, dc, :],
                                     cosd[dc * 128:(dc + 1) * 128, t0c:t0c + 256],
                                     reads=(cosd_b,), writes=(csc_b,))
                            self.dma("scalar", css[:, dc, :],
                                     sind[dc * 128:(dc + 1) * 128, t0c:t0c + 256],
                                     reads=(sind_b,), writes=(css_b,))
                        for wi, dst_scr, dst_scr_b in ((0, q_scr, q_scr_b),
                                                       (1, k_scr, k_scr_b)):
                            for dc in range(DC):
                                pb = psf[dc % 4]
                                terms = [(wT[wi][:, dci, dc * 128:(dc + 1) * 128],
                                          xn[:, dci, t0c:t0c + 256]) for dci in range(DC)]
                                self.mm_group(pb[0][:, 0:256], pb[1], terms,
                                              reads=(wT_b[wi], xn_b))
                                self.act(tA[:, dc, :], tA_b, pb[0][:, 0:256], pb[1],
                                         AF.Identity)
                                self.vcopy(qbf[:, wi, dc, :], qbf_b, tA[:, dc, :],
                                           tA_b)
                            for dc in range(DC):
                                pr = psf[4 + dc % 2]
                                self.mm_group(pr[0][:, 0:256], pr[1],
                                              [(rotT, qbf[:, wi, dc, :])],
                                              reads=(rotT_b, qbf_b))
                                t1 = tB[:, 0, :]
                                t2 = tB[:, 1, :]
                                self.tt(t1, tB_b, tA[:, dc, :], tA_b,
                                        csc[:, dc, :], csc_b, ALU.mult)
                                self.tt(t2, tB_b, pr[0][:, 0:256], pr[1],
                                        css[:, dc, :], css_b, ALU.mult)
                                qo = qbf[:, wi, dc, :]
                                self.tt(qo, qbf_b, t1, tB_b, t2, tB_b, ALU.add)
                                self.dma("sync", dst_scr[dc, :, t0c:t0c + 256], qo,
                                         reads=(qbf_b,), writes=(dst_scr_b,))
                    v_tm, v_tm_b = self.sb("l0_v", [128, NT, D], BF16, esL, side="left")
                    wd, wd_b = W["l0_wvT"]
                    self.dma("gpsimd", wT[0][...],
                             wd.rearrange("(dc p) e -> p dc e", p=128),
                             reads=(wd_b,), writes=(wT_b[0],))
                    for it in range(NT):
                        for eh in range(2):
                            pb = psf[4 + eh]
                            terms = [(xn[:, dci, it * 128:(it + 1) * 128],
                                      wT[0][:, dci, eh * 512:(eh + 1) * 512])
                                     for dci in range(DC)]
                            self.mm_group(pb[0][...], pb[1], terms, reads=(xn_b, wT_b[0]))
                            self.act(v_tm[:, it, eh * 512:(eh + 1) * 512], v_tm_b,
                                     pb[0][...], pb[1], AF.Identity)
                    pass
                self.close_scope(esB)
            self.close_scope(es_xn)

            esC = contextlib.ExitStack()
            if True:
                kd, kd_b = self.sb("l0_k", [128, DC, T], BF16, esC)
                self.dma("sync", kd[...], k_scr.rearrange("dc p t -> p dc t"),
                         reads=(k_scr_b,), writes=(kd_b,))
                mask, mask_b = self.sb("l0_mask", [128, 4, 512], F32, esC)
                self.dma("sync", mask[...], W["mask4"][0].rearrange("v p f -> p v f"),
                         reads=(W["mask4"][1],), writes=(mask_b,))
                qt, qt_b = [], []
                for k in range(2):
                    a, b = self.sb(f"l0_qt{k}", [128, DC, 128], BF16, esC)
                    qt.append(a); qt_b.append(b)
                PsbL, PsbL_b = [], []
                for k in range(2):
                    a, b = self.sb(f"l0_P{k}", [128, T], BF16, esC)
                    PsbL.append(a); PsbL_b.append(b)
                PtL, PtL_b = [], []
                for k in range(2):
                    a, b = self.sb(f"l0_Pt{k}", [128, 8, 128], BF16, esC)
                    PtL.append(a); PtL_b.append(b)
                yaccL, yaccL_b = [], []
                for k in range(2):
                    a, b = self.sb(f"l0_yacc{k}", [128, D], F32, esC)
                    yaccL.append(a); yaccL_b.append(b)
                xg, xg_b = [], []
                for k in range(4):
                    a, b = self.sb(f"l0_xg{k}", [128, D], F32, esC)
                    xg.append(a); xg_b.append(b)
                xo, xo_b = [], []
                for k in range(2):
                    a, b = self.sb(f"l0_xo{k}", [128, D], F32, esC)
                    xo.append(a); xo_b.append(b)
                mlsL, mlsL_b = [], []
                for k in range(2):
                    a, b = self.sb(f"l0_mls{k}", [128, 24], F32, esC)
                    mlsL.append(a); mlsL_b.append(b)

                def softmax0(it):
                    p = it % 2
                    cols, cols_b = mlsL[p], mlsL_b[p]
                    Psb, Psb_b = PsbL[p], PsbL_b[p]
                    jext = (it + 1) * 128
                    n_ch = ceil_div(jext, 512)
                    m_run = cols[:, 0:1]
                    bias_c = cols[:, 3:4]
                    lfin = cols[:, 4:5]
                    rec = cols[:, 5:6]
                    tmp = cols[:, 6:7]
                    corr = cols[:, 7:8]
                    self.dma("sync", qt[p][...],
                             q_scr[:, :, it * 128:(it + 1) * 128]
                             .rearrange("dc p t -> p dc t"),
                             reads=(q_scr_b,), writes=(qt_b[p],))
                    self.dma("sync", xg[p][...], x_src_ap[it * 128:(it + 1) * 128, :],
                             reads=(x_src_buf,), writes=(xg_b[p],))
                    self.dma("sync", xg[2 + p][...], g_scr[it * 128:(it + 1) * 128, :],
                             reads=(g_scr_b,), writes=(xg_b[2 + p],))
                    n_grp = ceil_div(n_ch, 4)
                    for g0 in range(n_grp):
                        ch0, ch1 = g0 * 4, min(n_ch, g0 * 4 + 4)
                        for ch in range(ch0, ch1):
                            c0 = ch * 512
                            cw = min(512, jext - c0)
                            pb = psf[ch - ch0]
                            terms = [(qt[p][:, dc, :], kd[:, dc, c0:c0 + cw])
                                     for dc in range(DC)]
                            self.mm_group(pb[0][:, 0:cw], pb[1], terms,
                                          reads=(qt_b[p], kd_b))
                            if c0 <= it * 128 < c0 + 512:
                                self.tt(pb[0][:, 0:cw], pb[1], pb[0][:, 0:cw], pb[1],
                                        mask[:, it % 4, 0:cw], mask_b, ALU.add)
                            self.vred(cols[:, 8 + ch - ch0:9 + ch - ch0], cols_b,
                                      pb[0][:, 0:cw], pb[1], ALU.max)
                        ng = ch1 - ch0
                        if ng == 1:
                            m_g = cols[:, 8:9]
                        else:
                            m_g = cols[:, 2:3]
                            self.vred(m_g, cols_b, cols[:, 8:8 + ng], cols_b, ALU.max)
                        if g0 == 0:
                            self.vcopy(m_run, cols_b, m_g, cols_b)
                        else:
                            self.tt(tmp, cols_b, m_run, cols_b, m_g, cols_b, ALU.max)
                            self.tt(corr, cols_b, m_run, cols_b, tmp, cols_b,
                                    ALU.subtract)
                            self.act(corr, cols_b, corr, cols_b, AF.Exp, scale=1.0 / 32.0)
                            self.ts(Psb[:, 0:ch0 * 512], Psb_b, Psb[:, 0:ch0 * 512],
                                    Psb_b, corr, None, ALU.mult, s_bufs=(cols_b,))
                            self.ts(lfin, cols_b, lfin, cols_b, corr, None, ALU.mult,
                                    s_bufs=(cols_b,))
                            self.vcopy(m_run, cols_b, tmp, cols_b)
                        self.ts(bias_c, cols_b, m_run, cols_b, -1.0 / 32.0, None, ALU.mult)
                        for ch in range(ch0, ch1):
                            c0 = ch * 512
                            cw = min(512, jext - c0)
                            pb = psf[ch - ch0]
                            self.act(Psb[:, c0:c0 + cw], Psb_b, pb[0][:, 0:cw], pb[1],
                                     AF.Exp, bias=bias_c, scale=1.0 / 32.0,
                                     accum=cols[:, 16 + ch - ch0:17 + ch - ch0],
                                     accum_buf=cols_b, extra_reads=(cols_b,))
                        lg = cols[:, 1:2]
                        if ng == 1:
                            self.vcopy(lg, cols_b, cols[:, 16:17], cols_b)
                        else:
                            self.vred(lg, cols_b, cols[:, 16:16 + ng], cols_b, ALU.add)
                        if g0 == 0:
                            self.vcopy(lfin, cols_b, lg, cols_b)
                        else:
                            self.tt(lfin, cols_b, lfin, cols_b, lg, cols_b, ALU.add)
                    self.recip(rec, cols_b, lfin, cols_b)
                    self.ts(Psb[:, 0:jext], Psb_b, Psb[:, 0:jext], Psb_b, rec, None,
                            ALU.mult, s_bufs=(cols_b,))

                def pv0(it):
                    p = it % 2
                    Psb, Psb_b = PsbL[p], PsbL_b[p]
                    PtH, PtH_b = PtL[p], PtL_b[p]
                    jext = (it + 1) * 128
                    n_jt = jext // 128
                    for jg in range(0, n_jt, 8):
                        je = min(n_jt, jg + 8)
                        for jj in range(jg, je):
                            self.transpose(pst[jj % 2][0][...], pst[jj % 2][1],
                                           Psb[:, jj * 128:(jj + 1) * 128], Psb_b,
                                           ident, ident_b)
                            self.vcopy(PtH[:, jj - jg, :], PtH_b,
                                       pst[jj % 2][0][...], pst[jj % 2][1])
                        for jj in range(jg, je):
                            for eh in range(2):
                                pb = psf[4 + eh]
                                fn = (lambda E, lt=PtH[:, jj - jg, :],
                                      rh=v_tm[:, jj, eh * 512:(eh + 1) * 512],
                                      o=pb[0][...], st=(jj == 0), sp=(jj == n_jt - 1):
                                      E.matmul(o, lt, rh, start=st, stop=sp))
                                self.prog.emit("tensor", fn, reads=(PtH_b, v_tm_b),
                                               writes=(pb[1],), sync=(jj == n_jt - 1))
                    for eh in range(2):
                        pb = psf[4 + eh]
                        sl = slice(eh * 512, (eh + 1) * 512)
                        self.tt(xo[p][:, sl], xo_b[p], pb[0][...], pb[1],
                                xg[2 + p][:, sl], xg_b[2 + p], ALU.mult)
                        self.tt(xo[p][:, sl], xo_b[p], xo[p][:, sl], xo_b[p],
                                xg[p][:, sl], xg_b[p], ALU.add)
                    self.dma("sync", x_dst_ap[it * 128:(it + 1) * 128, :], xo[p][...],
                             reads=(xo_b[p],), writes=(x_dst_buf,))

                for it in range(NT):
                    softmax0(it)
                    if it >= 1:
                        pv0(it - 1)
                pv0(NT - 1)
                pass
            self.close_scope(esC)

    def setup_consts(self):
        es = self.es
        consts = {}
        c = host_consts()
        a, b = self.load_const_tile("c_ident", [128, 128], BF16)
        consts["ident"] = (a[...], b)
        self.inputs["c_ident"] = c["ident"]
        a, b = self.load_const_tile("c_rotT", [128, 128], BF16)
        consts["rotT"] = (a[...], b)
        self.inputs["c_rotT"] = c["rotT"]
        a, b = self.load_const_tile("c_identf", [128, 128], F32)
        consts["identf"] = (a[...], b)
        self.inputs["c_identf"] = c["ident"]
        psf = []
        for i in range(6):
            ap, bb = self.ps(f"psf{i}", [128, 512], F32)
            psf.append((ap, bb))
        pstl = []
        for i in range(2):
            ap, bb = self.ps(f"pst{i}", [128, 128], BF16)
            pstl.append((ap, bb))
        consts["psf"] = psf
        consts["pst"] = pstl
        return consts, c

    def final_drain(self):
        P = self.prog
        for q in ("sync", "gpsimd", "scalar", "vector"):
            n = P.dma_count[q]
            if n:
                P.stream["sync"].append(
                    lambda E, s=P.dsem[q], t=16 * n: E.wait_ge(s, t))

    def finalize(self):
        self.final_drain()
        blk = self.es.enter_context(self.nc.Block())
        self.prog.finalize_into_block(blk)
        self.es.close()
        return self.nc


    def ln_pass_dual(self, x_src_ap, x_src_buf, xn_dmaj, xn_dmaj_buf,
                     xn_scr, xn_scr_buf, xn2_scr, xn2_scr_buf,
                     gam_ap, gam_b, bet_ap, bet_b,
                     ident, ident_b, pst, es_ph, tag):
        xt, xt_b, xc, xc_b, xnb, xnb_b, x2b, x2b_b, x2f, x2f_b = [], [], [], [], [], [], [], [], [], []
        for k in range(2):
            a, b = self.sb(f"{tag}_xt{k}", [128, 1024], F32, es_ph); xt.append(a); xt_b.append(b)
            a, b = self.sb(f"{tag}_xc{k}", [128, 1024], F32, es_ph); xc.append(a); xc_b.append(b)
            a, b = self.sb(f"{tag}_xnb{k}", [128, 1024], BF16, es_ph); xnb.append(a); xnb_b.append(b)
            a, b = self.sb(f"{tag}_x2b{k}", [128, 1024], BF16, es_ph); x2b.append(a); x2b_b.append(b)
            a, b = self.sb(f"{tag}_x2f{k}", [128, 1024], F32, es_ph); x2f.append(a); x2f_b.append(b)
        xnf, xnf_b = [], []
        for k in range(2):
            a, b = self.sb(f"{tag}_xnf{k}", [128, 1024], F32, es_ph); xnf.append(a); xnf_b.append(b)
        x2st, x2st_b = [], []
        for k in range(2):
            a, b = self.sb(f"{tag}_x2st{k}", [128, DC, 128], BF16, es_ph); x2st.append(a); x2st_b.append(b)
        colsL, colsL_b = [], []
        for k in range(2):
            a, b = self.sb(f"{tag}_cols{k}", [128, 16], F32, es_ph)
            colsL.append(a); colsL_b.append(b)
        for i in range(NT):
            p = i % 2
            cols, cols_b = colsL[p], colsL_b[p]
            self.dma("sync", xt[p][...], x_src_ap[i * 128:(i + 1) * 128, :],
                     reads=(x_src_buf,), writes=(xt_b[p],))
            c0 = 0
            s_ap = cols[:, c0:c0 + 1]
            self.vred(s_ap, cols_b, xt[p][...], xt_b[p], ALU.add)
            nm_ap = cols[:, c0 + 1:c0 + 2]
            self.ts(nm_ap, cols_b, s_ap, cols_b, -1.0 / D, None, ALU.mult)
            self.ts(xc[p][...], xc_b[p], xt[p][...], xt_b[p], nm_ap, None, ALU.add,
                    s_bufs=(cols_b,))
            v_ap = cols[:, c0 + 2:c0 + 3]
            self.act(xt[p][...], xt_b[p], xc[p][...], xc_b[p], AF.Square,
                     accum=v_ap, accum_buf=cols_b)
            r1 = cols[:, c0 + 3:c0 + 4]
            self.act(r1, cols_b, v_ap, cols_b, AF.Ln, bias=1.0e-5, scale=1.0 / D)
            self.act(r1, cols_b, r1, cols_b, AF.Exp, scale=-0.5)
            w_ap = cols[:, c0 + 4:c0 + 5]
            self.ts(w_ap, cols_b, v_ap, cols_b, r1, None, ALU.mult, s_bufs=(cols_b,))
            self.ts(w_ap, cols_b, w_ap, cols_b, r1, 1.0 / D, ALU.mult, ALU.mult,
                    s_bufs=(cols_b,))
            r2 = cols[:, c0 + 5:c0 + 6]
            self.act(r2, cols_b, w_ap, cols_b, AF.Ln, bias=1.0e-5)
            self.act(r2, cols_b, r2, cols_b, AF.Exp, scale=-0.5)
            r12 = cols[:, c0 + 6:c0 + 7]
            self.ts(r12, cols_b, r2, cols_b, r1, None, ALU.mult, s_bufs=(cols_b,))
            self.ts(xnf[p][...], xnf_b[p], xc[p][...], xc_b[p], r1, None, ALU.mult,
                    s_bufs=(cols_b,))
            self.dma("sync", xn_scr[i * 128:(i + 1) * 128, :], xnf[p][...],
                     reads=(xnf_b[p],), writes=(xn_scr_buf,))
            self.vcopy(xnb[p][...], xnb_b[p], xnf[p][...], xnf_b[p])
            for dc in range(DC):
                pb = dc % 2
                self.transpose(pst[pb][0][...], pst[pb][1],
                               xnb[p][:, dc * 128:(dc + 1) * 128], xnb_b[p],
                               ident, ident_b)
                self.vcopy(xn_dmaj[:, dc, i * 128:(i + 1) * 128], xn_dmaj_buf,
                           pst[pb][0][...], pst[pb][1])
            self.ts(x2f[p][...], x2f_b[p], xc[p][...], xc_b[p], r12, None, ALU.mult,
                    s_bufs=(cols_b,))
            self.tt(x2f[p][...], x2f_b[p], x2f[p][...], x2f_b[p], gam_ap, gam_b, ALU.mult)
            self.tt(x2f[p][...], x2f_b[p], x2f[p][...], x2f_b[p], bet_ap, bet_b, ALU.add)
            self.vcopy(x2b[p][...], x2b_b[p], x2f[p][...], x2f_b[p])
            for dc in range(DC):
                pb = dc % 2
                self.transpose(pst[pb][0][...], pst[pb][1],
                               x2b[p][:, dc * 128:(dc + 1) * 128], x2b_b[p],
                               ident, ident_b)
                self.vcopy(x2st[p][:, dc, :], x2st_b[p], pst[pb][0][...], pst[pb][1])
            self.dma("sync", xn2_scr[:, :, i * 128:(i + 1) * 128]
                     .rearrange("dc p t -> p dc t"), x2st[p][...],
                     reads=(x2st_b[p],), writes=(xn2_scr_buf,))

    def build_layer1(self, W, x_src_ap, x_src_buf, x_dst_ap, x_dst_buf, consts,
                     debug=False):
        P = self.prog
        ident, ident_b = consts["ident"]
        psf = consts["psf"]
        pst = consts["pst"]

        if debug:
            _, self._sdump_buf1 = self.dout("l1_s_dump", [T, D], F32)
        esL = contextlib.ExitStack()
        with esL:
            if debug:
                g_scr, g_scr_b = self.dout("l1_g_dump", [T, D], F32)
            else:
                g_scr, g_scr_b = self.dscratch("l1_g_scr", [T, D], F32)
            xn_scr, xn_scr_b = self.dscratch("l1_xn_scr", [T, D], F32)
            xn2_scr, xn2_scr_b = self.dscratch("l1_xn2_scr", [DC, 128, T], BF16)
            q_scr, q_scr_b = self.dscratch("l1_q_scr", [DC, 128, T], BF16)
            k_scr, k_scr_b = self.dscratch("l1_k_scr", [DC, 128, T], BF16)

            es_xn = contextlib.ExitStack()
            with es_xn:
                xn, xn_b = self.sb("l1_xn", [128, DC, T], BF16, es_xn)
                es_ln = contextlib.ExitStack()
                with es_ln:
                    gam1, gam1_b = self.load_tile(W, "l1_gam_d", [128, 1024], F32, es_ln, "l1gamt")
                    bet1, bet1_b = self.load_tile(W, "l1_bet_d", [128, 1024], F32, es_ln, "l1bett")
                    self.ln_pass_dual(x_src_ap, x_src_buf, xn, xn_b,
                                      xn_scr, xn_scr_b, xn2_scr, xn2_scr_b,
                                      gam1[...], gam1_b, bet1[...], bet1_b,
                                      ident, ident_b, pst, es_ln, "l1ln")
                self.close_scope(es_ln)

                esF = contextlib.ExitStack()
                if True:
                    y_sb, y_sb_b = self.sb("l1_yq", [128, 8, 1024], F32, esF)
                    w1u, w1u_b, w2u, w2u_b, h_sb, h_sb_b = [], [], [], [], [], []
                    for k in range(2):
                        a, b = self.sb(f"l1_w1u{k}", [128, 8, 512], BF16, esF)
                        w1u.append(a); w1u_b.append(b)
                        a, b = self.sb(f"l1_w2u{k}", [128, 4, 1024], BF16, esF)
                        w2u.append(a); w2u_b.append(b)
                        a, b = self.sb(f"l1_h{k}", [128, 4, 512], BF16, esF)
                        h_sb.append(a); h_sb_b.append(b)
                    hrel, hrel_b = self.sb("l1_hrel", [128, 512], F32, esF)
                    mem_sb, mem_sb_b = self.sb("l1_mem", [128, 2, 1024], F32, esF)
                    spk_sb, spk_sb_b = self.sb("l1_spk", [128, 2, 1024], F32, esF)
                    sum_sb, sum_sb_b = self.sb("l1_ssum", [128, 2, 1024], F32, esF)
                    gsil, gsil_b = self.sb("l1_gsil", [128, 2, 1024], F32, esF)
                    WfcT, WfcT_b = W["l1_WfcT_dram"]
                    WprT, WprT_b = W["l1_WprT_dram"]
                    NFB = 8
                    for quarter in range(4):
                        tq0 = quarter * 8
                        for fb in range(NFB):
                            pw = fb % 2
                            self.dma("gpsimd", w1u[pw][...],
                                     WfcT[:, fb * 512:(fb + 1) * 512]
                                     .rearrange("(dc p) f -> p dc f", p=128),
                                     reads=(WfcT_b,), writes=(w1u_b[pw],))
                            self.dma("gpsimd", w2u[pw][...],
                                     WprT[fb * 512:(fb + 1) * 512, :]
                                     .rearrange("(fc p) d -> p fc d", p=128),
                                     reads=(WprT_b,), writes=(w2u_b[pw],))
                            for tc in range(2):
                                t0c = tq0 * 128 + tc * 512
                                ph = tc % 2
                                for ft in range(4):
                                    pb = psf[ft % 4]
                                    terms = [(w1u[pw][:, dc, ft * 128:(ft + 1) * 128],
                                              xn[:, dc, t0c:t0c + 512])
                                             for dc in range(DC)]
                                    self.mm_group(pb[0][...], pb[1], terms,
                                                  reads=(w1u_b[pw], xn_b))
                                    self.ts(hrel[...], hrel_b, pb[0][...], pb[1],
                                            0.0, None, ALU.max)
                                    self.act(h_sb[ph][:, ft, :], h_sb_b[ph],
                                             hrel[...], hrel_b, AF.Square)
                            for tc in range(2):
                                t0c = tq0 * 128 + tc * 512
                                ph = tc % 2
                                for ti in range(4):
                                    itt = (t0c // 128) + ti
                                    yq_ti = itt - tq0
                                    for dh in range(2):
                                        pb = psf[4 + ((ti * 2 + dh) % 2)]
                                        terms = [(h_sb[ph][:, fc, ti * 128:(ti + 1) * 128],
                                                  w2u[pw][:, fc, dh * 512:(dh + 1) * 512])
                                                 for fc in range(4)]
                                        self.mm_group(pb[0][...], pb[1], terms,
                                                      reads=(h_sb_b[ph], w2u_b[pw]))
                                        dst = y_sb[:, yq_ti, dh * 512:(dh + 1) * 512]
                                        if fb == 0:
                                            self.act(dst, y_sb_b, pb[0][...], pb[1],
                                                     AF.Identity)
                                        else:
                                            self.tt(dst, y_sb_b, dst, y_sb_b,
                                                    pb[0][...], pb[1], ALU.add)
                        for ti in range(8):
                            it = tq0 + ti
                            y_ap = y_sb[:, ti, :]
                            mem = mem_sb[:, ti % 2, :]
                            spk = spk_sb[:, ti % 2, :]
                            ssum = sum_sb[:, ti % 2, :]
                            self.ts(spk, spk_sb_b, y_ap, y_sb_b, 1.0, None, ALU.is_gt)
                            self.vcopy(mem, mem_sb_b, y_ap, y_sb_b)
                            self.vcopy(ssum, sum_sb_b, spk, spk_sb_b)
                            for step in range(1, 5):
                                self.ts(mem, mem_sb_b, mem, mem_sb_b, 0.9, None, ALU.mult)
                                self.tt(mem, mem_sb_b, mem, mem_sb_b, y_ap, y_sb_b, ALU.add)
                                self.tt(mem, mem_sb_b, mem, mem_sb_b, spk, spk_sb_b,
                                        ALU.subtract)
                                self.ts(spk, spk_sb_b, mem, mem_sb_b, 1.0, None, ALU.is_gt)
                                self.tt(ssum, sum_sb_b, ssum, sum_sb_b, spk, spk_sb_b,
                                        ALU.add)
                            self.act(gsil[:, ti % 2, :], gsil_b, ssum, sum_sb_b, AF.Silu,
                                     scale=0.2)
                            self.dma("sync", g_scr[it * 128:(it + 1) * 128, :],
                                     gsil[:, ti % 2, :], reads=(gsil_b,), writes=(g_scr_b,))
                self.close_scope(esF)
            self.close_scope(es_xn)

            esQ = contextlib.ExitStack()
            if True:
                v_tm, v_tm_b = self.sb("l1_v", [128, NT, D], BF16, esL, side="left")
                wq_sb, wq_sb_b = self.sb("l1_wq", [128, DC, D], BF16, esQ)
                wk_sb, wk_sb_b = self.sb("l1_wk", [128, DC, D], BF16, esQ)
                wv_sb, wv_sb_b = self.sb("l1_wv", [128, DC, D], BF16, esQ)
                self.dma("gpsimd", wq_sb[...],
                         W["l1_WqT_dram"][0].rearrange("(dc p) e -> p dc e", p=128),
                         reads=(W["l1_WqT_dram"][1],), writes=(wq_sb_b,))
                self.dma("gpsimd", wk_sb[...],
                         W["l1_WkT_dram"][0].rearrange("(dc p) e -> p dc e", p=128),
                         reads=(W["l1_WkT_dram"][1],), writes=(wk_sb_b,))
                self.dma("gpsimd", wv_sb[...],
                         W["l1_WvT_dram"][0].rearrange("(dc p) e -> p dc e", p=128),
                         reads=(W["l1_WvT_dram"][1],), writes=(wv_sb_b,))
                x2c, x2c_b = [], []
                for k in range(2):
                    a, b = self.sb(f"l1_x2c{k}", [128, DC, 512], BF16, esQ)
                    x2c.append(a); x2c_b.append(b)
                qkv_bf, qkv_bf_b = self.sb("l1_qkvbf", [128, 4, 512], BF16, esQ)
                for tc in range(T // 512):
                    t0c = tc * 512
                    p = tc % 2
                    self.dma("sync", x2c[p][...],
                             xn2_scr[:, :, t0c:t0c + 512].rearrange("dc p t -> p dc t"),
                             reads=(xn2_scr_b,), writes=(x2c_b[p],))
                    for which, wsb, wsb_b, dscr, dscr_b in (
                            (0, wq_sb, wq_sb_b, q_scr, q_scr_b),
                            (1, wk_sb, wk_sb_b, k_scr, k_scr_b)):
                        for et in range(DC):
                            pb = psf[(et + which) % 3]
                            terms = [(wsb[:, dci, et * 128:(et + 1) * 128],
                                      x2c[p][:, dci, :]) for dci in range(DC)]
                            self.mm_group(pb[0][...], pb[1], terms,
                                          reads=(wsb_b, x2c_b[p]))
                            bf = qkv_bf[:, which, :]
                            self.act(bf, qkv_bf_b, pb[0][...], pb[1], AF.Identity)
                            self.dma("sync", dscr[et, :, t0c:t0c + 512], bf,
                                     reads=(qkv_bf_b,), writes=(dscr_b,))
                    for ti in range(4):
                        itt = tc * 4 + ti
                        for eh in range(2):
                            pb = psf[3 + eh]
                            terms = [(x2c[p][:, dci, ti * 128:(ti + 1) * 128],
                                      wv_sb[:, dci, eh * 512:(eh + 1) * 512])
                                     for dci in range(DC)]
                            self.mm_group(pb[0][...], pb[1], terms,
                                          reads=(x2c_b[p], wv_sb_b))
                            self.act(v_tm[:, itt, eh * 512:(eh + 1) * 512], v_tm_b,
                                     pb[0][...], pb[1], AF.Identity)
            self.close_scope(esQ)

            esC = contextlib.ExitStack()
            if True:
                kd, kd_b = self.sb("l1_k", [128, DC, T], BF16, esC)
                self.dma("sync", kd[...], k_scr.rearrange("dc p t -> p dc t"),
                         reads=(k_scr_b,), writes=(kd_b,))
                wo_sb, wo_sb_b = self.sb("l1_wo", [128, DC, D], BF16, esC)
                self.dma("gpsimd", wo_sb[...],
                         W["l1_WoT_dram"][0].rearrange("(dc p) e -> p dc e", p=128),
                         reads=(W["l1_WoT_dram"][1],), writes=(wo_sb_b,))
                B0_d, B0_d_b = W["l1_B0_dram"]
                B1_d, B1_d_b = W["l1_B1_dram"]
                far_d, far_d_b = W["l1_far_dram"]
                msk_d, msk_d_b = W["l1_maskdiag_dram"]
                B0p, B0p_b = self.sb("l1_B0p", [128, 16, 128], BF16, esC)
                B1p, B1p_b = self.sb("l1_B1p", [128, 16, 128], BF16, esC)
                farsb, farsb_b = self.sb("l1_farsb", [128, 16], F32, esC)
                msksb, msksb_b = self.sb("l1_mskd", [128, 128], F32, esC)
                self.dma("sync", farsb[...], far_d[...], reads=(far_d_b,),
                         writes=(farsb_b,))
                self.dma("sync", msksb[...], msk_d[...], reads=(msk_d_b,),
                         writes=(msksb_b,))
                es_bt = contextlib.ExitStack()
                if True:
                    btmp, btmp_b = self.sb("l1_btmp", [128, 16, 128], F32, es_bt)
                    self.dma("sync", btmp[...], B0_d.rearrange("h p f -> p h f"),
                             reads=(B0_d_b,), writes=(btmp_b,))
                    for h in range(16):
                        self.ts(btmp[:, h, :], btmp_b, btmp[:, h, :], btmp_b,
                                farsb[:, h:h + 1], 8.0, ALU.subtract, ALU.mult,
                                s_bufs=(farsb_b,))
                        self.tt(B0p[:, h, :], B0p_b, btmp[:, h, :], btmp_b,
                                msksb[...], msksb_b, ALU.add)
                    self.dma("sync", btmp[...], B1_d.rearrange("h p f -> p h f"),
                             reads=(B1_d_b,), writes=(btmp_b,))
                    for h in range(16):
                        self.ts(B1p[:, h, :], B1p_b, btmp[:, h, :], btmp_b,
                                farsb[:, h:h + 1], 8.0, ALU.subtract, ALU.mult,
                                s_bufs=(farsb_b,))
                self.close_scope(es_bt)

                qt, qt_b = [], []
                for k in range(2):
                    a, b = self.sb(f"l1_qt{k}", [128, DC, 128], BF16, esC)
                    qt.append(a); qt_b.append(b)
                PsbL, PsbL_b = [], []
                for k in range(2):
                    a, b = self.sb(f"l1_P{k}", [128, T], BF16, esC)
                    PsbL.append(a); PsbL_b.append(b)
                PtL, PtL_b = [], []
                for k in range(2):
                    a, b = self.sb(f"l1_Pt{k}", [128, 8, 128], BF16, esC)
                    PtL.append(a); PtL_b.append(b)
                ysb, ysb_b = self.sb("l1_ysb", [128, DC, 128], BF16, esC)
                xg, xg_b = [], []
                for k in range(4):
                    a, b = self.sb(f"l1_xg{k}", [128, D], F32, esC)
                    xg.append(a); xg_b.append(b)
                so, so_b = [], []
                for k in range(2):
                    a, b = self.sb(f"l1_so{k}", [128, D], F32, esC)
                    so.append(a); so_b.append(b)
                colsL, colsL_b = [], []
                for k in range(2):
                    a, b = self.sb(f"l1_acols{k}", [128, 32], F32, esC)
                    colsL.append(a); colsL_b.append(b)

                for it in range(NT):
                    p = it % 2
                    self.dma("sync", qt[p][...],
                             q_scr[:, :, it * 128:(it + 1) * 128]
                             .rearrange("dc p t -> p dc t"),
                             reads=(q_scr_b,), writes=(qt_b[p],))
                    self.dma("sync", xg[p][...], x_src_ap[it * 128:(it + 1) * 128, :],
                             reads=(x_src_buf,), writes=(xg_b[p],))
                    self.dma("sync", xg[2 + p][...], xn_scr[it * 128:(it + 1) * 128, :],
                             reads=(xn_scr_b,), writes=(xg_b[2 + p],))
                    jext = (it + 1) * 128
                    n_ch = ceil_div(jext, 512)

                    def softmax_part(h, p=p, it=it, jext=jext, n_ch=n_ch):
                        hp = h % 2
                        cols, cols_b = colsL[hp], colsL_b[hp]
                        Psb, Psb_b = PsbL[hp], PsbL_b[hp]
                        m_run = cols[:, 0:1]
                        bias_c = cols[:, 3:4]
                        lfin = cols[:, 4:5]
                        rec = cols[:, 5:6]
                        tmp = cols[:, 6:7]
                        corr = cols[:, 7:8]
                        dc_h = h // 2
                        pb_h = (h % 2) * 64
                        n_grp = ceil_div(n_ch, 4)
                        for g0 in range(n_grp):
                            ch0, ch1 = g0 * 4, min(n_ch, g0 * 4 + 4)
                            for ch in range(ch0, ch1):
                                c0 = ch * 512
                                cw = min(512, jext - c0)
                                pb = psf[ch - ch0]
                                q_ap = qt[p][pb_h:pb_h + 64, dc_h, :]
                                self.mm_group(pb[0][:, 0:cw], pb[1],
                                              [(q_ap, kd[pb_h:pb_h + 64, dc_h, c0:c0 + cw])],
                                              reads=(qt_b[p], kd_b))
                                jt_lo = c0 // 128
                                if jt_lo <= it < jt_lo + ceil_div(cw, 128):
                                    off = (it - jt_lo) * 128
                                    self.tt(pb[0][:, off:off + 128], pb[1],
                                            pb[0][:, off:off + 128], pb[1],
                                            B0p[:, h, :], B0p_b, ALU.add)
                                if it >= 1 and jt_lo <= it - 1 < jt_lo + ceil_div(cw, 128):
                                    off = (it - 1 - jt_lo) * 128
                                    self.tt(pb[0][:, off:off + 128], pb[1],
                                            pb[0][:, off:off + 128], pb[1],
                                            B1p[:, h, :], B1p_b, ALU.add)
                                self.vred(cols[:, 8 + ch - ch0:9 + ch - ch0], cols_b,
                                          pb[0][:, 0:cw], pb[1], ALU.max)
                            ng = ch1 - ch0
                            if ng == 1:
                                m_g = cols[:, 8:9]
                            else:
                                m_g = cols[:, 2:3]
                                self.vred(m_g, cols_b, cols[:, 8:8 + ng], cols_b, ALU.max)
                            if g0 == 0:
                                self.vcopy(m_run, cols_b, m_g, cols_b)
                            else:
                                self.tt(tmp, cols_b, m_run, cols_b, m_g, cols_b, ALU.max)
                                self.tt(corr, cols_b, m_run, cols_b, tmp, cols_b,
                                        ALU.subtract)
                                self.act(corr, cols_b, corr, cols_b, AF.Exp, scale=0.125)
                                self.ts(Psb[:, 0:ch0 * 512], Psb_b, Psb[:, 0:ch0 * 512],
                                        Psb_b, corr, None, ALU.mult, s_bufs=(cols_b,))
                                self.ts(lfin, cols_b, lfin, cols_b, corr, None, ALU.mult,
                                        s_bufs=(cols_b,))
                                self.vcopy(m_run, cols_b, tmp, cols_b)
                            self.ts(bias_c, cols_b, m_run, cols_b, -0.125, None, ALU.mult)
                            for ch in range(ch0, ch1):
                                c0 = ch * 512
                                cw = min(512, jext - c0)
                                pb = psf[ch - ch0]
                                self.act(Psb[:, c0:c0 + cw], Psb_b, pb[0][:, 0:cw], pb[1],
                                         AF.Exp, bias=bias_c, scale=0.125,
                                         accum=cols[:, 16 + ch - ch0:17 + ch - ch0],
                                         accum_buf=cols_b, extra_reads=(cols_b,))
                            lg = cols[:, 1:2]
                            if ng == 1:
                                self.vcopy(lg, cols_b, cols[:, 16:17], cols_b)
                            else:
                                self.vred(lg, cols_b, cols[:, 16:16 + ng], cols_b, ALU.add)
                            if g0 == 0:
                                self.vcopy(lfin, cols_b, lg, cols_b)
                            else:
                                self.tt(lfin, cols_b, lfin, cols_b, lg, cols_b, ALU.add)
                        self.recip(rec, cols_b, lfin, cols_b)
                        self.ts(Psb[:, 0:jext], Psb_b, Psb[:, 0:jext], Psb_b, rec, None,
                                ALU.mult, s_bufs=(cols_b,))

                    def pv_part(h, jext=jext):
                        hp = h % 2
                        Psb, Psb_b = PsbL[hp], PsbL_b[hp]
                        PtH, PtH_b = PtL[hp], PtL_b[hp]
                        dc_h = h // 2
                        pb_h = (h % 2) * 64
                        n_jt = jext // 128
                        pby = psf[4 + hp]
                        for jg in range(0, n_jt, 8):
                            je = min(n_jt, jg + 8)
                            for jj in range(jg, je):
                                self.transpose(pst[jj % 2][0][...], pst[jj % 2][1],
                                               Psb[:, jj * 128:(jj + 1) * 128], Psb_b,
                                               ident, ident_b)
                                self.vcopy(PtH[:, jj - jg, :], PtH_b,
                                           pst[jj % 2][0][...], pst[jj % 2][1])
                            for jj in range(jg, je):
                                fn = (lambda E, lt=v_tm[:, jj, h * 64:(h + 1) * 64],
                                      rh=PtH[:, jj - jg, :], o=pby[0][0:64, 0:128],
                                      st=(jj == 0), sp=(jj == n_jt - 1):
                                      E.matmul(o, lt, rh, start=st, stop=sp))
                                self.prog.emit("tensor", fn, reads=(v_tm_b, PtH_b),
                                               writes=(pby[1],), sync=(jj == n_jt - 1))
                        self.act(ysb[pb_h:pb_h + 64, dc_h, :], ysb_b,
                                 pby[0][0:64, 0:128], pby[1], AF.Identity)

                    for h in range(16):
                        softmax_part(h)
                        if h >= 1:
                            pv_part(h - 1)
                    pv_part(15)
                    for dh in range(2):
                        pb = psf[5]
                        terms = [(ysb[:, dci, :], wo_sb[:, dci, dh * 512:(dh + 1) * 512])
                                 for dci in range(DC)]
                        self.mm_group(pb[0][...], pb[1], terms, reads=(ysb_b, wo_sb_b))
                        self.tt(so[p][:, dh * 512:(dh + 1) * 512], so_b[p],
                                pb[0][...], pb[1], xg[2 + p][:, dh * 512:(dh + 1) * 512],
                                xg_b[2 + p], ALU.add)
                    if debug:
                        self.dma("sync", self.dram["l1_s_dump"][it * 128:(it + 1) * 128, :],
                                 so[p][...], reads=(so_b[p],),
                                 writes=(self._sdump_buf1,))
                    self.dma("sync", xg[2 + p][...], g_scr[it * 128:(it + 1) * 128, :],
                             reads=(g_scr_b,), writes=(xg_b[2 + p],))
                    self.tt(so[p][...], so_b[p], so[p][...], so_b[p], xg[2 + p][...],
                            xg_b[2 + p], ALU.mult)
                    self.tt(so[p][...], so_b[p], so[p][...], so_b[p], xg[p][...],
                            xg_b[p], ALU.add)
                    self.dma("sync", x_dst_ap[it * 128:(it + 1) * 128, :], so[p][...],
                             reads=(so_b[p],), writes=(x_dst_buf,))
            self.close_scope(esC)




    def build_layer2(self, W, x_src_ap, x_src_buf, x_dst_ap, x_dst_buf, consts,
                     debug=False):
        P = self.prog
        ident, ident_b = consts["ident"]
        psf = consts["psf"]
        pst = consts["pst"]
        EPS = 1.0e-6

        if debug:
            _, self._sdump_buf2 = self.dout("l2_s_dump", [T, D], F32)
        esL = contextlib.ExitStack()
        with esL:
            if debug:
                g_scr, g_scr_b = self.dout("l2_g_dump", [T, D], F32)
            else:
                g_scr, g_scr_b = self.dscratch("l2_g_scr", [T, D], F32)

            es_xn = contextlib.ExitStack()
            with es_xn:
                xn, xn_b = self.sb("l2_xn", [128, DC, T], BF16, es_xn)
                es_ln = contextlib.ExitStack()
                with es_ln:
                    self.ln_pass(x_src_ap, x_src_buf, xn, xn_b, ident, ident_b,
                                 [p[0] for p in pst], [p[1] for p in pst], es_ln, "l2ln")
                self.close_scope(es_ln)

                esG = contextlib.ExitStack()
                gates, gates_b = self.sb("l2_gates", [128, NT, 8], F32, esL, side="left")
                gates_dm, gates_dm_b = self.sb("l2_gates_dm", [8, T], BF16, esL, side="left")
                if True:
                    rw, rw_b = W["l2_Wr"]
                    rbt, rbt_b = W["l2_br_t"]
                    gt_tmp, gt_tmp_b = self.sb("l2_gt", [128, 4, 8], F32, esG)
                    gt_bf, gt_bf_b = self.sb("l2_gtbf", [128, 8], BF16, esG)
                    gcol, gcol_b = self.sb("l2_gcol", [128, 8], F32, esG)
                    for i in range(NT):
                        pb = psf[i % 2]
                        terms = [(xn[:, dc, i * 128:(i + 1) * 128], rw[:, dc, :])
                                 for dc in range(DC)]
                        self.mm_group(pb[0][:, 0:8], pb[1], terms, reads=(xn_b, rw_b))
                        t0 = gt_tmp[:, (i % 2) * 2, :]
                        t1 = gt_tmp[:, (i % 2) * 2 + 1, :]
                        self.tt(t0, gt_tmp_b, pb[0][:, 0:8], pb[1], rbt[...], rbt_b,
                                ALU.add)
                        c0 = (i % 2) * 4
                        m_ap = gcol[:, c0:c0 + 1]
                        self.vred(m_ap, gcol_b, t0, gt_tmp_b, ALU.max)
                        nm = gcol[:, c0 + 1:c0 + 2]
                        self.ts(nm, gcol_b, m_ap, gcol_b, -1.0, None, ALU.mult)
                        l_ap = gcol[:, c0 + 2:c0 + 3]
                        self.act(t1, gt_tmp_b, t0, gt_tmp_b, AF.Exp, bias=nm, scale=1.0,
                                 accum=l_ap, accum_buf=gcol_b, extra_reads=(gcol_b,))
                        r_ap = gcol[:, c0 + 3:c0 + 4]
                        self.recip(r_ap, gcol_b, l_ap, gcol_b)
                        self.ts(t1, gt_tmp_b, t1, gt_tmp_b, r_ap, None, ALU.mult,
                                s_bufs=(gcol_b,))
                        self.vred(m_ap, gcol_b, t1, gt_tmp_b, ALU.max)
                        self.ts(t0, gt_tmp_b, t1, gt_tmp_b, m_ap, None, ALU.is_equal,
                                s_bufs=(gcol_b,))
                        self.stt(t0, gt_tmp_b, t0, gt_tmp_b, NEG, None, t1,
                                 gt_tmp_b, ALU.mult, ALU.add)
                        self.vred(nm, gcol_b, t0, gt_tmp_b, ALU.max)
                        self.ts(gates[:, i, :], gates_b, t1, gt_tmp_b, nm, None,
                                ALU.is_ge, s_bufs=(gcol_b,))
                        self.tt(gates[:, i, :], gates_b, gates[:, i, :], gates_b, t1,
                                gt_tmp_b, ALU.mult)
                        self.vcopy(gt_bf[...], gt_bf_b, gates[:, i, :], gates_b)
                        self.transpose(pst[0][0][0:8, :], pst[0][1], gt_bf[...],
                                       gt_bf_b, ident, ident_b)
                        self.vcopy(gates_dm[:, i * 128:(i + 1) * 128], gates_dm_b,
                                   pst[0][0][0:8, :], pst[0][1])
                self.close_scope(esG)

                esF = contextlib.ExitStack()
                if True:
                    b2sb, b2sb_b = W["l2_b2"]
                    b1sb, b1sb_b = W["l2_b1"]
                    g_sb, g_sb_b = self.sb("l2_gq", [128, 8, 1024], F32, esF)
                    w1u, w1u_b, w2u, w2u_b, h_sb, h_sb_b = [], [], [], [], [], []
                    for k in range(2):
                        a, b = self.sb(f"l2_w1u{k}", [128, 8, 512], BF16, esF)
                        w1u.append(a); w1u_b.append(b)
                        a, b = self.sb(f"l2_w2u{k}", [128, 4, 1024], BF16, esF)
                        w2u.append(a); w2u_b.append(b)
                        a, b = self.sb(f"l2_h{k}", [128, 4, 512], BF16, esF)
                        h_sb.append(a); h_sb_b.append(b)
                    gsil, gsil_b = self.sb("l2_gsil", [128, 2, 1024], F32, esF)
                    W1d, W1d_b = W["l2_W1_dram"]
                    W2d, W2d_b = W["l2_W2_dram"]
                    for quarter in range(4):
                        tq0 = quarter * 8
                        for ti in range(8):
                            it = tq0 + ti
                            pb = psf[4 + (ti % 2)]
                            for dh in range(2):
                                self.mm_group(
                                    pb[0][...], pb[1],
                                    [(gates_dm[:, it * 128:(it + 1) * 128],
                                      b2sb[:, dh * 512:(dh + 1) * 512])],
                                    reads=(gates_dm_b, b2sb_b))
                                self.act(g_sb[:, ti, dh * 512:(dh + 1) * 512], g_sb_b,
                                         pb[0][...], pb[1], AF.Identity)
                        for e in range(8):
                            for fb in range(4):
                                u = e * 4 + fb
                                pw = u % 2
                                self.dma("gpsimd", w1u[pw][...],
                                         W1d[e, :, fb * 512:(fb + 1) * 512]
                                         .rearrange("(dc p) f -> p dc f", p=128),
                                         reads=(W1d_b,), writes=(w1u_b[pw],))
                                self.dma("gpsimd", w2u[pw][...],
                                         W2d[e, fb * 512:(fb + 1) * 512, :]
                                         .rearrange("(fc p) d -> p fc d", p=128),
                                         reads=(W2d_b,), writes=(w2u_b[pw],))
                                for tc in range(2):
                                    t0c = tq0 * 128 + tc * 512
                                    ph = tc % 2
                                    for ft in range(4):
                                        pb = psf[ft % 4]
                                        terms = [(w1u[pw][:, dc, ft * 128:(ft + 1) * 128],
                                                  xn[:, dc, t0c:t0c + 512])
                                                 for dc in range(DC)]
                                        self.mm_group(pb[0][...], pb[1], terms,
                                                      reads=(w1u_b[pw], xn_b))
                                        self.act(h_sb[ph][:, ft, :], h_sb_b[ph],
                                                 pb[0][...], pb[1], AF.Relu,
                                                 bias=b1sb[:, e, fb * 4 + ft:fb * 4 + ft + 1],
                                                 extra_reads=(b1sb_b,))
                                for tc in range(2):
                                    t0c = tq0 * 128 + tc * 512
                                    ph = tc % 2
                                    for ti in range(4):
                                        itt = (t0c // 128) + ti
                                        gq_ti = itt - tq0
                                        for dh in range(2):
                                            pb = psf[4 + ((ti * 2 + dh) % 2)]
                                            terms = [(h_sb[ph][:, fc, ti * 128:(ti + 1) * 128],
                                                      w2u[pw][:, fc, dh * 512:(dh + 1) * 512])
                                                     for fc in range(4)]
                                            self.mm_group(pb[0][...], pb[1], terms,
                                                          reads=(h_sb_b[ph], w2u_b[pw]))
                                            self.stt(g_sb[:, gq_ti, dh * 512:(dh + 1) * 512],
                                                     g_sb_b, pb[0][...], pb[1],
                                                     gates[:, itt, e:e + 1], gates_b,
                                                     g_sb[:, gq_ti, dh * 512:(dh + 1) * 512],
                                                     g_sb_b, ALU.mult, ALU.add)
                        for ti in range(8):
                            it = tq0 + ti
                            self.act(gsil[:, ti % 2, :], gsil_b, g_sb[:, ti, :], g_sb_b,
                                     AF.Silu)
                            self.dma("sync", g_scr[it * 128:(it + 1) * 128, :],
                                     gsil[:, ti % 2, :], reads=(gsil_b,),
                                     writes=(g_scr_b,))
                self.close_scope(esF)

                esA = contextlib.ExitStack()
                if True:
                    wk_sb, wk_sb_b = self.sb("l2_wk", [128, DC, D], BF16, esA)
                    wv_sb, wv_sb_b = self.sb("l2_wv", [128, DC, D], BF16, esA)
                    proj2, proj2_b = W["l2_proj2"]
                    onec, onec_b = W["l2_ones"]
                    self.dma("gpsimd", wk_sb[...],
                             W["l2_Wk_dram"][0].rearrange("(dc p) e -> p dc e", p=128),
                             reads=(W["l2_Wk_dram"][1],), writes=(wk_sb_b,))
                    self.dma("gpsimd", wv_sb[...],
                             W["l2_Wv_dram"][0].rearrange("(dc p) e -> p dc e", p=128),
                             reads=(W["l2_Wv_dram"][1],), writes=(wv_sb_b,))
                    kch, kch_b = self.sb("l2_kch", [128, DC, 512], BF16, esA)
                    vch, vch_b = self.sb("l2_vch", [128, 4, 1024], BF16, esA)
                    kfch, kfch_b = self.sb("l2_kfch", [128, 4, 1024], BF16, esA)
                    kv_ps = psf[4]
                    ks_ps = psf[5]
                    for tc in range(T // 512):
                        t0c = tc * 512
                        for et in range(DC):
                            pb = psf[et % 2]
                            terms = [(wk_sb[:, dci, et * 128:(et + 1) * 128],
                                      xn[:, dci, t0c:t0c + 512]) for dci in range(DC)]
                            self.mm_group(pb[0][...], pb[1], terms,
                                          reads=(wk_sb_b, xn_b))
                            self.act(kch[:, et, :], kch_b, pb[0][...], pb[1], AF.Identity)
                        for ti in range(4):
                            for eh in range(2):
                                pb = psf[2 + eh]
                                terms = [(xn[:, dci, t0c + ti * 128:t0c + (ti + 1) * 128],
                                          wv_sb[:, dci, eh * 512:(eh + 1) * 512])
                                         for dci in range(DC)]
                                self.mm_group(pb[0][...], pb[1], terms,
                                              reads=(xn_b, wv_sb_b))
                                self.act(vch[:, ti, eh * 512:(eh + 1) * 512], vch_b,
                                         pb[0][...], pb[1], AF.Identity)
                        for h in range(16):
                            b0 = (h % 2) * 64
                            dch = h // 2
                            for ti in range(4):
                                pb = psf[ti % 2]
                                self.mm_group(
                                    pb[0][0:128, 0:64], pb[1],
                                    [(kch[b0:b0 + 64, dch, ti * 128:(ti + 1) * 128],
                                      proj2[b0:b0 + 64, :])],
                                    reads=(kch_b, proj2_b))
                                self.ts(kfch[:, ti, h * 64:(h + 1) * 64], kfch_b,
                                        pb[0][0:128, 0:64], pb[1], EPS, EPS,
                                        ALU.add, ALU.max)
                        for h in range(16):
                            b0 = (h % 2) * 64
                            c0 = (h // 2) * 64
                            for ti in range(4):
                                first = (tc == 0 and ti == 0)
                                fn = (lambda E, lt=kfch[:, ti, h * 64:(h + 1) * 64],
                                      rh=vch[:, ti, h * 64:(h + 1) * 64],
                                      o=kv_ps[0][b0:b0 + 64, c0:c0 + 64], st=first:
                                      E.matmul(o, lt, rh, start=st, stop=True))
                                self.prog.emit("tensor", fn, reads=(kfch_b, vch_b),
                                               writes=(kv_ps[1],), sync=True)
                                fn2 = (lambda E, lt=kfch[:, ti, h * 64:(h + 1) * 64],
                                       rh=onec[...],
                                       o=ks_ps[0][b0:b0 + 64, h:h + 1], st=first:
                                       E.matmul(o, lt, rh, start=st, stop=True))
                                self.prog.emit("tensor", fn2, reads=(kfch_b, onec_b),
                                               writes=(ks_ps[1],), sync=True)
                    kv_sb, kv_sb_b = self.sb("l2_kv", [128, 512], BF16, esA)
                    ks_sb, ks_sb_b = self.sb("l2_ks", [128, 16], BF16, esA)
                    self.act(kv_sb[...], kv_sb_b, kv_ps[0][...], kv_ps[1], AF.Identity)
                    self.act(ks_sb[...], ks_sb_b, ks_ps[0][:, 0:16], ks_ps[1], AF.Identity)

                    wq_sb, wq_sb_b = wk_sb, wk_sb_b
                    self.dma("gpsimd", wq_sb[...],
                             W["l2_Wq_dram"][0].rearrange("(dc p) e -> p dc e", p=128),
                             reads=(W["l2_Wq_dram"][1],), writes=(wq_sb_b,))
                    wo_sb, wo_sb_b = wv_sb, wv_sb_b
                    self.dma("gpsimd", wo_sb[...],
                             W["l2_Wo_dram"][0].rearrange("(ec p) o -> p ec o", p=128),
                             reads=(W["l2_Wo_dram"][1],), writes=(wo_sb_b,))
                    bot_ap, bot_b = self.load_tile(W, "l2_bo_d", [128, 1024], F32, esA, "l2bott")
                    bot = bot_ap[...]
                    qch, qch_b = kch, kch_b
                    qfd, qfd_b = self.sb("l2_qfd", [128, 2, 512], BF16, esA)
                    otm, otm_b = vch, vch_b
                    odm, odm_b = self.sb("l2_odm", [128, DC, 128], BF16, esA)
                    dcol, dcol_b = self.sb("l2_dcol", [128, 8], F32, esA)
                    xg, xg_b = [], []
                    for k in range(4):
                        a, b = self.sb(f"l2_xg{k}", [128, D], F32, esA)
                        xg.append(a); xg_b.append(b)
                    so, so_b = [], []
                    for k in range(2):
                        a, b = self.sb(f"l2_so{k}", [128, D], F32, esA)
                        so.append(a); so_b.append(b)
                    for tc in range(T // 512):
                        t0c = tc * 512
                        for et in range(DC):
                            pb = psf[et % 2]
                            terms = [(wq_sb[:, dci, et * 128:(et + 1) * 128],
                                      xn[:, dci, t0c:t0c + 512]) for dci in range(DC)]
                            self.mm_group(pb[0][...], pb[1], terms,
                                          reads=(wq_sb_b, xn_b))
                            self.act(qch[:, et, :], qch_b, pb[0][...], pb[1], AF.Identity)
                        for h in range(16):
                            b0 = (h % 2) * 64
                            dch = h // 2
                            pb = psf[2 + h % 2]
                            self.mm_group(
                                pb[0][b0:b0 + 64, 0:512], pb[1],
                                [(proj2[b0:b0 + 64, :], qch[b0:b0 + 64, dch, :])],
                                reads=(proj2_b, qch_b))
                            self.ts(qfd[:, h % 2, :][b0:b0 + 64, :], qfd_b,
                                    pb[0][b0:b0 + 64, 0:512], pb[1], EPS, EPS,
                                    ALU.add, ALU.max)
                            for ti in range(4):
                                pbn = psf[4]
                                self.mm_group(
                                    pbn[0][0:128, 0:64], pbn[1],
                                    [(qfd[:, h % 2, ti * 128:(ti + 1) * 128][b0:b0 + 64, :],
                                      kv_sb[b0:b0 + 64, (h // 2) * 64:(h // 2) * 64 + 64])],
                                    reads=(qfd_b, kv_sb_b))
                                pbd = psf[5]
                                self.mm_group(
                                    pbd[0][0:128, 0:1], pbd[1],
                                    [(qfd[:, h % 2, ti * 128:(ti + 1) * 128][b0:b0 + 64, :],
                                      ks_sb[b0:b0 + 64, h:h + 1])],
                                    reads=(qfd_b, ks_sb_b))
                                dc_ap = dcol[:, ti * 2:ti * 2 + 1]
                                self.ts(dc_ap, dcol_b, pbd[0][0:128, 0:1], pbd[1],
                                        EPS, None, ALU.add)
                                rc_ap = dcol[:, ti * 2 + 1:ti * 2 + 2]
                                self.recip(rc_ap, dcol_b, dc_ap, dcol_b)
                                self.ts(otm[:, ti, h * 64:(h + 1) * 64], otm_b,
                                        pbn[0][0:128, 0:64], pbn[1], rc_ap, None,
                                        ALU.mult, s_bufs=(dcol_b,))
                        for ti in range(4):
                            it = tc * 4 + ti
                            p = it % 2
                            self.dma("sync", xg[p][...],
                                     x_src_ap[it * 128:(it + 1) * 128, :],
                                     reads=(x_src_buf,), writes=(xg_b[p],))
                            self.dma("sync", xg[2 + p][...],
                                     g_scr[it * 128:(it + 1) * 128, :],
                                     reads=(g_scr_b,), writes=(xg_b[2 + p],))
                            for dc in range(DC):
                                pb2 = pst[dc % 2]
                                self.transpose(pb2[0][...], pb2[1],
                                               otm[:, ti, dc * 128:(dc + 1) * 128],
                                               otm_b, ident, ident_b)
                                self.vcopy(odm[:, dc, :], odm_b, pb2[0][...], pb2[1])
                            for dh in range(2):
                                pb = psf[dh % 2]
                                terms = [(odm[:, dci, :],
                                          wo_sb[:, dci, dh * 512:(dh + 1) * 512])
                                         for dci in range(DC)]
                                self.mm_group(pb[0][...], pb[1], terms,
                                              reads=(odm_b, wo_sb_b))
                                self.tt(so[p][:, dh * 512:(dh + 1) * 512], so_b[p],
                                        pb[0][...], pb[1],
                                        bot[:, dh * 512:(dh + 1) * 512], bot_b, ALU.add)
                            if debug:
                                self.dma("sync",
                                         self.dram["l2_s_dump"][it * 128:(it + 1) * 128, :],
                                         so[p][...], reads=(so_b[p],),
                                         writes=(self._sdump_buf2,))
                            self.tt(so[p][...], so_b[p], so[p][...], so_b[p],
                                    xg[2 + p][...], xg_b[2 + p], ALU.mult)
                            self.tt(so[p][...], so_b[p], so[p][...], so_b[p],
                                    xg[p][...], xg_b[p], ALU.add)
                            self.dma("sync", x_dst_ap[it * 128:(it + 1) * 128, :],
                                     so[p][...], reads=(so_b[p],), writes=(x_dst_buf,))
                self.close_scope(esA)
            self.close_scope(es_xn)




    def ln_pass_affine(self, x_src_ap, x_src_buf, out_dmaj, out_dmaj_buf,
                      gam_ap, gam_b, bet_ap, bet_b, ident, ident_b, pst, es_ph, tag):
        xt, xt_b, xc, xc_b, xnb, xnb_b = [], [], [], [], [], []
        for k in range(2):
            a, b = self.sb(f"{tag}_xt{k}", [128, 1024], F32, es_ph); xt.append(a); xt_b.append(b)
            a, b = self.sb(f"{tag}_xc{k}", [128, 1024], F32, es_ph); xc.append(a); xc_b.append(b)
            a, b = self.sb(f"{tag}_xnb{k}", [128, 1024], BF16, es_ph); xnb.append(a); xnb_b.append(b)
        colsL, colsL_b = [], []
        for k in range(2):
            a, b = self.sb(f"{tag}_cols{k}", [128, 8], F32, es_ph)
            colsL.append(a); colsL_b.append(b)
        for i in range(NT):
            p = i % 2
            cols, cols_b = colsL[p], colsL_b[p]
            self.dma("sync", xt[p][...], x_src_ap[i * 128:(i + 1) * 128, :],
                     reads=(x_src_buf,), writes=(xt_b[p],))
            c0 = 0
            s_ap = cols[:, c0:c0 + 1]
            self.vred(s_ap, cols_b, xt[p][...], xt_b[p], ALU.add)
            nm_ap = cols[:, c0 + 1:c0 + 2]
            self.ts(nm_ap, cols_b, s_ap, cols_b, -1.0 / D, None, ALU.mult)
            self.ts(xc[p][...], xc_b[p], xt[p][...], xt_b[p], nm_ap, None, ALU.add,
                    s_bufs=(cols_b,))
            v_ap = cols[:, c0 + 2:c0 + 3]
            self.act(xt[p][...], xt_b[p], xc[p][...], xc_b[p], AF.Square,
                     accum=v_ap, accum_buf=cols_b)
            r1 = cols[:, c0 + 3:c0 + 4]
            self.act(r1, cols_b, v_ap, cols_b, AF.Ln, bias=1.0e-5, scale=1.0 / D)
            self.act(r1, cols_b, r1, cols_b, AF.Exp, scale=-0.5)
            self.ts(xc[p][...], xc_b[p], xc[p][...], xc_b[p], r1, None, ALU.mult,
                    s_bufs=(cols_b,))
            self.tt(xc[p][...], xc_b[p], xc[p][...], xc_b[p], gam_ap, gam_b, ALU.mult)
            self.tt(xc[p][...], xc_b[p], xc[p][...], xc_b[p], bet_ap, bet_b, ALU.add)
            self.vcopy(xnb[p][...], xnb_b[p], xc[p][...], xc_b[p])
            for dc in range(DC):
                pb = dc % 2
                self.transpose(pst[pb][0][...], pst[pb][1],
                               xnb[p][:, dc * 128:(dc + 1) * 128], xnb_b[p],
                               ident, ident_b)
                self.vcopy(out_dmaj[:, dc, i * 128:(i + 1) * 128], out_dmaj_buf,
                           pst[pb][0][...], pst[pb][1])

    def build_layer3(self, W, x_src_ap, x_src_buf, x_dst_ap, x_dst_buf, consts,
                     debug=False):
        P = self.prog
        ident, ident_b = consts["ident"]
        identf, identf_b = consts["identf"]
        psf = consts["psf"]
        pst = consts["pst"]

        if debug:
            _, self._sdump_buf3 = self.dout("l3_s_dump", [T, D], F32)
        esL = contextlib.ExitStack()
        with esL:
            if debug:
                g_scr, g_scr_b = self.dout("l3_g_dump", [T, D], F32)
            else:
                g_scr, g_scr_b = self.dscratch("l3_g_scr", [T, D], F32)
            xn_scr, xn_scr_b = self.dscratch("l3_xn_scr", [T, D], F32)
            h1_scr, h1_scr_b = self.dscratch("l3_h1_scr", [DC, 128, T + 8], BF16)
            x2_scr, x2_scr_b = self.dscratch("l3_x2_scr", [T, D], F32)
            x3_scr, x3_scr_b = self.dscratch("l3_x3_scr", [T, D], F32)

            es_xn = contextlib.ExitStack()
            with es_xn:
                xn, xn_b = self.sb("l3_xn", [128, DC, T], BF16, es_xn)
                es_ln = contextlib.ExitStack()
                with es_ln:
                    zpad, zpad_b = self.sb("l3_zpad", [128, DC, 8], BF16, es_ln)
                    self.prog.emit("vector", lambda E: E.memset(zpad[...], 0.0),
                                   writes=(zpad_b,))
                    self.dma("sync", h1_scr[:, :, 0:8].rearrange("dc p t -> p dc t"),
                             zpad[...], reads=(zpad_b,), writes=(h1_scr_b,))
                    cg, cg_b = self.load_tile(W, "l3_cg_d", [128, 1024], F32, es_ln, "l3cgt")
                    cb, cb_b = self.load_tile(W, "l3_cb_d", [128, 1024], F32, es_ln, "l3cbt")
                    self.ln_pass_dual(x_src_ap, x_src_buf, xn, xn_b,
                                      xn_scr, xn_scr_b,
                                      h1_scr[:, :, 8:8 + T], h1_scr_b,
                                      cg[...], cg_b, cb[...], cb_b,
                                      ident, ident_b, pst, es_ln, "l3ln")
                self.close_scope(es_ln)

                esG = contextlib.ExitStack()
                gates, gates_b = self.sb("l3_gates", [128, NT, 8], F32, esL, side="left")
                sigc, sigc_b = self.sb("l3_sig", [128, NT], F32, esL, side="left")
                if True:
                    rw, rw_b = W["l3_rw"]
                    sgw, sgw_b = W["l3_sgw"]
                    gt_tmp, gt_tmp_b = self.sb("l3_gt", [128, 4, 8], F32, esG)
                    gcol, gcol_b = self.sb("l3_gcol", [128, 8], F32, esG)
                    for i in range(NT):
                        pb = psf[i % 2]
                        terms = [(xn[:, dc, i * 128:(i + 1) * 128], rw[:, dc, :])
                                 for dc in range(DC)]
                        self.mm_group(pb[0][:, 0:8], pb[1], terms, reads=(xn_b, rw_b))
                        t0 = gt_tmp[:, (i % 2) * 2, :]
                        t1 = gt_tmp[:, (i % 2) * 2 + 1, :]
                        c0 = (i % 2) * 4
                        m_ap = gcol[:, c0:c0 + 1]
                        self.vred(m_ap, gcol_b, pb[0][:, 0:8], pb[1], ALU.max)
                        nm = gcol[:, c0 + 1:c0 + 2]
                        self.ts(nm, gcol_b, m_ap, gcol_b, -1.0, None, ALU.mult)
                        l_ap = gcol[:, c0 + 2:c0 + 3]
                        self.act(t1, gt_tmp_b, pb[0][:, 0:8], pb[1], AF.Exp, bias=nm,
                                 scale=1.0, accum=l_ap, accum_buf=gcol_b,
                                 extra_reads=(gcol_b,))
                        r_ap = gcol[:, c0 + 3:c0 + 4]
                        self.recip(r_ap, gcol_b, l_ap, gcol_b)
                        self.ts(t1, gt_tmp_b, t1, gt_tmp_b, r_ap, None, ALU.mult,
                                s_bufs=(gcol_b,))
                        self.vred(m_ap, gcol_b, t1, gt_tmp_b, ALU.max)
                        self.ts(t0, gt_tmp_b, t1, gt_tmp_b, m_ap, None, ALU.is_equal,
                                s_bufs=(gcol_b,))
                        self.stt(t0, gt_tmp_b, t0, gt_tmp_b, NEG, None, t1,
                                 gt_tmp_b, ALU.mult, ALU.add)
                        self.vred(nm, gcol_b, t0, gt_tmp_b, ALU.max)
                        self.ts(gates[:, i, :], gates_b, t1, gt_tmp_b, nm, None,
                                ALU.is_ge, s_bufs=(gcol_b,))
                        self.tt(gates[:, i, :], gates_b, gates[:, i, :], gates_b, t1,
                                gt_tmp_b, ALU.mult)
                        pbs = psf[2]
                        terms = [(xn[:, dc, i * 128:(i + 1) * 128], sgw[:, dc:dc + 1])
                                 for dc in range(DC)]
                        self.mm_group(pbs[0][:, 0:1], pbs[1], terms, reads=(xn_b, sgw_b))
                        self.act(sigc[:, i:i + 1], sigc_b, pbs[0][:, 0:1], pbs[1],
                                 AF.Sigmoid)
                self.close_scope(esG)

                esF = contextlib.ExitStack()
                if True:
                    g_sb, g_sb_b = self.sb("l3_gq", [128, 8, 1024], F32, esF)
                    w1u, w1u_b, w2u, w2u_b, h_sb, h_sb_b = [], [], [], [], [], []
                    for k in range(2):
                        a, b = self.sb(f"l3_w1u{k}", [128, 8, 512], BF16, esF)
                        w1u.append(a); w1u_b.append(b)
                        a, b = self.sb(f"l3_w2u{k}", [128, 4, 1024], BF16, esF)
                        w2u.append(a); w2u_b.append(b)
                        a, b = self.sb(f"l3_h{k}", [128, 4, 512], BF16, esF)
                        h_sb.append(a); h_sb_b.append(b)
                    sgu_u, sgu_u_b = [], []
                    for k in range(2):
                        a, b = self.sb(f"l3_sgu{k}", [128, 8, 1024], BF16, esF)
                        sgu_u.append(a); sgu_u_b.append(b)
                    sdw, sdw_b = self.sb("l3_sdw", [128, 8, 1024], BF16, esF)
                    self.dma("gpsimd", sdw[...],
                             W["l3_sdown_dram"][0].rearrange("(fc p) d -> p fc d", p=128),
                             reads=(W["l3_sdown_dram"][1],), writes=(sdw_b,))
                    for k in range(2):
                        self.dma("gpsimd", sgu_u[k][...],
                                 W["l3_sgu_dram"][0][:, k, :, :]
                                 .rearrange("(dc p) g f -> p dc (g f)", p=128),
                                 reads=(W["l3_sgu_dram"][1],), writes=(sgu_u_b[k],))
                    sh1, sh1_b = self.sb("l3_sh1", [128, 2, 512], BF16, esF)
                    sh1d, sh1d_b = self.sb("l3_sh1d", [128, 8, 128], BF16, esF)
                    shg, shg_b = self.sb("l3_shg", [128, 512], F32, esF)
                    W1d, W1d_b = W["l3_w1_dram"]
                    W2d, W2d_b = W["l3_w2_dram"]
                    for quarter in range(4):
                        tq0 = quarter * 8
                        for ti in range(8):
                            it = tq0 + ti
                            tt0 = it * 128
                            for u in range(2):
                                for half in range(2):
                                    pb = psf[u * 2 + half]
                                    terms = [(xn[:, dc, tt0:tt0 + 128],
                                              sgu_u[u][:, dc, half * 512:(half + 1) * 512])
                                             for dc in range(DC)]
                                    self.mm_group(pb[0][...], pb[1], terms,
                                                  reads=(xn_b, sgu_u_b[u]))
                                self.act(shg[...], shg_b, psf[u * 2][0][...],
                                         psf[u * 2][1], AF.Silu)
                                self.tt(sh1[:, u, :], sh1_b, shg[...], shg_b,
                                        psf[u * 2 + 1][0][...], psf[u * 2 + 1][1],
                                        ALU.mult)
                            for fc in range(DC):
                                pb2 = pst[fc % 2]
                                u, off = fc // 4, (fc % 4) * 128
                                self.transpose(pb2[0][...], pb2[1],
                                               sh1[:, u, off:off + 128], sh1_b,
                                               ident, ident_b)
                                self.vcopy(sh1d[:, fc, :], sh1d_b, pb2[0][...], pb2[1])
                            for dh in range(2):
                                pb = psf[4 + dh]
                                terms = [(sh1d[:, fci, :],
                                          sdw[:, fci, dh * 512:(dh + 1) * 512])
                                         for fci in range(DC)]
                                self.mm_group(pb[0][...], pb[1], terms,
                                              reads=(sh1d_b, sdw_b))
                                self.ts(g_sb[:, ti, dh * 512:(dh + 1) * 512], g_sb_b,
                                        pb[0][...], pb[1], sigc[:, it:it + 1], None,
                                        ALU.mult, s_bufs=(sigc_b,))
                        for e in range(8):
                            for fb in range(2):
                                u = e * 2 + fb
                                pw = u % 2
                                self.dma("gpsimd", w1u[pw][...],
                                         W1d[e, :, fb * 512:(fb + 1) * 512]
                                         .rearrange("(dc p) f -> p dc f", p=128),
                                         reads=(W1d_b,), writes=(w1u_b[pw],))
                                self.dma("gpsimd", w2u[pw][...],
                                         W2d[e, fb * 512:(fb + 1) * 512, :]
                                         .rearrange("(fc p) d -> p fc d", p=128),
                                         reads=(W2d_b,), writes=(w2u_b[pw],))
                                for tc in range(2):
                                    t0c = tq0 * 128 + tc * 512
                                    ph = tc % 2
                                    for ft in range(4):
                                        pb = psf[ft % 4]
                                        terms = [(w1u[pw][:, dc, ft * 128:(ft + 1) * 128],
                                                  xn[:, dc, t0c:t0c + 512])
                                                 for dc in range(DC)]
                                        self.mm_group(pb[0][...], pb[1], terms,
                                                      reads=(w1u_b[pw], xn_b))
                                        self.act(h_sb[ph][:, ft, :], h_sb_b[ph],
                                                 pb[0][...], pb[1], AF.Silu)
                                for tc in range(2):
                                    t0c = tq0 * 128 + tc * 512
                                    ph = tc % 2
                                    for ti in range(4):
                                        itt = (t0c // 128) + ti
                                        gq_ti = itt - tq0
                                        for dh in range(2):
                                            pb = psf[4 + ((ti * 2 + dh) % 2)]
                                            terms = [(h_sb[ph][:, fc, ti * 128:(ti + 1) * 128],
                                                      w2u[pw][:, fc, dh * 512:(dh + 1) * 512])
                                                     for fc in range(4)]
                                            self.mm_group(pb[0][...], pb[1], terms,
                                                          reads=(h_sb_b[ph], w2u_b[pw]))
                                            self.stt(g_sb[:, gq_ti, dh * 512:(dh + 1) * 512],
                                                     g_sb_b, pb[0][...], pb[1],
                                                     gates[:, itt, e:e + 1], gates_b,
                                                     g_sb[:, gq_ti, dh * 512:(dh + 1) * 512],
                                                     g_sb_b, ALU.mult, ALU.add)
                        for ti in range(8):
                            it = tq0 + ti
                            self.act(shg[...], shg_b, g_sb[:, ti, 0:512], g_sb_b, AF.Silu)
                            self.dma("sync", g_scr[it * 128:(it + 1) * 128, 0:512],
                                     shg[...], reads=(shg_b,), writes=(g_scr_b,))
                            self.act(shg[...], shg_b, g_sb[:, ti, 512:1024], g_sb_b,
                                     AF.Silu)
                            self.dma("sync", g_scr[it * 128:(it + 1) * 128, 512:1024],
                                     shg[...], reads=(shg_b,), writes=(g_scr_b,))
                self.close_scope(esF)
            self.close_scope(es_xn)

            esCV = contextlib.ExitStack()
            if True:
                wcv, wcv_b = self.sb("l3_wcv", [128, 64, 128], BF16, esCV)
                self.dma("gpsimd", wcv[...],
                         W["l3_convw_dram"][0].rearrange("k o c p f -> p (k o c) f"),
                         reads=(W["l3_convw_dram"][1],), writes=(wcv_b,))
                cbs, cbs_b = W["l3_cbias"]
                h1c, h1c_b = [], []
                for k in range(2):
                    a, b = self.sb(f"l3_h1c{k}", [128, DC, 520], BF16, esCV)
                    h1c.append(a); h1c_b.append(b)
                cvd, cvd_b = self.sb("l3_cvd", [128, DC, 512], F32, esCV)
                x2t, x2t_b = [], []
                for k in range(2):
                    a, b = self.sb(f"l3_x2t{k}", [128, 1024], F32, esCV)
                    x2t.append(a); x2t_b.append(b)
                for tc in range(T // 512):
                    t0c = tc * 512
                    p = tc % 2
                    self.dma("sync", h1c[p][:, :, 0:519],
                             h1_scr[:, :, t0c + 1:t0c + 520]
                             .rearrange("dc p t -> p dc t"),
                             reads=(h1_scr_b,), writes=(h1c_b[p],))
                    for oc8 in range(8):
                        g = oc8 // 2
                        pb = psf[oc8 % 4]
                        terms = []
                        for k in range(4):
                            for cic in range(2):
                                widx = k * 16 + oc8 * 2 + cic
                                terms.append((wcv[:, widx, :],
                                              h1c[p][:, g * 2 + cic, 4 + k:4 + k + 512]))
                        self.mm_group(pb[0][...], pb[1], terms, reads=(wcv_b, h1c_b[p]))
                        self.act(cvd[:, oc8, :], cvd_b, pb[0][...], pb[1], AF.Identity,
                                 bias=cbs[:, oc8:oc8 + 1], extra_reads=(cbs_b,))
                    for ti in range(4):
                        it = tc * 4 + ti
                        self.dma("sync", x2t[p][...],
                                 xn_scr[it * 128:(it + 1) * 128, :],
                                 reads=(xn_scr_b,), writes=(x2t_b[p],))
                        for oc8 in range(8):
                            pb2 = psf[4 + oc8 % 2]
                            self.transpose(pb2[0][:, 0:128], pb2[1],
                                           cvd[:, oc8, ti * 128:(ti + 1) * 128], cvd_b,
                                           identf, identf_b)
                            self.tt(x2t[p][:, oc8 * 128:(oc8 + 1) * 128], x2t_b[p],
                                    x2t[p][:, oc8 * 128:(oc8 + 1) * 128], x2t_b[p],
                                    pb2[0][:, 0:128], pb2[1], ALU.add)
                        self.dma("sync", x2_scr[it * 128:(it + 1) * 128, :],
                                 x2t[p][...], reads=(x2t_b[p],), writes=(x2_scr_b,))
            self.close_scope(esCV)

            es4 = contextlib.ExitStack()
            if True:
                h2, h2_b = self.sb("l3_h2", [128, DC, T], BF16, es4)
                es_ln2 = contextlib.ExitStack()
                with es_ln2:
                    ag, ag_b = self.load_tile(W, "l3_ag_d", [128, 1024], F32, es_ln2, "l3agt")
                    ab, ab_b = self.load_tile(W, "l3_ab_d", [128, 1024], F32, es_ln2, "l3abt")
                    self.ln_pass_affine(x2_scr, x2_scr_b, h2, h2_b,
                                        ag[...], ag_b, ab[...], ab_b,
                                        ident, ident_b, pst, es_ln2, "l3ln2")
                self.close_scope(es_ln2)
                ones_sc, ones_sc_b = self.sb("l3_ones", [128, T], F32, es4)
                self.prog.emit("vector", lambda E: E.memset(ones_sc[...], 1.0),
                               writes=(ones_sc_b,))
                isc, isc_b = self.sb("l3_isc", [128, T], F32, es4)
                self.dma("sync", isc[...], W["l3_iscale_dram"][0][...],
                         reads=(W["l3_iscale_dram"][1],), writes=(isc_b,))
                qb_, qb_b = W["l3_bq"]
                kb_, kb_b = W["l3_bk"]
                vb_, vb_b = W["l3_bv"]
                wq3, wq3_b = self.sb("l3_wq3", [128, DC, 128], BF16, es4)
                wk3, wk3_b = self.sb("l3_wk3", [128, DC, 128], BF16, es4)
                wv3, wv3_b = self.sb("l3_wv3", [128, DC, 128], BF16, es4)
                qs, qs_b = self.sb("l3_qs", [128, T], F32, es4)
                ks, ks_b = self.sb("l3_ks", [128, T], F32, es4)
                vs, vs_b = self.sb("l3_vs", [128, T], F32, es4)
                tmps, tmps_b = self.sb("l3_tmps", [128, T], F32, es4)
                x2cb, x2cb_b = self.sb("l3_x2cb", [128, 32, 128], F32, es4)
                for dc in range(DC):
                    self.dma("gpsimd", wq3[...],
                             W["l3_WqT_dram"][0][:, dc * 128:(dc + 1) * 128]
                             .rearrange("(dci p) e -> p dci e", p=128),
                             reads=(W["l3_WqT_dram"][1],), writes=(wq3_b,))
                    self.dma("gpsimd", wk3[...],
                             W["l3_WkT_dram"][0][:, dc * 128:(dc + 1) * 128]
                             .rearrange("(dci p) e -> p dci e", p=128),
                             reads=(W["l3_WkT_dram"][1],), writes=(wk3_b,))
                    self.dma("gpsimd", wv3[...],
                             W["l3_WvT_dram"][0][:, dc * 128:(dc + 1) * 128]
                             .rearrange("(dci p) e -> p dci e", p=128),
                             reads=(W["l3_WvT_dram"][1],), writes=(wv3_b,))
                    for tc in range(DC):
                        t0c = tc * 512
                        for which, wsb, wsb_b, dst, dst_b, bias_t, bias_tb in (
                                (0, wq3, wq3_b, qs, qs_b, qb_, qb_b),
                                (1, wk3, wk3_b, ks, ks_b, kb_, kb_b),
                                (2, wv3, wv3_b, vs, vs_b, vb_, vb_b)):
                            pb = psf[which % 4]
                            terms = [(wsb[:, dci, :], h2[:, dci, t0c:t0c + 512])
                                     for dci in range(DC)]
                            self.mm_group(pb[0][...], pb[1], terms, reads=(wsb_b, h2_b))
                            self.act(dst[:, t0c:t0c + 512], dst_b, pb[0][...], pb[1],
                                     AF.Identity, bias=bias_t[:, dc:dc + 1],
                                     extra_reads=(bias_tb,))
                    self.prog.emit("vector",
                                   lambda E, o=tmps[...], d0=ones_sc[...], d1=qs[...]:
                                   E.tensor_tensor_scan(o, d0, d1, 0.0, ALU.mult, ALU.add),
                                   reads=(ones_sc_b, qs_b), writes=(tmps_b,))
                    self.tt(tmps[...], tmps_b, tmps[...], tmps_b, isc[...], isc_b,
                            ALU.mult)
                    self.tt(tmps[...], tmps_b, tmps[...], tmps_b, ks[...], ks_b,
                            ALU.mult)
                    self.prog.emit("vector",
                                   lambda E, o=qs[...], d0=ones_sc[...], d1=tmps[...]:
                                   E.tensor_tensor_scan(o, d0, d1, 0.0, ALU.mult, ALU.add),
                                   reads=(ones_sc_b, tmps_b), writes=(qs_b,))
                    self.tt(qs[...], qs_b, qs[...], qs_b, isc[...], isc_b, ALU.mult)
                    self.tt(qs[...], qs_b, qs[...], qs_b, vs[...], vs_b, ALU.mult)
                    self.dma("sync", x2cb[...],
                             x2_scr[:, dc * 128:(dc + 1) * 128]
                             .rearrange("(tt p) o -> p tt o", p=128),
                             reads=(x2_scr_b,), writes=(x2cb_b,))
                    for ti in range(NT):
                        pb2 = psf[4 + ti % 2]
                        self.transpose(pb2[0][:, 0:128], pb2[1],
                                       qs[:, ti * 128:(ti + 1) * 128], qs_b,
                                       identf, identf_b)
                        self.tt(x2cb[:, ti, :], x2cb_b, x2cb[:, ti, :], x2cb_b,
                                pb2[0][:, 0:128], pb2[1], ALU.add)
                    self.dma("sync",
                             x3_scr[:, dc * 128:(dc + 1) * 128]
                             .rearrange("(tt p) o -> p tt o", p=128),
                             x2cb[...], reads=(x2cb_b,), writes=(x3_scr_b,))
            self.close_scope(es4)

            es5 = contextlib.ExitStack()
            if True:
                h3, h3_b = self.sb("l3_h3", [128, DC, T], BF16, es5)
                es_ln3 = contextlib.ExitStack()
                with es_ln3:
                    bg, bg_b = self.load_tile(W, "l3_bg_d", [128, 1024], F32, es_ln3, "l3bgt")
                    bb2, bb2_b = self.load_tile(W, "l3_bb_d", [128, 1024], F32, es_ln3, "l3bbt")
                    self.ln_pass_affine(x3_scr, x3_scr_b, h3, h3_b,
                                        bg[...], bg_b, bb2[...], bb2_b,
                                        ident, ident_b, pst, es_ln3, "l3ln3")
                self.close_scope(es_ln3)
                y_sb, y_sb_b = self.sb("l3_yq", [128, 8, 1024], F32, es5)
                w1u, w1u_b, w2u, w2u_b, h_sb, h_sb_b = [], [], [], [], [], []
                for k in range(2):
                    a, b = self.sb(f"l3_bw1u{k}", [128, 8, 512], BF16, es5)
                    w1u.append(a); w1u_b.append(b)
                    a, b = self.sb(f"l3_bw2u{k}", [128, 4, 1024], BF16, es5)
                    w2u.append(a); w2u_b.append(b)
                    a, b = self.sb(f"l3_bh{k}", [128, 4, 512], BF16, es5)
                    h_sb.append(a); h_sb_b.append(b)
                but, but_b = self.load_tile(W, "l3_bu_d", [128, 1024], F32, es5, "l3but")
                but = but[...]
                bbr, bbr_b = W["l3_bbias"]
                WbT, WbT_b = W["l3_WbT_dram"]
                WuT, WuT_b = W["l3_WuT_dram"]
                xg, xg_b = [], []
                for k in range(6):
                    a, b = self.sb(f"l3_xg{k}", [128, D], F32, es5)
                    xg.append(a); xg_b.append(b)
                so, so_b = [], []
                for k in range(2):
                    a, b = self.sb(f"l3_so{k}", [128, D], F32, es5)
                    so.append(a); so_b.append(b)
                NFB = 8
                for quarter in range(4):
                    tq0 = quarter * 8
                    for fb in range(NFB):
                        pw = fb % 2
                        self.dma("gpsimd", w1u[pw][...],
                                 WbT[:, fb * 512:(fb + 1) * 512]
                                 .rearrange("(dc p) f -> p dc f", p=128),
                                 reads=(WbT_b,), writes=(w1u_b[pw],))
                        self.dma("gpsimd", w2u[pw][...],
                                 WuT[fb * 512:(fb + 1) * 512, :]
                                 .rearrange("(fc p) d -> p fc d", p=128),
                                 reads=(WuT_b,), writes=(w2u_b[pw],))
                        for tc in range(2):
                            t0c = tq0 * 128 + tc * 512
                            ph = tc % 2
                            for ft in range(4):
                                pb = psf[ft % 4]
                                terms = [(w1u[pw][:, dc, ft * 128:(ft + 1) * 128],
                                          h3[:, dc, t0c:t0c + 512])
                                         for dc in range(DC)]
                                self.mm_group(pb[0][...], pb[1], terms,
                                              reads=(w1u_b[pw], h3_b))
                                self.act(h_sb[ph][:, ft, :], h_sb_b[ph],
                                         pb[0][...], pb[1], AF.Gelu,
                                         bias=bbr[:, fb * 4 + ft:fb * 4 + ft + 1],
                                         extra_reads=(bbr_b,))
                        for tc in range(2):
                            t0c = tq0 * 128 + tc * 512
                            ph = tc % 2
                            for ti in range(4):
                                itt = (t0c // 128) + ti
                                yq_ti = itt - tq0
                                for dh in range(2):
                                    pb = psf[4 + ((ti * 2 + dh) % 2)]
                                    terms = [(h_sb[ph][:, fc, ti * 128:(ti + 1) * 128],
                                              w2u[pw][:, fc, dh * 512:(dh + 1) * 512])
                                             for fc in range(4)]
                                    self.mm_group(pb[0][...], pb[1], terms,
                                                  reads=(h_sb_b[ph], w2u_b[pw]))
                                    dst = y_sb[:, yq_ti, dh * 512:(dh + 1) * 512]
                                    if fb == 0:
                                        self.act(dst, y_sb_b, pb[0][...], pb[1],
                                                 AF.Identity)
                                    else:
                                        self.tt(dst, y_sb_b, dst, y_sb_b,
                                                pb[0][...], pb[1], ALU.add)
                    for ti in range(8):
                        it = tq0 + ti
                        p = it % 2
                        self.dma("sync", xg[p][...],
                                 x_src_ap[it * 128:(it + 1) * 128, :],
                                 reads=(x_src_buf,), writes=(xg_b[p],))
                        self.dma("sync", xg[2 + p][...],
                                 g_scr[it * 128:(it + 1) * 128, :],
                                 reads=(g_scr_b,), writes=(xg_b[2 + p],))
                        self.dma("sync", xg[4 + p][...],
                                 x3_scr[it * 128:(it + 1) * 128, :],
                                 reads=(x3_scr_b,), writes=(xg_b[4 + p],))
                        self.tt(so[p][...], so_b[p], y_sb[:, ti, :], y_sb_b,
                                but[...], but_b, ALU.add)
                        self.tt(so[p][...], so_b[p], so[p][...], so_b[p],
                                xg[4 + p][...], xg_b[4 + p], ALU.add)
                        if debug:
                            self.dma("sync",
                                     self.dram["l3_s_dump"][it * 128:(it + 1) * 128, :],
                                     so[p][...], reads=(so_b[p],),
                                     writes=(self._sdump_buf3,))
                        self.tt(so[p][...], so_b[p], so[p][...], so_b[p],
                                xg[2 + p][...], xg_b[2 + p], ALU.mult)
                        self.tt(so[p][...], so_b[p], so[p][...], so_b[p],
                                xg[p][...], xg_b[p], ALU.add)
                        self.dma("sync", x_dst_ap[it * 128:(it + 1) * 128, :],
                                 so[p][...], reads=(so_b[p],), writes=(x_dst_buf,))
            self.close_scope(es5)





def prep_l0_host(inp, cp, consts_np):
    W = {}
    nc = cp.nc

    def dram(name, arr, dtype=F32):
        ap, buf = cp.din(name, list(arr.shape), dtype)
        cp.inputs[name] = np.ascontiguousarray(arr, dtype=np.float32)
        return ap, buf

    def sbuf_const(name, arr, sb_dtype):
        ap, buf = dram(name, arr)
        s_ap, s_buf = cp.sb("s_" + name, list(arr.shape), sb_dtype)
        cp.dma("gpsimd", s_ap[...], ap[...], reads=(buf,), writes=(s_buf,))
        return s_ap, s_buf

    W["l0_wqT"] = dram("l0_wqT", inp["L0_attn_wq"].T)
    W["l0_wkT"] = dram("l0_wkT", inp["L0_attn_wk"].T)
    W["l0_wvT"] = dram("l0_wvT", inp["L0_attn_wv"].T)
    W["l0_W1_dram"] = dram("l0_W1", inp["L0_ffn_W1"])
    W["l0_W2_dram"] = dram("l0_W2", inp["L0_ffn_W2"])
    W["cos_d"] = dram("c_cos_d", consts_np["cos_d"])
    W["sin_d"] = dram("c_sin_d", consts_np["sin_d"])
    W["mask4"] = dram("c_mask4", consts_np["mask4"])
    rw = inp["L0_ffn_router_W"].reshape(8, 128, 8).transpose(1, 0, 2)
    W["l0_rw_d"] = dram("l0_rw", rw)
    rbt = np.tile(inp["L0_ffn_router_b"][None, :], (128, 1))
    W["l0_rbt_d"] = dram("l0_rbt", rbt)
    b1r = inp["L0_ffn_b1"].reshape(8, 32, 128).transpose(2, 0, 1)
    W["l0_b1_d"] = dram("l0_b1r", b1r)
    W["l0_b2_d"] = dram("l0_b2", inp["L0_ffn_b2"])
    return W

def prep_l1_host(inp, cp):
    W = {}

    def dram(name, arr, dtype=F32):
        ap, buf = cp.din(name, list(arr.shape), dtype)
        cp.inputs[name] = np.ascontiguousarray(arr, dtype=np.float32)
        return ap, buf

    def sbuf_const(name, arr, sb_dtype):
        ap, buf = dram(name, arr)
        s_ap, s_buf = cp.sb("s_" + name, list(arr.shape), sb_dtype)
        cp.dma("gpsimd", s_ap[...], ap[...], reads=(buf,), writes=(s_buf,))
        return s_ap, s_buf

    W["l1_WqT_dram"] = dram("l1_WqT", inp["L1_attn_Wq"].T)
    W["l1_WkT_dram"] = dram("l1_WkT", inp["L1_attn_Wk"].T)
    W["l1_WvT_dram"] = dram("l1_WvT", inp["L1_attn_Wv"].T)
    W["l1_WoT_dram"] = dram("l1_WoT", inp["L1_attn_Wo"].T)
    W["l1_WfcT_dram"] = dram("l1_WfcT", inp["L1_ffn_W_fc"].T)
    W["l1_WprT_dram"] = dram("l1_WprT", inp["L1_ffn_W_proj"].T)
    W["l1_gam_d"] = dram("l1_gam", np.tile(inp["L1_attn_ln_gamma"][None, :], (128, 1)))
    W["l1_bet_d"] = dram("l1_bet", np.tile(inp["L1_attn_ln_beta"][None, :], (128, 1)))
    rel = inp["L1_attn_rel"]
    ii = np.arange(128)[:, None]
    jj = np.arange(128)[None, :]
    idx0 = np.clip(0 + ii - jj, 0, 128)
    idx1 = np.clip(128 + ii - jj, 0, 128)
    B0 = rel[:, idx0]
    B1 = rel[:, idx1]
    W["l1_B0_dram"] = dram("l1_B0", B0)
    W["l1_B1_dram"] = dram("l1_B1", B1)
    far = np.tile(rel[:, 128][None, :], (128, 1))
    W["l1_far_dram"] = dram("l1_far", far)
    mskd = np.triu(np.full((128, 128), NEG, np.float32), k=1)
    W["l1_maskdiag_dram"] = dram("l1_maskdiag", mskd)
    return W


def prep_l2_host(inp, cp):
    W = {}

    def dram(name, arr, dtype=F32):
        ap, buf = cp.din(name, list(arr.shape), dtype)
        cp.inputs[name] = np.ascontiguousarray(arr, dtype=np.float32)
        return ap, buf

    def sbuf_const(name, arr, sb_dtype):
        ap, buf = dram(name, arr)
        s_ap, s_buf = cp.sb("s_" + name, list(arr.shape), sb_dtype)
        cp.dma("gpsimd", s_ap[...], ap[...], reads=(buf,), writes=(s_buf,))
        return s_ap, s_buf

    W["l2_Wq_dram"] = dram("l2_Wq", inp["L2_attn_Wq"])
    W["l2_Wk_dram"] = dram("l2_Wk", inp["L2_attn_Wk"])
    W["l2_Wv_dram"] = dram("l2_Wv", inp["L2_attn_Wv"])
    W["l2_Wo_dram"] = dram("l2_Wo", inp["L2_attn_Wo"])
    W["l2_W1_dram"] = dram("l2_W1", inp["L2_ffn_W1"])
    W["l2_W2_dram"] = dram("l2_W2", inp["L2_ffn_W2"])
    W["l2_proj2"] = sbuf_const("l2_proj2",
                               np.concatenate([inp["L2_attn_proj"]] * 2, axis=0), BF16)
    W["l2_ones"] = sbuf_const("l2_onesc", np.ones((128, 1), np.float32), BF16)
    rw = inp["L2_ffn_Wr"].reshape(8, 128, 8).transpose(1, 0, 2)
    W["l2_Wr"] = sbuf_const("l2_rw", rw, BF16)
    W["l2_br_t"] = sbuf_const("l2_rbt", np.tile(inp["L2_ffn_br"][None, :], (128, 1)), F32)
    b1r = inp["L2_ffn_b1"].reshape(8, 16, 128).transpose(2, 0, 1)
    W["l2_b1"] = sbuf_const("l2_b1r", b1r, F32)
    W["l2_b2"] = sbuf_const("l2_b2", inp["L2_ffn_b2"], BF16)
    W["l2_bo_d"] = dram("l2_bot", np.tile(inp["L2_attn_bo"][None, :], (128, 1)))
    return W


def prep_l3_host(inp, cp):
    W = {}

    def dram(name, arr, dtype=F32):
        ap, buf = cp.din(name, list(arr.shape), dtype)
        cp.inputs[name] = np.ascontiguousarray(arr, dtype=np.float32)
        return ap, buf

    def sbuf_const(name, arr, sb_dtype):
        ap, buf = dram(name, arr)
        s_ap, s_buf = cp.sb("s_" + name, list(arr.shape), sb_dtype)
        cp.dma("gpsimd", s_ap[...], ap[...], reads=(buf,), writes=(s_buf,))
        return s_ap, s_buf

    W["l3_WqT_dram"] = dram("l3_WqT", inp["L3_attn_Wq"].T)
    W["l3_WkT_dram"] = dram("l3_WkT", inp["L3_attn_Wk"].T)
    W["l3_WvT_dram"] = dram("l3_WvT", inp["L3_attn_Wv"].T)
    W["l3_WbT_dram"] = dram("l3_WbT", inp["L3_attn_Wb"].T)
    W["l3_WuT_dram"] = dram("l3_WuT", inp["L3_attn_Wu"].T)
    W["l3_w1_dram"] = dram("l3_w1", inp["L3_ffn_w1"])
    W["l3_w2_dram"] = dram("l3_w2", inp["L3_ffn_w2"])
    W["l3_sdown_dram"] = dram("l3_sdown", inp["L3_ffn_shared_down"])
    sgu = inp["L3_ffn_shared_gate_up"]
    sgu_u = np.stack([
        np.stack([sgu[:, u * 512:(u + 1) * 512], sgu[:, 1024 + u * 512:1024 + (u + 1) * 512]],
                 axis=1)
        for u in range(2)], axis=1)
    W["l3_sgu_dram"] = dram("l3_sgu", sgu_u)
    cw = inp["L3_attn_conv_w"]
    blocks = np.zeros((4, 8, 2, 128, 128), np.float32)
    for k in range(4):
        for oc8 in range(8):
            g = oc8 // 2
            for cic in range(2):
                blk = cw[oc8 * 128:(oc8 + 1) * 128, cic * 128:(cic + 1) * 128, k]
                blocks[k, oc8, cic] = blk.T
    W["l3_convw_dram"] = dram("l3_convw", blocks)
    W["l3_cg_d"] = dram("l3_cg", np.tile(inp["L3_attn_ln_conv_g"][None, :], (128, 1)))
    W["l3_cb_d"] = dram("l3_cb", np.tile(inp["L3_attn_ln_conv_b"][None, :], (128, 1)))
    W["l3_ag_d"] = dram("l3_ag", np.tile(inp["L3_attn_ln_attn_g"][None, :], (128, 1)))
    W["l3_ab_d"] = dram("l3_ab", np.tile(inp["L3_attn_ln_attn_b"][None, :], (128, 1)))
    W["l3_bg_d"] = dram("l3_bg", np.tile(inp["L3_attn_ln_boom_g"][None, :], (128, 1)))
    W["l3_bb_d"] = dram("l3_bb", np.tile(inp["L3_attn_ln_boom_b"][None, :], (128, 1)))
    W["l3_bu_d"] = dram("l3_bu", np.tile(inp["L3_attn_bu"][None, :], (128, 1)))
    W["l3_cbias"] = sbuf_const("l3_cbv", inp["L3_attn_conv_b"].reshape(8, 128).T, F32)
    W["l3_bq"] = sbuf_const("l3_bqv", inp["L3_attn_bq"].reshape(8, 128).T, F32)
    W["l3_bk"] = sbuf_const("l3_bkv", inp["L3_attn_bk"].reshape(8, 128).T, F32)
    W["l3_bv"] = sbuf_const("l3_bvv", inp["L3_attn_bv"].reshape(8, 128).T, F32)
    W["l3_bbias"] = sbuf_const("l3_bbv", inp["L3_attn_bb"].reshape(32, 128).T, F32)
    rw = inp["L3_ffn_router_w"].reshape(8, 128, 8).transpose(1, 0, 2)
    W["l3_rw"] = sbuf_const("l3_rwv", rw, BF16)
    W["l3_sgw"] = sbuf_const("l3_sgwv", inp["L3_ffn_shared_gate_w"].reshape(8, 128).T, BF16)
    isc = (1.0 / np.arange(1, T + 1, dtype=np.float64)).astype(np.float32)
    W["l3_iscale_dram"] = dram("l3_iscale", np.tile(isc[None, :], (128, 1)))
    return W



from concourse.bass_utils import run_bass_kernel_spmd


def _build_full(inp):
    cp = CoreProgram()
    consts, c_np = cp.setup_consts()
    W0 = prep_l0_host(inp, cp, c_np)
    W1 = prep_l1_host(inp, cp)
    W2 = prep_l2_host(inp, cp)
    W3 = prep_l3_host(inp, cp)
    x_in_ap, x_in_b = cp.din("x_in", [T, D], F32)
    xA_ap, xA_b = cp.dscratch("xA", [T, D], F32)
    xB_ap, xB_b = cp.dscratch("xB", [T, D], F32)
    xout_ap, xout_b = cp.dout("x_out", [T, D], F32)

    cp.build_layer0(W0, x_in_ap, x_in_b, xA_ap, xA_b, consts)
    cp.prog.global_barrier()
    cp.build_layer1(W1, xA_ap, xA_b, xB_ap, xB_b, consts)
    cp.prog.global_barrier()
    cp.build_layer2(W2, xB_ap, xB_b, xA_ap, xA_b, consts)
    cp.prog.global_barrier()
    cp.build_layer3(W3, xA_ap, xA_b, xout_ap, xout_b, consts)
    nc = cp.finalize()
    return nc, cp


def kernel(**inputs):
    inp = {k: np.asarray(v) for k, v in inputs.items()}
    nc, cp = _build_full(inp)
    x_full = np.ascontiguousarray(inp["L0_attn_x"], dtype=np.float32)
    in_maps = []
    for core in range(8):
        m = dict(cp.inputs)
        m["x_in"] = np.ascontiguousarray(x_full[core])
        in_maps.append(m)
    res = run_bass_kernel_spmd(nc, in_maps, core_ids=list(range(8)))
    out = np.stack([np.asarray(r["x_out"]) for r in res.results]).astype(np.float32)
    return out


# version 10
# speedup vs baseline: 1.1029x; 1.0266x over previous
import contextlib
import math
import numpy as np
import ml_dtypes
import concourse.bass as bass
import concourse.mybir as mybir

F32 = mybir.dt.float32
BF16 = mybir.dt.bfloat16
AF = mybir.ActivationFunctionType
ALU = mybir.AluOpType
AX = mybir.AxisListType

T = 4096
D = 1024
NT = T // 128
DC = D // 128
NEG = -1.0e30


class Buf:
    __slots__ = ("name", "last_w", "readers")

    def __init__(self, name):
        self.name = name
        self.last_w = None
        self.readers = []


class Tok:
    __slots__ = ("key", "val", "snap")

    def __init__(self, key, val, snap):
        self.key = key
        self.val = val
        self.snap = snap


class Prog:
    ENGINES = ("tensor", "vector", "scalar", "gpsimd", "sync")

    def __init__(self, nc, es):
        self.nc = nc
        self.stream = {e: [] for e in self.ENGINES}
        self.count = {e: 0 for e in self.ENGINES}
        self.dma_count = {e: 0 for e in self.ENGINES}
        self.clock = {e: {} for e in self.ENGINES}
        self.sem = {}
        self.dsem = {}
        for e in self.ENGINES:
            self.sem[e] = es.enter_context(nc.semaphore(f"sp_{e}"))
        for e in ("gpsimd", "sync", "scalar", "vector"):
            self.dsem[e] = es.enter_context(nc.semaphore(f"sd_{e}"))
        self.n_instr = 0
        self.n_waits = 0

    def _need(self, eng, deps):
        need = {}
        for t in deps:
            if t is None:
                continue
            for k, v in t.snap.items():
                if need.get(k, -1) < v:
                    need[k] = v
        ck = self.clock[eng]
        out = {k: v for k, v in need.items() if ck.get(k, 0) < v}
        return out

    def _wait(self, eng, key, val):
        kind, src = key
        if kind == "e":
            sem, target = self.sem[src], val
        else:
            sem, target = self.dsem[src], 16 * val
        self.stream[eng].append(lambda E, s=sem, t=target: E.wait_ge(s, t))
        self.n_waits += 1
        ck = self.clock[eng]
        if ck.get(key, 0) < val:
            ck[key] = val

    def emit(self, eng, fn, reads=(), writes=(), deps=(), sync=True, dma=False):
        all_deps = []
        for b in reads:
            if b.last_w is not None:
                all_deps.append(b.last_w)
        for b in writes:
            if b.last_w is not None:
                all_deps.append(b.last_w)
            all_deps.extend(b.readers)
        all_deps.extend(deps)
        need = self._need(eng, all_deps)
        for k in sorted(need):
            self._wait(eng, k, need[k])
        ck = self.clock[eng]
        for t in all_deps:
            for k, v in t.snap.items():
                if ck.get(k, 0) < v:
                    ck[k] = v

        tok = None
        if dma:
            self.dma_count[eng] += 1
            key = ("d", eng)
            val = self.dma_count[eng]
            sem = self.dsem[eng]

            def op(E, f=fn, s=sem):
                ins = f(E)
                ins.then_inc(s, 16)
            self.stream[eng].append(op)
        elif sync:
            self.count[eng] += 1
            key = ("e", eng)
            val = self.count[eng]
            sem = self.sem[eng]

            def op(E, f=fn, s=sem):
                ins = f(E)
                ins.then_inc(s, 1)
            self.stream[eng].append(op)
        else:
            self.stream[eng].append(fn)
        self.n_instr += 1
        if dma or sync:
            snap = dict(ck)
            snap[key] = val
            tok = Tok(key, val, snap)
            for b in reads:
                b.readers.append(tok)
            for b in writes:
                b.last_w = tok
                b.readers = []
        else:
            pass
        return tok

    def fence(self, bufs):
        out = []
        for b in bufs:
            if b.last_w is not None:
                out.append(b.last_w)
            out.extend(b.readers)
        return out

    def apply_fence(self, deps, engines=None):
        for eng in (engines or self.ENGINES):
            need = self._need(eng, deps)
            for k in sorted(need):
                self._wait(eng, k, need[k])
            ck = self.clock[eng]
            for t in deps:
                if t is None:
                    continue
                for k, v in t.snap.items():
                    if ck.get(k, 0) < v:
                        ck[k] = v

    def global_barrier(self):
        targets = {}
        for e in self.ENGINES:
            if self.count[e]:
                targets[("e", e)] = self.count[e]
        for e in ("gpsimd", "sync", "scalar", "vector"):
            if self.dma_count[e]:
                targets[("d", e)] = self.dma_count[e]
        for eng in self.ENGINES:
            ck = self.clock[eng]
            for k in sorted(targets):
                v = targets[k]
                if k == ("e", eng):
                    continue
                if ck.get(k, 0) < v:
                    self._wait(eng, k, v)

    def finalize_into_block(self, block):
        for eng_name in self.ENGINES:
            ops = self.stream[eng_name]
            if not ops:
                continue

            def body(E, ops=ops):
                for f in ops:
                    f(E)
            getattr(block, eng_name)(body)


def ceil_div(a, b):
    return (a + b - 1) // b


def _f(v):
    return np.float32(v) if isinstance(v, float) else v



def host_consts():
    c = {}
    c["ident"] = np.eye(128, dtype=np.float32)
    R = np.zeros((128, 128), dtype=np.float32)
    for e in range(0, 128, 2):
        R[e, e + 1] = -1.0
        R[e + 1, e] = 1.0
    c["rotT"] = R.T.copy()
    pos = np.arange(T, dtype=np.float64)[:, None]
    inv = 1.0 / (10000.0 ** (np.arange(0, D, 2, dtype=np.float64) / D))
    fr = pos * inv
    cos = np.repeat(np.cos(fr), 2, axis=-1).astype(np.float32)
    sin = np.repeat(np.sin(fr), 2, axis=-1).astype(np.float32)
    c["cos_d"] = np.ascontiguousarray(cos.T)
    c["sin_d"] = np.ascontiguousarray(sin.T)
    m = np.zeros((4, 128, 512), dtype=np.float32)
    for p in range(4):
        for b in range(4):
            if b > p:
                m[p, :, b * 128:(b + 1) * 128] = NEG
            elif b == p:
                blk = np.triu(np.full((128, 128), NEG, np.float32), k=1)
                m[p, :, b * 128:(b + 1) * 128] = blk
    c["mask4"] = m
    return c



class CoreProgram:
    def __init__(self, debug_taps=()):
        self.nc = bass.Bass("TRN2", target_bir_lowering=False)
        self.es = contextlib.ExitStack()
        self.prog = Prog(self.nc, self.es)
        self.inputs = {}
        self.dram = {}
        self.debug_taps = set(debug_taps)
        self.sbytes = 0
        self.scope_bufs = {}

    def din(self, name, shape, dtype=F32):
        t = self.nc.dram_tensor(name, list(shape), dtype, kind="ExternalInput")
        self.dram[name] = t.ap()
        return self.dram[name], Buf("d_" + name)

    def dout(self, name, shape, dtype=F32):
        t = self.nc.dram_tensor(name, list(shape), dtype, kind="ExternalOutput")
        self.dram[name] = t.ap()
        return self.dram[name], Buf("d_" + name)

    def dscratch(self, name, shape, dtype=F32):
        t = self.nc.dram_tensor(name, list(shape), dtype, kind="Internal")
        self.dram[name] = t.ap()
        return self.dram[name], Buf("d_" + name)

    def sb(self, name, shape, dtype=F32, cm=None, side=None):
        if side is None:
            side = "left" if cm is None else "right"
        h = (cm or self.es).enter_context(
            self.nc.sbuf_tensor(name, list(shape), dtype, side=side))
        itm = 4 if dtype == F32 else 2
        self.sbytes += int(np.prod(shape[1:])) * itm
        buf = Buf(name)
        self.scope_bufs.setdefault(id(cm or self.es), []).append(buf)
        import os as _os
        if _os.environ.get("MK_DEBUG_ALLOC"):
            print(f"  sb {name} {shape} {dtype} side={side} bytes/part={int(np.prod(shape[1:])) * itm}")
        return h.ap(), buf

    def ps(self, name, shape, dtype=F32, cm=None):
        h = (cm or self.es).enter_context(self.nc.psum_tensor(name, list(shape), dtype))
        return h.ap(), Buf(name)

    def dma(self, q, out_ap, in_ap, reads=(), writes=(), deps=()):
        return self.prog.emit(q, lambda E: E.dma_start(out=out_ap, in_=in_ap),
                              reads=reads, writes=writes, deps=deps, dma=True)

    def mm_group(self, ps_ap_fn, ps_buf, terms, reads, deps=()):
        n = len(terms)
        tok = None
        for i, (lt, rh) in enumerate(terms):
            st = (i == 0)
            sp = (i == n - 1)
            fn = (lambda E, lt=lt, rh=rh, st=st, sp=sp:
                  E.matmul(ps_ap_fn, lt, rh, start=st, stop=sp))
            if i < n - 1:
                self.prog.emit("tensor", fn, reads=reads, writes=(ps_buf,),
                               deps=deps, sync=False)
            else:
                tok = self.prog.emit("tensor", fn, reads=reads, writes=(ps_buf,),
                                     deps=deps, sync=True)
        return tok

    def act(self, out_ap, out_buf, in_ap, in_buf, func=AF.Identity, bias=0.0,
            scale=1.0, accum=None, accum_buf=None, extra_reads=(), deps=()):
        reads = [in_buf] + list(extra_reads)
        writes = [out_buf] + ([accum_buf] if accum_buf is not None else [])
        bias = _f(bias); scale = _f(scale)
        if accum is None:
            fn = lambda E: E.activation(out_ap, in_ap, func, bias=bias, scale=scale)
        else:
            fn = lambda E: E.activation(out_ap, in_ap, func, bias=bias, scale=scale,
                                        accum_out=accum)
        return self.prog.emit("scalar", fn, reads=reads, writes=writes, deps=deps)

    def scopy(self, out_ap, out_buf, in_ap, in_buf, deps=()):
        return self.prog.emit("scalar", lambda E: E.copy(out_ap, in_ap),
                              reads=(in_buf,), writes=(out_buf,), deps=deps)

    def tt(self, out_ap, out_buf, a_ap, a_buf, b_ap, b_buf, op, deps=()):
        return self.prog.emit("vector",
                              lambda E: E.tensor_tensor(out_ap, a_ap, b_ap, op),
                              reads=(a_buf, b_buf), writes=(out_buf,), deps=deps)

    def ts(self, out_ap, out_buf, in_ap, in_buf, s1, s2, op0, op1=None,
           s_bufs=(), deps=()):
        s1 = float(s1) if isinstance(s1, (int, np.floating)) and not hasattr(s1, "ap") else s1
        if op1 is None:
            fn = lambda E: E.tensor_scalar(out_ap, in_ap, s1, None, op0)
        else:
            fn = lambda E: E.tensor_scalar(out_ap, in_ap, s1, s2, op0, op1)
        return self.prog.emit("vector", fn, reads=(in_buf,) + tuple(s_bufs),
                              writes=(out_buf,), deps=deps)

    def stt(self, out_ap, out_buf, in0_ap, in0_buf, scalar_ap, scalar_buf,
            in1_ap, in1_buf, op0, op1, deps=()):
        fn = lambda E: E.scalar_tensor_tensor(out_ap, in0_ap, scalar_ap, in1_ap, op0, op1)
        reads = tuple(b for b in (in0_buf, scalar_buf, in1_buf) if b is not None)
        return self.prog.emit("vector", fn, reads=reads, writes=(out_buf,), deps=deps)

    def vred(self, out_ap, out_buf, in_ap, in_buf, op, deps=()):
        fn = lambda E: E.tensor_reduce(out_ap, in_ap, AX.X, op)
        return self.prog.emit("vector", fn, reads=(in_buf,), writes=(out_buf,), deps=deps)

    def vcopy(self, out_ap, out_buf, in_ap, in_buf, deps=()):
        return self.prog.emit("vector", lambda E: E.tensor_copy(out_ap, in_ap),
                              reads=(in_buf,), writes=(out_buf,), deps=deps)

    def recip(self, out_ap, out_buf, in_ap, in_buf, deps=()):
        return self.prog.emit("vector", lambda E: E.reciprocal(out_ap, in_ap),
                              reads=(in_buf,), writes=(out_buf,), deps=deps)

    def transpose(self, ps_ap, ps_buf, in_ap, in_buf, ident_ap, ident_buf, deps=()):
        fn = lambda E: E.transpose(ps_ap, in_ap, ident_ap)
        return self.prog.emit("tensor", fn, reads=(in_buf, ident_buf),
                              writes=(ps_buf,), deps=deps)

    def load_const_tile(self, name, arr_shape, dtype, q="gpsimd"):
        d_ap, d_buf = self.din(name, arr_shape, F32)
        sb_ap, sb_buf = self.sb("c_" + name, arr_shape, dtype)
        self.dma(q, sb_ap[...], d_ap[...], writes=(sb_buf,))
        return sb_ap, sb_buf

    def ln_pass(self, x_src_ap, x_src_buf, xn_dmaj, xn_dmaj_buf, ident, ident_b,
                pstr, pstr_b, es_ph, tag):
        P = self.prog
        xt, xt_b = [], []
        xc, xc_b = [], []
        xnb, xnb_b = [], []
        for k in range(2):
            a, b = self.sb(f"{tag}_xt{k}", [128, 1024], F32, es_ph); xt.append(a); xt_b.append(b)
            a, b = self.sb(f"{tag}_xc{k}", [128, 1024], F32, es_ph); xc.append(a); xc_b.append(b)
            a, b = self.sb(f"{tag}_xnb{k}", [128, 1024], BF16, es_ph); xnb.append(a); xnb_b.append(b)
        colsL, colsL_b = [], []
        for k in range(2):
            a, b = self.sb(f"{tag}_lncols{k}", [128, 8], F32, es_ph)
            colsL.append(a); colsL_b.append(b)
        for i in range(NT):
            p = i % 2
            cols, cols_b = colsL[p], colsL_b[p]
            self.dma("sync", xt[p][...], x_src_ap[i * 128:(i + 1) * 128, :],
                     reads=(x_src_buf,), writes=(xt_b[p],))
            c0 = 0
            s_ap = cols[:, c0:c0 + 1]
            self.vred(s_ap, cols_b, xt[p][...], xt_b[p], ALU.add)
            nm_ap = cols[:, c0 + 1:c0 + 2]
            self.ts(nm_ap, cols_b, s_ap, cols_b, -1.0 / D, None, ALU.mult)
            self.ts(xc[p][...], xc_b[p], xt[p][...], xt_b[p], nm_ap, None, ALU.add,
                    s_bufs=(cols_b,))
            v_ap = cols[:, c0 + 2:c0 + 3]
            self.act(xt[p][...], xt_b[p], xc[p][...], xc_b[p], AF.Square,
                     accum=v_ap, accum_buf=cols_b)
            ln_ap = cols[:, c0 + 3:c0 + 4]
            self.act(ln_ap, cols_b, v_ap, cols_b, AF.Ln, bias=1.0e-5, scale=1.0 / D)
            self.act(ln_ap, cols_b, ln_ap, cols_b, AF.Exp, scale=-0.5)
            self.ts(xnb[p][...], xnb_b[p], xc[p][...], xc_b[p], ln_ap, None, ALU.mult,
                    s_bufs=(cols_b,))
            for dc in range(DC):
                pb = dc % 2
                self.transpose(pstr[pb][...], pstr_b[pb],
                               xnb[p][:, dc * 128:(dc + 1) * 128], xnb_b[p],
                               ident, ident_b)
                self.vcopy(xn_dmaj[:, dc, i * 128:(i + 1) * 128], xn_dmaj_buf,
                           pstr[pb][...], pstr_b[pb])

    def load_tile(self, W, key, shape, dtype, cm, tag):
        d_ap, d_buf = W[key]
        s_ap, s_buf = self.sb(tag, shape, dtype, cm)
        q = "sync" if dtype == F32 else "gpsimd"
        self.dma(q, s_ap[...], d_ap[...], reads=(d_buf,), writes=(s_buf,))
        return s_ap, s_buf

    def close_scope(self, es):
        bufs = self.scope_bufs.pop(id(es), [])
        deps = self.prog.fence(bufs)
        es.close()
        self.prog.apply_fence(deps)

    def build_layer0(self, W, x_src_ap, x_src_buf, x_dst_ap, x_dst_buf, consts,
                     debug=False):
        P = self.prog
        ident, ident_b = consts["ident"]
        rotT, rotT_b = consts["rotT"]
        psf = consts["psf"]
        pst = consts["pst"]

        if debug:
            _, self._sdump_buf = self.dout("l0_s_dump", [T, D], F32)
            g_scr_is_out = True
        else:
            g_scr_is_out = False
        esL = contextlib.ExitStack()
        with esL:
            if g_scr_is_out:
                g_scr, g_scr_b = self.dout("l0_g_dump", [T, D], F32)
            else:
                g_scr, g_scr_b = self.dscratch("l0_g_scr", [T, D], F32)
            q_scr, q_scr_b = self.dscratch("l0_q_scr", [DC, 128, T], BF16)
            k_scr, k_scr_b = self.dscratch("l0_k_scr", [DC, 128, T], BF16)

            es_xn = contextlib.ExitStack()
            with es_xn:
                xn, xn_b = self.sb("l0_xn", [128, DC, T], BF16, es_xn)
                es_ln = contextlib.ExitStack()
                with es_ln:
                    self.ln_pass(x_src_ap, x_src_buf, xn, xn_b, ident, ident_b,
                                 [p[0] for p in pst], [p[1] for p in pst], es_ln, "l0ln")
                self.close_scope(es_ln)

                es0 = contextlib.ExitStack()
                es0.__enter__()
                gates, gates_b = self.sb("l0_gates", [128, NT, 8], F32, es0)
                gates_dm, gates_dm_b = self.sb("l0_gates_dm", [8, T], BF16, es0)
                rw_ap, rw_b = self.load_tile(W, "l0_rw_d", [128, 8, 8], BF16, es0, "l0rwt")
                rw = rw_ap
                rbt_ap, rbt_b = self.load_tile(W, "l0_rbt_d", [128, 8], F32, es0, "l0rbtt")
                rbt = rbt_ap
                gt_tmp, gt_tmp_b = self.sb("l0_gt", [128, 4, 8], F32, es0)
                gt_bf, gt_bf_b = self.sb("l0_gtbf", [128, 8], BF16, es0)
                gcol, gcol_b = self.sb("l0_gcol", [128, 8], F32, es0)
                for i in range(NT):
                    pb = psf[i % 2]
                    terms = [(xn[:, dc, i * 128:(i + 1) * 128], rw[:, dc, :])
                             for dc in range(DC)]
                    self.mm_group(pb[0][:, 0:8], pb[1], terms, reads=(xn_b, rw_b))
                    t0 = gt_tmp[:, (i % 2) * 2, :]
                    t1 = gt_tmp[:, (i % 2) * 2 + 1, :]
                    self.tt(t0, gt_tmp_b, pb[0][:, 0:8], pb[1], rbt[...], rbt_b, ALU.add)
                    c0 = (i % 2) * 4
                    m_ap = gcol[:, c0:c0 + 1]
                    self.vred(m_ap, gcol_b, t0, gt_tmp_b, ALU.max)
                    nm = gcol[:, c0 + 1:c0 + 2]
                    self.ts(nm, gcol_b, m_ap, gcol_b, -1.0, None, ALU.mult)
                    l_ap = gcol[:, c0 + 2:c0 + 3]
                    self.act(t1, gt_tmp_b, t0, gt_tmp_b, AF.Exp, bias=nm, scale=1.0,
                             accum=l_ap, accum_buf=gcol_b, extra_reads=(gcol_b,))
                    r_ap = gcol[:, c0 + 3:c0 + 4]
                    self.recip(r_ap, gcol_b, l_ap, gcol_b)
                    self.ts(t1, gt_tmp_b, t1, gt_tmp_b, r_ap, None, ALU.mult,
                            s_bufs=(gcol_b,))
                    self.vred(m_ap, gcol_b, t1, gt_tmp_b, ALU.max)
                    self.ts(gates[:, i, :], gates_b, t1, gt_tmp_b, m_ap, None, ALU.is_ge,
                            s_bufs=(gcol_b,))
                    self.tt(gates[:, i, :], gates_b, gates[:, i, :], gates_b, t1,
                            gt_tmp_b, ALU.mult)
                    self.vcopy(gt_bf[...], gt_bf_b, gates[:, i, :], gates_b)
                    self.transpose(pst[0][0][0:8, :], pst[0][1], gt_bf[...], gt_bf_b,
                                   ident, ident_b)
                    self.vcopy(gates_dm[:, i * 128:(i + 1) * 128], gates_dm_b,
                               pst[0][0][0:8, :], pst[0][1])

                b2sb, b2sb_b = self.load_tile(W, "l0_b2_d", [8, 1024], BF16, es0, "l0b2t")
                b1sb, b1sb_b = self.load_tile(W, "l0_b1_d", [128, 8, 32], F32, es0, "l0b1t")
                g_sb, g_sb_b = self.sb("l0_gq", [128, 16, 1024], F32, es0)
                w1u, w1u_b, w2u, w2u_b, h_sb, h_sb_b = [], [], [], [], [], []
                for k in range(2):
                    a, b = self.sb(f"l0_w1u{k}", [128, 8, 512], BF16, es0)
                    w1u.append(a); w1u_b.append(b)
                    a, b = self.sb(f"l0_w2u{k}", [128, 4, 1024], BF16, es0)
                    w2u.append(a); w2u_b.append(b)
                    a, b = self.sb(f"l0_h{k}", [128, 4, 512], BF16, es0)
                    h_sb.append(a); h_sb_b.append(b)
                gsil, gsil_b = self.sb("l0_gsil", [128, 2, 1024], F32, es0)
                W1d, W1d_b = W["l0_W1_dram"]
                W2d, W2d_b = W["l0_W2_dram"]

                for quarter in range(2):
                    tq0 = quarter * 16
                    for ti in range(16):
                        it = tq0 + ti
                        pb = psf[4 + (ti % 2)]
                        for dh in range(2):
                            self.mm_group(
                                pb[0][...], pb[1],
                                [(gates_dm[:, it * 128:(it + 1) * 128],
                                  b2sb[:, dh * 512:(dh + 1) * 512])],
                                reads=(gates_dm_b, b2sb_b))
                            self.act(g_sb[:, ti, dh * 512:(dh + 1) * 512], g_sb_b,
                                     pb[0][...], pb[1], AF.Identity)
                    for e in range(8):
                        for fb in range(8):
                            u = e * 8 + fb
                            pw = u % 2
                            self.dma("gpsimd", w1u[pw][...],
                                     W1d[e, :, fb * 512:(fb + 1) * 512]
                                     .rearrange("(dc p) f -> p dc f", p=128),
                                     reads=(W1d_b,), writes=(w1u_b[pw],))
                            self.dma("gpsimd", w2u[pw][...],
                                     W2d[e, fb * 512:(fb + 1) * 512, :]
                                     .rearrange("(fc p) d -> p fc d", p=128),
                                     reads=(W2d_b,), writes=(w2u_b[pw],))
                            for tcp in range(2):
                                for tc in (2 * tcp, 2 * tcp + 1):
                                    t0c = tq0 * 128 + tc * 512
                                    ph = tc % 2
                                    for ft in range(4):
                                        pb = psf[ft % 4]
                                        terms = [(w1u[pw][:, dc, ft * 128:(ft + 1) * 128],
                                                  xn[:, dc, t0c:t0c + 512])
                                                 for dc in range(DC)]
                                        self.mm_group(pb[0][...], pb[1], terms,
                                                      reads=(w1u_b[pw], xn_b))
                                        self.act(h_sb[ph][:, ft, :], h_sb_b[ph],
                                                 pb[0][...], pb[1], AF.Relu,
                                                 bias=b1sb[:, e, fb * 4 + ft:fb * 4 + ft + 1],
                                                 extra_reads=(b1sb_b,))
                                for tc in (2 * tcp, 2 * tcp + 1):
                                    t0c = tq0 * 128 + tc * 512
                                    ph = tc % 2
                                    for ti in range(4):
                                        itt = (t0c // 128) + ti
                                        gq_ti = itt - tq0
                                        for dh in range(2):
                                            pb = psf[4 + ((ti * 2 + dh) % 2)]
                                            terms = [(h_sb[ph][:, fc, ti * 128:(ti + 1) * 128],
                                                      w2u[pw][:, fc, dh * 512:(dh + 1) * 512])
                                                     for fc in range(4)]
                                            self.mm_group(pb[0][...], pb[1], terms,
                                                          reads=(h_sb_b[ph], w2u_b[pw]))
                                            self.stt(g_sb[:, gq_ti, dh * 512:(dh + 1) * 512],
                                                     g_sb_b, pb[0][...], pb[1],
                                                     gates[:, itt, e:e + 1], gates_b,
                                                     g_sb[:, gq_ti, dh * 512:(dh + 1) * 512],
                                                     g_sb_b, ALU.mult, ALU.add)
                    for ti in range(16):
                        it = tq0 + ti
                        self.act(gsil[:, ti % 2, :], gsil_b, g_sb[:, ti, :], g_sb_b,
                                 AF.Silu)
                        self.dma("sync", g_scr[it * 128:(it + 1) * 128, :],
                                 gsil[:, ti % 2, :], reads=(gsil_b,), writes=(g_scr_b,))

                self.close_scope(es0)

                cosd, cosd_b = W["cos_d"]
                sind, sind_b = W["sin_d"]
                esB = contextlib.ExitStack()
                if True:
                    wT, wT_b = [], []
                    for k in range(2):
                        a, b = self.sb(f"l0_wT{k}", [128, DC, D], BF16, esB)
                        wT.append(a); wT_b.append(b)
                    csc, csc_b = self.sb("l0_csc", [128, DC, 256], F32, esB)
                    css, css_b = self.sb("l0_css", [128, DC, 256], F32, esB)
                    tA, tA_b = self.sb("l0_tA", [128, DC, 256], F32, esB)
                    qbf, qbf_b = self.sb("l0_qbf", [128, 2, DC, 256], BF16, esB)
                    tB, tB_b = self.sb("l0_tB", [128, 2, 256], F32, esB)
                    self.dma("gpsimd", wT[0][...],
                             W["l0_wqT"][0].rearrange("(dc p) e -> p dc e", p=128),
                             reads=(W["l0_wqT"][1],), writes=(wT_b[0],))
                    self.dma("gpsimd", wT[1][...],
                             W["l0_wkT"][0].rearrange("(dc p) e -> p dc e", p=128),
                             reads=(W["l0_wkT"][1],), writes=(wT_b[1],))
                    for tc in range(T // 256):
                        t0c = tc * 256
                        for dc in range(DC):
                            self.dma("scalar", csc[:, dc, :],
                                     cosd[dc * 128:(dc + 1) * 128, t0c:t0c + 256],
                                     reads=(cosd_b,), writes=(csc_b,))
                            self.dma("scalar", css[:, dc, :],
                                     sind[dc * 128:(dc + 1) * 128, t0c:t0c + 256],
                                     reads=(sind_b,), writes=(css_b,))
                        for wi, dst_scr, dst_scr_b in ((0, q_scr, q_scr_b),
                                                       (1, k_scr, k_scr_b)):
                            for dc in range(DC):
                                pb = psf[dc % 4]
                                terms = [(wT[wi][:, dci, dc * 128:(dc + 1) * 128],
                                          xn[:, dci, t0c:t0c + 256]) for dci in range(DC)]
                                self.mm_group(pb[0][:, 0:256], pb[1], terms,
                                              reads=(wT_b[wi], xn_b))
                                self.act(tA[:, dc, :], tA_b, pb[0][:, 0:256], pb[1],
                                         AF.Identity)
                                self.vcopy(qbf[:, wi, dc, :], qbf_b, tA[:, dc, :],
                                           tA_b)
                            for dc in range(DC):
                                pr = psf[4 + dc % 2]
                                self.mm_group(pr[0][:, 0:256], pr[1],
                                              [(rotT, qbf[:, wi, dc, :])],
                                              reads=(rotT_b, qbf_b))
                                t1 = tB[:, 0, :]
                                t2 = tB[:, 1, :]
                                self.tt(t1, tB_b, tA[:, dc, :], tA_b,
                                        csc[:, dc, :], csc_b, ALU.mult)
                                self.tt(t2, tB_b, pr[0][:, 0:256], pr[1],
                                        css[:, dc, :], css_b, ALU.mult)
                                qo = qbf[:, wi, dc, :]
                                self.tt(qo, qbf_b, t1, tB_b, t2, tB_b, ALU.add)
                                self.dma("sync", dst_scr[dc, :, t0c:t0c + 256], qo,
                                         reads=(qbf_b,), writes=(dst_scr_b,))
                    v_tm, v_tm_b = self.sb("l0_v", [128, NT, D], BF16, esL, side="left")
                    wd, wd_b = W["l0_wvT"]
                    self.dma("gpsimd", wT[0][...],
                             wd.rearrange("(dc p) e -> p dc e", p=128),
                             reads=(wd_b,), writes=(wT_b[0],))
                    for it in range(NT):
                        for eh in range(2):
                            pb = psf[4 + eh]
                            terms = [(xn[:, dci, it * 128:(it + 1) * 128],
                                      wT[0][:, dci, eh * 512:(eh + 1) * 512])
                                     for dci in range(DC)]
                            self.mm_group(pb[0][...], pb[1], terms, reads=(xn_b, wT_b[0]))
                            self.act(v_tm[:, it, eh * 512:(eh + 1) * 512], v_tm_b,
                                     pb[0][...], pb[1], AF.Identity)
                    pass
                self.close_scope(esB)
            self.close_scope(es_xn)

            esC = contextlib.ExitStack()
            if True:
                kd, kd_b = self.sb("l0_k", [128, DC, T], BF16, esC)
                self.dma("sync", kd[...], k_scr.rearrange("dc p t -> p dc t"),
                         reads=(k_scr_b,), writes=(kd_b,))
                mask, mask_b = self.sb("l0_mask", [128, 4, 512], F32, esC)
                self.dma("sync", mask[...], W["mask4"][0].rearrange("v p f -> p v f"),
                         reads=(W["mask4"][1],), writes=(mask_b,))
                qt, qt_b = [], []
                for k in range(2):
                    a, b = self.sb(f"l0_qt{k}", [128, DC, 128], BF16, esC)
                    qt.append(a); qt_b.append(b)
                PsbL, PsbL_b = [], []
                for k in range(2):
                    a, b = self.sb(f"l0_P{k}", [128, T], BF16, esC)
                    PsbL.append(a); PsbL_b.append(b)
                PtL, PtL_b = [], []
                for k in range(2):
                    a, b = self.sb(f"l0_Pt{k}", [128, 8, 128], BF16, esC)
                    PtL.append(a); PtL_b.append(b)
                yaccL, yaccL_b = [], []
                for k in range(2):
                    a, b = self.sb(f"l0_yacc{k}", [128, D], F32, esC)
                    yaccL.append(a); yaccL_b.append(b)
                xg, xg_b = [], []
                for k in range(4):
                    a, b = self.sb(f"l0_xg{k}", [128, D], F32, esC)
                    xg.append(a); xg_b.append(b)
                xo, xo_b = [], []
                for k in range(2):
                    a, b = self.sb(f"l0_xo{k}", [128, D], F32, esC)
                    xo.append(a); xo_b.append(b)
                mlsL, mlsL_b = [], []
                for k in range(2):
                    a, b = self.sb(f"l0_mls{k}", [128, 24], F32, esC)
                    mlsL.append(a); mlsL_b.append(b)

                def softmax0(it):
                    p = it % 2
                    cols, cols_b = mlsL[p], mlsL_b[p]
                    Psb, Psb_b = PsbL[p], PsbL_b[p]
                    jext = (it + 1) * 128
                    n_ch = ceil_div(jext, 512)
                    m_run = cols[:, 0:1]
                    bias_c = cols[:, 3:4]
                    lfin = cols[:, 4:5]
                    rec = cols[:, 5:6]
                    tmp = cols[:, 6:7]
                    corr = cols[:, 7:8]
                    self.dma("sync", qt[p][...],
                             q_scr[:, :, it * 128:(it + 1) * 128]
                             .rearrange("dc p t -> p dc t"),
                             reads=(q_scr_b,), writes=(qt_b[p],))
                    self.dma("sync", xg[p][...], x_src_ap[it * 128:(it + 1) * 128, :],
                             reads=(x_src_buf,), writes=(xg_b[p],))
                    self.dma("sync", xg[2 + p][...], g_scr[it * 128:(it + 1) * 128, :],
                             reads=(g_scr_b,), writes=(xg_b[2 + p],))
                    n_grp = ceil_div(n_ch, 4)
                    for g0 in range(n_grp):
                        ch0, ch1 = g0 * 4, min(n_ch, g0 * 4 + 4)
                        for ch in range(ch0, ch1):
                            c0 = ch * 512
                            cw = min(512, jext - c0)
                            pb = psf[ch - ch0]
                            terms = [(qt[p][:, dc, :], kd[:, dc, c0:c0 + cw])
                                     for dc in range(DC)]
                            self.mm_group(pb[0][:, 0:cw], pb[1], terms,
                                          reads=(qt_b[p], kd_b))
                            if c0 <= it * 128 < c0 + 512:
                                self.tt(pb[0][:, 0:cw], pb[1], pb[0][:, 0:cw], pb[1],
                                        mask[:, it % 4, 0:cw], mask_b, ALU.add)
                            self.vred(cols[:, 8 + ch - ch0:9 + ch - ch0], cols_b,
                                      pb[0][:, 0:cw], pb[1], ALU.max)
                        ng = ch1 - ch0
                        if ng == 1:
                            m_g = cols[:, 8:9]
                        else:
                            m_g = cols[:, 2:3]
                            self.vred(m_g, cols_b, cols[:, 8:8 + ng], cols_b, ALU.max)
                        if g0 == 0:
                            self.vcopy(m_run, cols_b, m_g, cols_b)
                        else:
                            self.tt(tmp, cols_b, m_run, cols_b, m_g, cols_b, ALU.max)
                            self.tt(corr, cols_b, m_run, cols_b, tmp, cols_b,
                                    ALU.subtract)
                            self.act(corr, cols_b, corr, cols_b, AF.Exp, scale=1.0 / 32.0)
                            self.ts(Psb[:, 0:ch0 * 512], Psb_b, Psb[:, 0:ch0 * 512],
                                    Psb_b, corr, None, ALU.mult, s_bufs=(cols_b,))
                            self.ts(lfin, cols_b, lfin, cols_b, corr, None, ALU.mult,
                                    s_bufs=(cols_b,))
                            self.vcopy(m_run, cols_b, tmp, cols_b)
                        self.ts(bias_c, cols_b, m_run, cols_b, -1.0 / 32.0, None, ALU.mult)
                        for ch in range(ch0, ch1):
                            c0 = ch * 512
                            cw = min(512, jext - c0)
                            pb = psf[ch - ch0]
                            self.act(Psb[:, c0:c0 + cw], Psb_b, pb[0][:, 0:cw], pb[1],
                                     AF.Exp, bias=bias_c, scale=1.0 / 32.0,
                                     accum=cols[:, 16 + ch - ch0:17 + ch - ch0],
                                     accum_buf=cols_b, extra_reads=(cols_b,))
                        lg = cols[:, 1:2]
                        if ng == 1:
                            self.vcopy(lg, cols_b, cols[:, 16:17], cols_b)
                        else:
                            self.vred(lg, cols_b, cols[:, 16:16 + ng], cols_b, ALU.add)
                        if g0 == 0:
                            self.vcopy(lfin, cols_b, lg, cols_b)
                        else:
                            self.tt(lfin, cols_b, lfin, cols_b, lg, cols_b, ALU.add)
                    self.recip(rec, cols_b, lfin, cols_b)
                    self.ts(Psb[:, 0:jext], Psb_b, Psb[:, 0:jext], Psb_b, rec, None,
                            ALU.mult, s_bufs=(cols_b,))

                def pv0(it):
                    p = it % 2
                    Psb, Psb_b = PsbL[p], PsbL_b[p]
                    PtH, PtH_b = PtL[p], PtL_b[p]
                    jext = (it + 1) * 128
                    n_jt = jext // 128
                    for jg in range(0, n_jt, 8):
                        je = min(n_jt, jg + 8)
                        for jj in range(jg, je):
                            self.transpose(pst[jj % 2][0][...], pst[jj % 2][1],
                                           Psb[:, jj * 128:(jj + 1) * 128], Psb_b,
                                           ident, ident_b)
                            self.vcopy(PtH[:, jj - jg, :], PtH_b,
                                       pst[jj % 2][0][...], pst[jj % 2][1])
                        for jj in range(jg, je):
                            for eh in range(2):
                                pb = psf[4 + eh]
                                fn = (lambda E, lt=PtH[:, jj - jg, :],
                                      rh=v_tm[:, jj, eh * 512:(eh + 1) * 512],
                                      o=pb[0][...], st=(jj == 0), sp=(jj == n_jt - 1):
                                      E.matmul(o, lt, rh, start=st, stop=sp))
                                self.prog.emit("tensor", fn, reads=(PtH_b, v_tm_b),
                                               writes=(pb[1],), sync=(jj == n_jt - 1))
                    for eh in range(2):
                        pb = psf[4 + eh]
                        sl = slice(eh * 512, (eh + 1) * 512)
                        self.tt(xo[p][:, sl], xo_b[p], pb[0][...], pb[1],
                                xg[2 + p][:, sl], xg_b[2 + p], ALU.mult)
                        self.tt(xo[p][:, sl], xo_b[p], xo[p][:, sl], xo_b[p],
                                xg[p][:, sl], xg_b[p], ALU.add)
                    self.dma("sync", x_dst_ap[it * 128:(it + 1) * 128, :], xo[p][...],
                             reads=(xo_b[p],), writes=(x_dst_buf,))

                for it in range(NT):
                    softmax0(it)
                    if it >= 1:
                        pv0(it - 1)
                pv0(NT - 1)
                pass
            self.close_scope(esC)

    def setup_consts(self):
        es = self.es
        consts = {}
        c = host_consts()
        a, b = self.load_const_tile("c_ident", [128, 128], BF16)
        consts["ident"] = (a[...], b)
        self.inputs["c_ident"] = c["ident"]
        a, b = self.load_const_tile("c_rotT", [128, 128], BF16)
        consts["rotT"] = (a[...], b)
        self.inputs["c_rotT"] = c["rotT"]
        a, b = self.load_const_tile("c_identf", [128, 128], F32)
        consts["identf"] = (a[...], b)
        self.inputs["c_identf"] = c["ident"]
        psf = []
        for i in range(6):
            ap, bb = self.ps(f"psf{i}", [128, 512], F32)
            psf.append((ap, bb))
        pstl = []
        for i in range(2):
            ap, bb = self.ps(f"pst{i}", [128, 128], BF16)
            pstl.append((ap, bb))
        consts["psf"] = psf
        consts["pst"] = pstl
        return consts, c

    def final_drain(self):
        P = self.prog
        for q in ("sync", "gpsimd", "scalar", "vector"):
            n = P.dma_count[q]
            if n:
                P.stream["sync"].append(
                    lambda E, s=P.dsem[q], t=16 * n: E.wait_ge(s, t))

    def finalize(self):
        self.final_drain()
        blk = self.es.enter_context(self.nc.Block())
        self.prog.finalize_into_block(blk)
        self.es.close()
        return self.nc


    def ln_pass_dual(self, x_src_ap, x_src_buf, xn_dmaj, xn_dmaj_buf,
                     xn_scr, xn_scr_buf, xn2_scr, xn2_scr_buf,
                     gam_ap, gam_b, bet_ap, bet_b,
                     ident, ident_b, pst, es_ph, tag):
        xt, xt_b, xc, xc_b, xnb, xnb_b, x2b, x2b_b, x2f, x2f_b = [], [], [], [], [], [], [], [], [], []
        for k in range(2):
            a, b = self.sb(f"{tag}_xt{k}", [128, 1024], F32, es_ph); xt.append(a); xt_b.append(b)
            a, b = self.sb(f"{tag}_xc{k}", [128, 1024], F32, es_ph); xc.append(a); xc_b.append(b)
            a, b = self.sb(f"{tag}_xnb{k}", [128, 1024], BF16, es_ph); xnb.append(a); xnb_b.append(b)
            a, b = self.sb(f"{tag}_x2b{k}", [128, 1024], BF16, es_ph); x2b.append(a); x2b_b.append(b)
            a, b = self.sb(f"{tag}_x2f{k}", [128, 1024], F32, es_ph); x2f.append(a); x2f_b.append(b)
        xnf, xnf_b = [], []
        for k in range(2):
            a, b = self.sb(f"{tag}_xnf{k}", [128, 1024], F32, es_ph); xnf.append(a); xnf_b.append(b)
        x2st, x2st_b = [], []
        for k in range(2):
            a, b = self.sb(f"{tag}_x2st{k}", [128, DC, 128], BF16, es_ph); x2st.append(a); x2st_b.append(b)
        colsL, colsL_b = [], []
        for k in range(2):
            a, b = self.sb(f"{tag}_cols{k}", [128, 16], F32, es_ph)
            colsL.append(a); colsL_b.append(b)
        for i in range(NT):
            p = i % 2
            cols, cols_b = colsL[p], colsL_b[p]
            self.dma("sync", xt[p][...], x_src_ap[i * 128:(i + 1) * 128, :],
                     reads=(x_src_buf,), writes=(xt_b[p],))
            c0 = 0
            s_ap = cols[:, c0:c0 + 1]
            self.vred(s_ap, cols_b, xt[p][...], xt_b[p], ALU.add)
            nm_ap = cols[:, c0 + 1:c0 + 2]
            self.ts(nm_ap, cols_b, s_ap, cols_b, -1.0 / D, None, ALU.mult)
            self.ts(xc[p][...], xc_b[p], xt[p][...], xt_b[p], nm_ap, None, ALU.add,
                    s_bufs=(cols_b,))
            v_ap = cols[:, c0 + 2:c0 + 3]
            self.act(xt[p][...], xt_b[p], xc[p][...], xc_b[p], AF.Square,
                     accum=v_ap, accum_buf=cols_b)
            r1 = cols[:, c0 + 3:c0 + 4]
            self.act(r1, cols_b, v_ap, cols_b, AF.Ln, bias=1.0e-5, scale=1.0 / D)
            self.act(r1, cols_b, r1, cols_b, AF.Exp, scale=-0.5)
            w_ap = cols[:, c0 + 4:c0 + 5]
            self.ts(w_ap, cols_b, v_ap, cols_b, r1, None, ALU.mult, s_bufs=(cols_b,))
            self.ts(w_ap, cols_b, w_ap, cols_b, r1, 1.0 / D, ALU.mult, ALU.mult,
                    s_bufs=(cols_b,))
            r2 = cols[:, c0 + 5:c0 + 6]
            self.act(r2, cols_b, w_ap, cols_b, AF.Ln, bias=1.0e-5)
            self.act(r2, cols_b, r2, cols_b, AF.Exp, scale=-0.5)
            r12 = cols[:, c0 + 6:c0 + 7]
            self.ts(r12, cols_b, r2, cols_b, r1, None, ALU.mult, s_bufs=(cols_b,))
            self.ts(xnf[p][...], xnf_b[p], xc[p][...], xc_b[p], r1, None, ALU.mult,
                    s_bufs=(cols_b,))
            self.dma("sync", xn_scr[i * 128:(i + 1) * 128, :], xnf[p][...],
                     reads=(xnf_b[p],), writes=(xn_scr_buf,))
            self.vcopy(xnb[p][...], xnb_b[p], xnf[p][...], xnf_b[p])
            for dc in range(DC):
                pb = dc % 2
                self.transpose(pst[pb][0][...], pst[pb][1],
                               xnb[p][:, dc * 128:(dc + 1) * 128], xnb_b[p],
                               ident, ident_b)
                self.vcopy(xn_dmaj[:, dc, i * 128:(i + 1) * 128], xn_dmaj_buf,
                           pst[pb][0][...], pst[pb][1])
            self.ts(x2f[p][...], x2f_b[p], xc[p][...], xc_b[p], r12, None, ALU.mult,
                    s_bufs=(cols_b,))
            self.tt(x2f[p][...], x2f_b[p], x2f[p][...], x2f_b[p], gam_ap, gam_b, ALU.mult)
            self.tt(x2f[p][...], x2f_b[p], x2f[p][...], x2f_b[p], bet_ap, bet_b, ALU.add)
            self.vcopy(x2b[p][...], x2b_b[p], x2f[p][...], x2f_b[p])
            for dc in range(DC):
                pb = dc % 2
                self.transpose(pst[pb][0][...], pst[pb][1],
                               x2b[p][:, dc * 128:(dc + 1) * 128], x2b_b[p],
                               ident, ident_b)
                self.vcopy(x2st[p][:, dc, :], x2st_b[p], pst[pb][0][...], pst[pb][1])
            self.dma("sync", xn2_scr[:, :, i * 128:(i + 1) * 128]
                     .rearrange("dc p t -> p dc t"), x2st[p][...],
                     reads=(x2st_b[p],), writes=(xn2_scr_buf,))

    def build_layer1(self, W, x_src_ap, x_src_buf, x_dst_ap, x_dst_buf, consts,
                     debug=False):
        P = self.prog
        ident, ident_b = consts["ident"]
        psf = consts["psf"]
        pst = consts["pst"]

        if debug:
            _, self._sdump_buf1 = self.dout("l1_s_dump", [T, D], F32)
        esL = contextlib.ExitStack()
        with esL:
            if debug:
                g_scr, g_scr_b = self.dout("l1_g_dump", [T, D], F32)
            else:
                g_scr, g_scr_b = self.dscratch("l1_g_scr", [T, D], F32)
            xn_scr, xn_scr_b = self.dscratch("l1_xn_scr", [T, D], F32)
            xn2_scr, xn2_scr_b = self.dscratch("l1_xn2_scr", [DC, 128, T], BF16)
            q_scr, q_scr_b = self.dscratch("l1_q_scr", [DC, 128, T], BF16)
            k_scr, k_scr_b = self.dscratch("l1_k_scr", [DC, 128, T], BF16)

            es_xn = contextlib.ExitStack()
            with es_xn:
                xn, xn_b = self.sb("l1_xn", [128, DC, T], BF16, es_xn)
                es_ln = contextlib.ExitStack()
                with es_ln:
                    gam1, gam1_b = self.load_tile(W, "l1_gam_d", [128, 1024], F32, es_ln, "l1gamt")
                    bet1, bet1_b = self.load_tile(W, "l1_bet_d", [128, 1024], F32, es_ln, "l1bett")
                    self.ln_pass_dual(x_src_ap, x_src_buf, xn, xn_b,
                                      xn_scr, xn_scr_b, xn2_scr, xn2_scr_b,
                                      gam1[...], gam1_b, bet1[...], bet1_b,
                                      ident, ident_b, pst, es_ln, "l1ln")
                self.close_scope(es_ln)

                esF = contextlib.ExitStack()
                if True:
                    y_sb, y_sb_b = self.sb("l1_yq", [128, 16, 1024], F32, esF)
                    w1u, w1u_b, w2u, w2u_b, h_sb, h_sb_b = [], [], [], [], [], []
                    for k in range(2):
                        a, b = self.sb(f"l1_w1u{k}", [128, 8, 512], BF16, esF)
                        w1u.append(a); w1u_b.append(b)
                        a, b = self.sb(f"l1_w2u{k}", [128, 4, 1024], BF16, esF)
                        w2u.append(a); w2u_b.append(b)
                        a, b = self.sb(f"l1_h{k}", [128, 4, 512], BF16, esF)
                        h_sb.append(a); h_sb_b.append(b)
                    hrel, hrel_b = self.sb("l1_hrel", [128, 512], F32, esF)
                    mem_sb, mem_sb_b = self.sb("l1_mem", [128, 2, 1024], F32, esF)
                    spk_sb, spk_sb_b = self.sb("l1_spk", [128, 2, 1024], F32, esF)
                    sum_sb, sum_sb_b = self.sb("l1_ssum", [128, 2, 1024], F32, esF)
                    gsil, gsil_b = self.sb("l1_gsil", [128, 2, 1024], F32, esF)
                    WfcT, WfcT_b = W["l1_WfcT_dram"]
                    WprT, WprT_b = W["l1_WprT_dram"]
                    NFB = 8
                    for quarter in range(2):
                        tq0 = quarter * 16
                        for fb in range(NFB):
                            pw = fb % 2
                            self.dma("gpsimd", w1u[pw][...],
                                     WfcT[:, fb * 512:(fb + 1) * 512]
                                     .rearrange("(dc p) f -> p dc f", p=128),
                                     reads=(WfcT_b,), writes=(w1u_b[pw],))
                            self.dma("gpsimd", w2u[pw][...],
                                     WprT[fb * 512:(fb + 1) * 512, :]
                                     .rearrange("(fc p) d -> p fc d", p=128),
                                     reads=(WprT_b,), writes=(w2u_b[pw],))
                            for tcp in range(2):
                                for tc in (2 * tcp, 2 * tcp + 1):
                                    t0c = tq0 * 128 + tc * 512
                                    ph = tc % 2
                                    for ft in range(4):
                                        pb = psf[ft % 4]
                                        terms = [(w1u[pw][:, dc, ft * 128:(ft + 1) * 128],
                                                  xn[:, dc, t0c:t0c + 512])
                                                 for dc in range(DC)]
                                        self.mm_group(pb[0][...], pb[1], terms,
                                                      reads=(w1u_b[pw], xn_b))
                                        self.ts(hrel[...], hrel_b, pb[0][...], pb[1],
                                                0.0, None, ALU.max)
                                        self.act(h_sb[ph][:, ft, :], h_sb_b[ph],
                                                 hrel[...], hrel_b, AF.Square)
                                for tc in (2 * tcp, 2 * tcp + 1):
                                    t0c = tq0 * 128 + tc * 512
                                    ph = tc % 2
                                    for ti in range(4):
                                        itt = (t0c // 128) + ti
                                        yq_ti = itt - tq0
                                        for dh in range(2):
                                            pb = psf[4 + ((ti * 2 + dh) % 2)]
                                            terms = [(h_sb[ph][:, fc, ti * 128:(ti + 1) * 128],
                                                      w2u[pw][:, fc, dh * 512:(dh + 1) * 512])
                                                     for fc in range(4)]
                                            self.mm_group(pb[0][...], pb[1], terms,
                                                          reads=(h_sb_b[ph], w2u_b[pw]))
                                            dst = y_sb[:, yq_ti, dh * 512:(dh + 1) * 512]
                                            if fb == 0:
                                                self.act(dst, y_sb_b, pb[0][...], pb[1],
                                                         AF.Identity)
                                            else:
                                                self.tt(dst, y_sb_b, dst, y_sb_b,
                                                        pb[0][...], pb[1], ALU.add)
                        for ti in range(16):
                            it = tq0 + ti
                            y_ap = y_sb[:, ti, :]
                            mem = mem_sb[:, ti % 2, :]
                            spk = spk_sb[:, ti % 2, :]
                            ssum = sum_sb[:, ti % 2, :]
                            self.ts(spk, spk_sb_b, y_ap, y_sb_b, 1.0, None, ALU.is_gt)
                            self.vcopy(mem, mem_sb_b, y_ap, y_sb_b)
                            self.vcopy(ssum, sum_sb_b, spk, spk_sb_b)
                            for step in range(1, 5):
                                self.ts(mem, mem_sb_b, mem, mem_sb_b, 0.9, None, ALU.mult)
                                self.tt(mem, mem_sb_b, mem, mem_sb_b, y_ap, y_sb_b, ALU.add)
                                self.tt(mem, mem_sb_b, mem, mem_sb_b, spk, spk_sb_b,
                                        ALU.subtract)
                                self.ts(spk, spk_sb_b, mem, mem_sb_b, 1.0, None, ALU.is_gt)
                                self.tt(ssum, sum_sb_b, ssum, sum_sb_b, spk, spk_sb_b,
                                        ALU.add)
                            self.act(gsil[:, ti % 2, :], gsil_b, ssum, sum_sb_b, AF.Silu,
                                     scale=0.2)
                            self.dma("sync", g_scr[it * 128:(it + 1) * 128, :],
                                     gsil[:, ti % 2, :], reads=(gsil_b,), writes=(g_scr_b,))
                self.close_scope(esF)
            self.close_scope(es_xn)

            esQ = contextlib.ExitStack()
            if True:
                v_tm, v_tm_b = self.sb("l1_v", [128, NT, D], BF16, esL, side="left")
                wq_sb, wq_sb_b = self.sb("l1_wq", [128, DC, D], BF16, esQ)
                wk_sb, wk_sb_b = self.sb("l1_wk", [128, DC, D], BF16, esQ)
                wv_sb, wv_sb_b = self.sb("l1_wv", [128, DC, D], BF16, esQ)
                self.dma("gpsimd", wq_sb[...],
                         W["l1_WqT_dram"][0].rearrange("(dc p) e -> p dc e", p=128),
                         reads=(W["l1_WqT_dram"][1],), writes=(wq_sb_b,))
                self.dma("gpsimd", wk_sb[...],
                         W["l1_WkT_dram"][0].rearrange("(dc p) e -> p dc e", p=128),
                         reads=(W["l1_WkT_dram"][1],), writes=(wk_sb_b,))
                self.dma("gpsimd", wv_sb[...],
                         W["l1_WvT_dram"][0].rearrange("(dc p) e -> p dc e", p=128),
                         reads=(W["l1_WvT_dram"][1],), writes=(wv_sb_b,))
                x2c, x2c_b = [], []
                for k in range(2):
                    a, b = self.sb(f"l1_x2c{k}", [128, DC, 512], BF16, esQ)
                    x2c.append(a); x2c_b.append(b)
                qkv_bf, qkv_bf_b = self.sb("l1_qkvbf", [128, 4, 512], BF16, esQ)
                for tc in range(T // 512):
                    t0c = tc * 512
                    p = tc % 2
                    self.dma("sync", x2c[p][...],
                             xn2_scr[:, :, t0c:t0c + 512].rearrange("dc p t -> p dc t"),
                             reads=(xn2_scr_b,), writes=(x2c_b[p],))
                    for which, wsb, wsb_b, dscr, dscr_b in (
                            (0, wq_sb, wq_sb_b, q_scr, q_scr_b),
                            (1, wk_sb, wk_sb_b, k_scr, k_scr_b)):
                        for et in range(DC):
                            pb = psf[(et + which) % 3]
                            terms = [(wsb[:, dci, et * 128:(et + 1) * 128],
                                      x2c[p][:, dci, :]) for dci in range(DC)]
                            self.mm_group(pb[0][...], pb[1], terms,
                                          reads=(wsb_b, x2c_b[p]))
                            bf = qkv_bf[:, which, :]
                            self.act(bf, qkv_bf_b, pb[0][...], pb[1], AF.Identity)
                            self.dma("sync", dscr[et, :, t0c:t0c + 512], bf,
                                     reads=(qkv_bf_b,), writes=(dscr_b,))
                    for ti in range(4):
                        itt = tc * 4 + ti
                        for eh in range(2):
                            pb = psf[3 + eh]
                            terms = [(x2c[p][:, dci, ti * 128:(ti + 1) * 128],
                                      wv_sb[:, dci, eh * 512:(eh + 1) * 512])
                                     for dci in range(DC)]
                            self.mm_group(pb[0][...], pb[1], terms,
                                          reads=(x2c_b[p], wv_sb_b))
                            self.act(v_tm[:, itt, eh * 512:(eh + 1) * 512], v_tm_b,
                                     pb[0][...], pb[1], AF.Identity)
            self.close_scope(esQ)

            esC = contextlib.ExitStack()
            if True:
                kd, kd_b = self.sb("l1_k", [128, DC, T], BF16, esC)
                self.dma("sync", kd[...], k_scr.rearrange("dc p t -> p dc t"),
                         reads=(k_scr_b,), writes=(kd_b,))
                wo_sb, wo_sb_b = self.sb("l1_wo", [128, DC, D], BF16, esC)
                self.dma("gpsimd", wo_sb[...],
                         W["l1_WoT_dram"][0].rearrange("(dc p) e -> p dc e", p=128),
                         reads=(W["l1_WoT_dram"][1],), writes=(wo_sb_b,))
                B0_d, B0_d_b = W["l1_B0_dram"]
                B1_d, B1_d_b = W["l1_B1_dram"]
                far_d, far_d_b = W["l1_far_dram"]
                msk_d, msk_d_b = W["l1_maskdiag_dram"]
                B0p, B0p_b = self.sb("l1_B0p", [128, 16, 128], BF16, esC)
                B1p, B1p_b = self.sb("l1_B1p", [128, 16, 128], BF16, esC)
                farsb, farsb_b = self.sb("l1_farsb", [128, 16], F32, esC)
                msksb, msksb_b = self.sb("l1_mskd", [128, 128], F32, esC)
                self.dma("sync", farsb[...], far_d[...], reads=(far_d_b,),
                         writes=(farsb_b,))
                self.dma("sync", msksb[...], msk_d[...], reads=(msk_d_b,),
                         writes=(msksb_b,))
                es_bt = contextlib.ExitStack()
                if True:
                    btmp, btmp_b = self.sb("l1_btmp", [128, 16, 128], F32, es_bt)
                    self.dma("sync", btmp[...], B0_d.rearrange("h p f -> p h f"),
                             reads=(B0_d_b,), writes=(btmp_b,))
                    for h in range(16):
                        self.ts(btmp[:, h, :], btmp_b, btmp[:, h, :], btmp_b,
                                farsb[:, h:h + 1], 8.0, ALU.subtract, ALU.mult,
                                s_bufs=(farsb_b,))
                        self.tt(B0p[:, h, :], B0p_b, btmp[:, h, :], btmp_b,
                                msksb[...], msksb_b, ALU.add)
                    self.dma("sync", btmp[...], B1_d.rearrange("h p f -> p h f"),
                             reads=(B1_d_b,), writes=(btmp_b,))
                    for h in range(16):
                        self.ts(B1p[:, h, :], B1p_b, btmp[:, h, :], btmp_b,
                                farsb[:, h:h + 1], 8.0, ALU.subtract, ALU.mult,
                                s_bufs=(farsb_b,))
                self.close_scope(es_bt)

                qt, qt_b = [], []
                for k in range(2):
                    a, b = self.sb(f"l1_qt{k}", [128, DC, 128], BF16, esC)
                    qt.append(a); qt_b.append(b)
                PsbL, PsbL_b = [], []
                for k in range(2):
                    a, b = self.sb(f"l1_P{k}", [128, T], BF16, esC)
                    PsbL.append(a); PsbL_b.append(b)
                PtL, PtL_b = [], []
                for k in range(2):
                    a, b = self.sb(f"l1_Pt{k}", [128, 8, 128], BF16, esC)
                    PtL.append(a); PtL_b.append(b)
                ysb, ysb_b = self.sb("l1_ysb", [128, DC, 128], BF16, esC)
                xg, xg_b = [], []
                for k in range(4):
                    a, b = self.sb(f"l1_xg{k}", [128, D], F32, esC)
                    xg.append(a); xg_b.append(b)
                so, so_b = [], []
                for k in range(2):
                    a, b = self.sb(f"l1_so{k}", [128, D], F32, esC)
                    so.append(a); so_b.append(b)
                colsL, colsL_b = [], []
                for k in range(2):
                    a, b = self.sb(f"l1_acols{k}", [128, 32], F32, esC)
                    colsL.append(a); colsL_b.append(b)

                for it in range(NT):
                    p = it % 2
                    self.dma("sync", qt[p][...],
                             q_scr[:, :, it * 128:(it + 1) * 128]
                             .rearrange("dc p t -> p dc t"),
                             reads=(q_scr_b,), writes=(qt_b[p],))
                    self.dma("sync", xg[p][...], x_src_ap[it * 128:(it + 1) * 128, :],
                             reads=(x_src_buf,), writes=(xg_b[p],))
                    self.dma("sync", xg[2 + p][...], xn_scr[it * 128:(it + 1) * 128, :],
                             reads=(xn_scr_b,), writes=(xg_b[2 + p],))
                    jext = (it + 1) * 128
                    n_ch = ceil_div(jext, 512)

                    def softmax_part(h, p=p, it=it, jext=jext, n_ch=n_ch):
                        hp = h % 2
                        cols, cols_b = colsL[hp], colsL_b[hp]
                        Psb, Psb_b = PsbL[hp], PsbL_b[hp]
                        m_run = cols[:, 0:1]
                        bias_c = cols[:, 3:4]
                        lfin = cols[:, 4:5]
                        rec = cols[:, 5:6]
                        tmp = cols[:, 6:7]
                        corr = cols[:, 7:8]
                        dc_h = h // 2
                        pb_h = (h % 2) * 64
                        n_grp = ceil_div(n_ch, 4)
                        for g0 in range(n_grp):
                            ch0, ch1 = g0 * 4, min(n_ch, g0 * 4 + 4)
                            for ch in range(ch0, ch1):
                                c0 = ch * 512
                                cw = min(512, jext - c0)
                                pb = psf[ch - ch0]
                                q_ap = qt[p][pb_h:pb_h + 64, dc_h, :]
                                self.mm_group(pb[0][:, 0:cw], pb[1],
                                              [(q_ap, kd[pb_h:pb_h + 64, dc_h, c0:c0 + cw])],
                                              reads=(qt_b[p], kd_b))
                                jt_lo = c0 // 128
                                if jt_lo <= it < jt_lo + ceil_div(cw, 128):
                                    off = (it - jt_lo) * 128
                                    self.tt(pb[0][:, off:off + 128], pb[1],
                                            pb[0][:, off:off + 128], pb[1],
                                            B0p[:, h, :], B0p_b, ALU.add)
                                if it >= 1 and jt_lo <= it - 1 < jt_lo + ceil_div(cw, 128):
                                    off = (it - 1 - jt_lo) * 128
                                    self.tt(pb[0][:, off:off + 128], pb[1],
                                            pb[0][:, off:off + 128], pb[1],
                                            B1p[:, h, :], B1p_b, ALU.add)
                                self.vred(cols[:, 8 + ch - ch0:9 + ch - ch0], cols_b,
                                          pb[0][:, 0:cw], pb[1], ALU.max)
                            ng = ch1 - ch0
                            if ng == 1:
                                m_g = cols[:, 8:9]
                            else:
                                m_g = cols[:, 2:3]
                                self.vred(m_g, cols_b, cols[:, 8:8 + ng], cols_b, ALU.max)
                            if g0 == 0:
                                self.vcopy(m_run, cols_b, m_g, cols_b)
                            else:
                                self.tt(tmp, cols_b, m_run, cols_b, m_g, cols_b, ALU.max)
                                self.tt(corr, cols_b, m_run, cols_b, tmp, cols_b,
                                        ALU.subtract)
                                self.act(corr, cols_b, corr, cols_b, AF.Exp, scale=0.125)
                                self.ts(Psb[:, 0:ch0 * 512], Psb_b, Psb[:, 0:ch0 * 512],
                                        Psb_b, corr, None, ALU.mult, s_bufs=(cols_b,))
                                self.ts(lfin, cols_b, lfin, cols_b, corr, None, ALU.mult,
                                        s_bufs=(cols_b,))
                                self.vcopy(m_run, cols_b, tmp, cols_b)
                            self.ts(bias_c, cols_b, m_run, cols_b, -0.125, None, ALU.mult)
                            for ch in range(ch0, ch1):
                                c0 = ch * 512
                                cw = min(512, jext - c0)
                                pb = psf[ch - ch0]
                                self.act(Psb[:, c0:c0 + cw], Psb_b, pb[0][:, 0:cw], pb[1],
                                         AF.Exp, bias=bias_c, scale=0.125,
                                         accum=cols[:, 16 + ch - ch0:17 + ch - ch0],
                                         accum_buf=cols_b, extra_reads=(cols_b,))
                            lg = cols[:, 1:2]
                            if ng == 1:
                                self.vcopy(lg, cols_b, cols[:, 16:17], cols_b)
                            else:
                                self.vred(lg, cols_b, cols[:, 16:16 + ng], cols_b, ALU.add)
                            if g0 == 0:
                                self.vcopy(lfin, cols_b, lg, cols_b)
                            else:
                                self.tt(lfin, cols_b, lfin, cols_b, lg, cols_b, ALU.add)
                        self.recip(rec, cols_b, lfin, cols_b)
                        self.ts(Psb[:, 0:jext], Psb_b, Psb[:, 0:jext], Psb_b, rec, None,
                                ALU.mult, s_bufs=(cols_b,))

                    def pv_part(h, jext=jext):
                        hp = h % 2
                        Psb, Psb_b = PsbL[hp], PsbL_b[hp]
                        PtH, PtH_b = PtL[hp], PtL_b[hp]
                        dc_h = h // 2
                        pb_h = (h % 2) * 64
                        n_jt = jext // 128
                        pby = psf[4 + hp]
                        for jg in range(0, n_jt, 8):
                            je = min(n_jt, jg + 8)
                            for jj in range(jg, je):
                                self.transpose(pst[jj % 2][0][...], pst[jj % 2][1],
                                               Psb[:, jj * 128:(jj + 1) * 128], Psb_b,
                                               ident, ident_b)
                                self.vcopy(PtH[:, jj - jg, :], PtH_b,
                                           pst[jj % 2][0][...], pst[jj % 2][1])
                            for jj in range(jg, je):
                                fn = (lambda E, lt=v_tm[:, jj, h * 64:(h + 1) * 64],
                                      rh=PtH[:, jj - jg, :], o=pby[0][0:64, 0:128],
                                      st=(jj == 0), sp=(jj == n_jt - 1):
                                      E.matmul(o, lt, rh, start=st, stop=sp))
                                self.prog.emit("tensor", fn, reads=(v_tm_b, PtH_b),
                                               writes=(pby[1],), sync=(jj == n_jt - 1))
                        self.act(ysb[pb_h:pb_h + 64, dc_h, :], ysb_b,
                                 pby[0][0:64, 0:128], pby[1], AF.Identity)

                    for h in range(16):
                        softmax_part(h)
                        if h >= 1:
                            pv_part(h - 1)
                    pv_part(15)
                    for dh in range(2):
                        pb = psf[5]
                        terms = [(ysb[:, dci, :], wo_sb[:, dci, dh * 512:(dh + 1) * 512])
                                 for dci in range(DC)]
                        self.mm_group(pb[0][...], pb[1], terms, reads=(ysb_b, wo_sb_b))
                        self.tt(so[p][:, dh * 512:(dh + 1) * 512], so_b[p],
                                pb[0][...], pb[1], xg[2 + p][:, dh * 512:(dh + 1) * 512],
                                xg_b[2 + p], ALU.add)
                    if debug:
                        self.dma("sync", self.dram["l1_s_dump"][it * 128:(it + 1) * 128, :],
                                 so[p][...], reads=(so_b[p],),
                                 writes=(self._sdump_buf1,))
                    self.dma("sync", xg[2 + p][...], g_scr[it * 128:(it + 1) * 128, :],
                             reads=(g_scr_b,), writes=(xg_b[2 + p],))
                    self.tt(so[p][...], so_b[p], so[p][...], so_b[p], xg[2 + p][...],
                            xg_b[2 + p], ALU.mult)
                    self.tt(so[p][...], so_b[p], so[p][...], so_b[p], xg[p][...],
                            xg_b[p], ALU.add)
                    self.dma("sync", x_dst_ap[it * 128:(it + 1) * 128, :], so[p][...],
                             reads=(so_b[p],), writes=(x_dst_buf,))
            self.close_scope(esC)




    def build_layer2(self, W, x_src_ap, x_src_buf, x_dst_ap, x_dst_buf, consts,
                     debug=False):
        P = self.prog
        ident, ident_b = consts["ident"]
        psf = consts["psf"]
        pst = consts["pst"]
        EPS = 1.0e-6

        if debug:
            _, self._sdump_buf2 = self.dout("l2_s_dump", [T, D], F32)
        esL = contextlib.ExitStack()
        with esL:
            if debug:
                g_scr, g_scr_b = self.dout("l2_g_dump", [T, D], F32)
            else:
                g_scr, g_scr_b = self.dscratch("l2_g_scr", [T, D], F32)

            es_xn = contextlib.ExitStack()
            with es_xn:
                xn, xn_b = self.sb("l2_xn", [128, DC, T], BF16, es_xn)
                es_ln = contextlib.ExitStack()
                with es_ln:
                    self.ln_pass(x_src_ap, x_src_buf, xn, xn_b, ident, ident_b,
                                 [p[0] for p in pst], [p[1] for p in pst], es_ln, "l2ln")
                self.close_scope(es_ln)

                esG = contextlib.ExitStack()
                gates, gates_b = self.sb("l2_gates", [128, NT, 8], F32, esL, side="left")
                gates_dm, gates_dm_b = self.sb("l2_gates_dm", [8, T], BF16, esL, side="left")
                if True:
                    rw, rw_b = W["l2_Wr"]
                    rbt, rbt_b = W["l2_br_t"]
                    gt_tmp, gt_tmp_b = self.sb("l2_gt", [128, 4, 8], F32, esG)
                    gt_bf, gt_bf_b = self.sb("l2_gtbf", [128, 8], BF16, esG)
                    gcol, gcol_b = self.sb("l2_gcol", [128, 8], F32, esG)
                    for i in range(NT):
                        pb = psf[i % 2]
                        terms = [(xn[:, dc, i * 128:(i + 1) * 128], rw[:, dc, :])
                                 for dc in range(DC)]
                        self.mm_group(pb[0][:, 0:8], pb[1], terms, reads=(xn_b, rw_b))
                        t0 = gt_tmp[:, (i % 2) * 2, :]
                        t1 = gt_tmp[:, (i % 2) * 2 + 1, :]
                        self.tt(t0, gt_tmp_b, pb[0][:, 0:8], pb[1], rbt[...], rbt_b,
                                ALU.add)
                        c0 = (i % 2) * 4
                        m_ap = gcol[:, c0:c0 + 1]
                        self.vred(m_ap, gcol_b, t0, gt_tmp_b, ALU.max)
                        nm = gcol[:, c0 + 1:c0 + 2]
                        self.ts(nm, gcol_b, m_ap, gcol_b, -1.0, None, ALU.mult)
                        l_ap = gcol[:, c0 + 2:c0 + 3]
                        self.act(t1, gt_tmp_b, t0, gt_tmp_b, AF.Exp, bias=nm, scale=1.0,
                                 accum=l_ap, accum_buf=gcol_b, extra_reads=(gcol_b,))
                        r_ap = gcol[:, c0 + 3:c0 + 4]
                        self.recip(r_ap, gcol_b, l_ap, gcol_b)
                        self.ts(t1, gt_tmp_b, t1, gt_tmp_b, r_ap, None, ALU.mult,
                                s_bufs=(gcol_b,))
                        self.vred(m_ap, gcol_b, t1, gt_tmp_b, ALU.max)
                        self.ts(t0, gt_tmp_b, t1, gt_tmp_b, m_ap, None, ALU.is_equal,
                                s_bufs=(gcol_b,))
                        self.stt(t0, gt_tmp_b, t0, gt_tmp_b, NEG, None, t1,
                                 gt_tmp_b, ALU.mult, ALU.add)
                        self.vred(nm, gcol_b, t0, gt_tmp_b, ALU.max)
                        self.ts(gates[:, i, :], gates_b, t1, gt_tmp_b, nm, None,
                                ALU.is_ge, s_bufs=(gcol_b,))
                        self.tt(gates[:, i, :], gates_b, gates[:, i, :], gates_b, t1,
                                gt_tmp_b, ALU.mult)
                        self.vcopy(gt_bf[...], gt_bf_b, gates[:, i, :], gates_b)
                        self.transpose(pst[0][0][0:8, :], pst[0][1], gt_bf[...],
                                       gt_bf_b, ident, ident_b)
                        self.vcopy(gates_dm[:, i * 128:(i + 1) * 128], gates_dm_b,
                                   pst[0][0][0:8, :], pst[0][1])
                self.close_scope(esG)

                esF = contextlib.ExitStack()
                if True:
                    b2sb, b2sb_b = W["l2_b2"]
                    b1sb, b1sb_b = W["l2_b1"]
                    g_sb, g_sb_b = self.sb("l2_gq", [128, 16, 1024], F32, esF)
                    w1u, w1u_b, w2u, w2u_b, h_sb, h_sb_b = [], [], [], [], [], []
                    for k in range(2):
                        a, b = self.sb(f"l2_w1u{k}", [128, 8, 512], BF16, esF)
                        w1u.append(a); w1u_b.append(b)
                        a, b = self.sb(f"l2_w2u{k}", [128, 4, 1024], BF16, esF)
                        w2u.append(a); w2u_b.append(b)
                        a, b = self.sb(f"l2_h{k}", [128, 4, 512], BF16, esF)
                        h_sb.append(a); h_sb_b.append(b)
                    gsil, gsil_b = self.sb("l2_gsil", [128, 2, 1024], F32, esF)
                    W1d, W1d_b = W["l2_W1_dram"]
                    W2d, W2d_b = W["l2_W2_dram"]
                    for quarter in range(2):
                        tq0 = quarter * 16
                        for ti in range(16):
                            it = tq0 + ti
                            pb = psf[4 + (ti % 2)]
                            for dh in range(2):
                                self.mm_group(
                                    pb[0][...], pb[1],
                                    [(gates_dm[:, it * 128:(it + 1) * 128],
                                      b2sb[:, dh * 512:(dh + 1) * 512])],
                                    reads=(gates_dm_b, b2sb_b))
                                self.act(g_sb[:, ti, dh * 512:(dh + 1) * 512], g_sb_b,
                                         pb[0][...], pb[1], AF.Identity)
                        for e in range(8):
                            for fb in range(4):
                                u = e * 4 + fb
                                pw = u % 2
                                self.dma("gpsimd", w1u[pw][...],
                                         W1d[e, :, fb * 512:(fb + 1) * 512]
                                         .rearrange("(dc p) f -> p dc f", p=128),
                                         reads=(W1d_b,), writes=(w1u_b[pw],))
                                self.dma("gpsimd", w2u[pw][...],
                                         W2d[e, fb * 512:(fb + 1) * 512, :]
                                         .rearrange("(fc p) d -> p fc d", p=128),
                                         reads=(W2d_b,), writes=(w2u_b[pw],))
                                for tcp in range(2):
                                    for tc in (2 * tcp, 2 * tcp + 1):
                                        t0c = tq0 * 128 + tc * 512
                                        ph = tc % 2
                                        for ft in range(4):
                                            pb = psf[ft % 4]
                                            terms = [(w1u[pw][:, dc, ft * 128:(ft + 1) * 128],
                                                      xn[:, dc, t0c:t0c + 512])
                                                     for dc in range(DC)]
                                            self.mm_group(pb[0][...], pb[1], terms,
                                                          reads=(w1u_b[pw], xn_b))
                                            self.act(h_sb[ph][:, ft, :], h_sb_b[ph],
                                                     pb[0][...], pb[1], AF.Relu,
                                                     bias=b1sb[:, e, fb * 4 + ft:fb * 4 + ft + 1],
                                                     extra_reads=(b1sb_b,))
                                    for tc in (2 * tcp, 2 * tcp + 1):
                                        t0c = tq0 * 128 + tc * 512
                                        ph = tc % 2
                                        for ti in range(4):
                                            itt = (t0c // 128) + ti
                                            gq_ti = itt - tq0
                                            for dh in range(2):
                                                pb = psf[4 + ((ti * 2 + dh) % 2)]
                                                terms = [(h_sb[ph][:, fc, ti * 128:(ti + 1) * 128],
                                                          w2u[pw][:, fc, dh * 512:(dh + 1) * 512])
                                                         for fc in range(4)]
                                                self.mm_group(pb[0][...], pb[1], terms,
                                                              reads=(h_sb_b[ph], w2u_b[pw]))
                                                self.stt(g_sb[:, gq_ti, dh * 512:(dh + 1) * 512],
                                                         g_sb_b, pb[0][...], pb[1],
                                                         gates[:, itt, e:e + 1], gates_b,
                                                         g_sb[:, gq_ti, dh * 512:(dh + 1) * 512],
                                                         g_sb_b, ALU.mult, ALU.add)
                        for ti in range(16):
                            it = tq0 + ti
                            self.act(gsil[:, ti % 2, :], gsil_b, g_sb[:, ti, :], g_sb_b,
                                     AF.Silu)
                            self.dma("sync", g_scr[it * 128:(it + 1) * 128, :],
                                     gsil[:, ti % 2, :], reads=(gsil_b,),
                                     writes=(g_scr_b,))
                self.close_scope(esF)

                esA = contextlib.ExitStack()
                if True:
                    wk_sb, wk_sb_b = self.sb("l2_wk", [128, DC, D], BF16, esA)
                    wv_sb, wv_sb_b = self.sb("l2_wv", [128, DC, D], BF16, esA)
                    proj2, proj2_b = W["l2_proj2"]
                    onec, onec_b = W["l2_ones"]
                    self.dma("gpsimd", wk_sb[...],
                             W["l2_Wk_dram"][0].rearrange("(dc p) e -> p dc e", p=128),
                             reads=(W["l2_Wk_dram"][1],), writes=(wk_sb_b,))
                    self.dma("gpsimd", wv_sb[...],
                             W["l2_Wv_dram"][0].rearrange("(dc p) e -> p dc e", p=128),
                             reads=(W["l2_Wv_dram"][1],), writes=(wv_sb_b,))
                    kch, kch_b = self.sb("l2_kch", [128, DC, 512], BF16, esA)
                    vch, vch_b = self.sb("l2_vch", [128, 4, 1024], BF16, esA)
                    kfch, kfch_b = self.sb("l2_kfch", [128, 4, 1024], BF16, esA)
                    kv_ps = psf[4]
                    ks_ps = psf[5]
                    for tc in range(T // 512):
                        t0c = tc * 512
                        for et in range(DC):
                            pb = psf[et % 2]
                            terms = [(wk_sb[:, dci, et * 128:(et + 1) * 128],
                                      xn[:, dci, t0c:t0c + 512]) for dci in range(DC)]
                            self.mm_group(pb[0][...], pb[1], terms,
                                          reads=(wk_sb_b, xn_b))
                            self.act(kch[:, et, :], kch_b, pb[0][...], pb[1], AF.Identity)
                        for ti in range(4):
                            for eh in range(2):
                                pb = psf[2 + eh]
                                terms = [(xn[:, dci, t0c + ti * 128:t0c + (ti + 1) * 128],
                                          wv_sb[:, dci, eh * 512:(eh + 1) * 512])
                                         for dci in range(DC)]
                                self.mm_group(pb[0][...], pb[1], terms,
                                              reads=(xn_b, wv_sb_b))
                                self.act(vch[:, ti, eh * 512:(eh + 1) * 512], vch_b,
                                         pb[0][...], pb[1], AF.Identity)
                        for h in range(16):
                            b0 = (h % 2) * 64
                            dch = h // 2
                            for ti in range(4):
                                pb = psf[ti % 2]
                                self.mm_group(
                                    pb[0][0:128, 0:64], pb[1],
                                    [(kch[b0:b0 + 64, dch, ti * 128:(ti + 1) * 128],
                                      proj2[b0:b0 + 64, :])],
                                    reads=(kch_b, proj2_b))
                                self.ts(kfch[:, ti, h * 64:(h + 1) * 64], kfch_b,
                                        pb[0][0:128, 0:64], pb[1], EPS, EPS,
                                        ALU.add, ALU.max)
                        for h in range(16):
                            b0 = (h % 2) * 64
                            c0 = (h // 2) * 64
                            for ti in range(4):
                                first = (tc == 0 and ti == 0)
                                fn = (lambda E, lt=kfch[:, ti, h * 64:(h + 1) * 64],
                                      rh=vch[:, ti, h * 64:(h + 1) * 64],
                                      o=kv_ps[0][b0:b0 + 64, c0:c0 + 64], st=first:
                                      E.matmul(o, lt, rh, start=st, stop=True))
                                self.prog.emit("tensor", fn, reads=(kfch_b, vch_b),
                                               writes=(kv_ps[1],), sync=True)
                                fn2 = (lambda E, lt=kfch[:, ti, h * 64:(h + 1) * 64],
                                       rh=onec[...],
                                       o=ks_ps[0][b0:b0 + 64, h:h + 1], st=first:
                                       E.matmul(o, lt, rh, start=st, stop=True))
                                self.prog.emit("tensor", fn2, reads=(kfch_b, onec_b),
                                               writes=(ks_ps[1],), sync=True)
                    kv_sb, kv_sb_b = self.sb("l2_kv", [128, 512], BF16, esA)
                    ks_sb, ks_sb_b = self.sb("l2_ks", [128, 16], BF16, esA)
                    self.act(kv_sb[...], kv_sb_b, kv_ps[0][...], kv_ps[1], AF.Identity)
                    self.act(ks_sb[...], ks_sb_b, ks_ps[0][:, 0:16], ks_ps[1], AF.Identity)

                    wq_sb, wq_sb_b = wk_sb, wk_sb_b
                    self.dma("gpsimd", wq_sb[...],
                             W["l2_Wq_dram"][0].rearrange("(dc p) e -> p dc e", p=128),
                             reads=(W["l2_Wq_dram"][1],), writes=(wq_sb_b,))
                    wo_sb, wo_sb_b = wv_sb, wv_sb_b
                    self.dma("gpsimd", wo_sb[...],
                             W["l2_Wo_dram"][0].rearrange("(ec p) o -> p ec o", p=128),
                             reads=(W["l2_Wo_dram"][1],), writes=(wo_sb_b,))
                    bot_ap, bot_b = self.load_tile(W, "l2_bo_d", [128, 1024], F32, esA, "l2bott")
                    bot = bot_ap[...]
                    qch, qch_b = kch, kch_b
                    qfd, qfd_b = self.sb("l2_qfd", [128, 2, 512], BF16, esA)
                    otm, otm_b = vch, vch_b
                    odm, odm_b = self.sb("l2_odm", [128, DC, 128], BF16, esA)
                    dcol, dcol_b = self.sb("l2_dcol", [128, 8], F32, esA)
                    xg, xg_b = [], []
                    for k in range(4):
                        a, b = self.sb(f"l2_xg{k}", [128, D], F32, esA)
                        xg.append(a); xg_b.append(b)
                    so, so_b = [], []
                    for k in range(2):
                        a, b = self.sb(f"l2_so{k}", [128, D], F32, esA)
                        so.append(a); so_b.append(b)
                    for tc in range(T // 512):
                        t0c = tc * 512
                        for et in range(DC):
                            pb = psf[et % 2]
                            terms = [(wq_sb[:, dci, et * 128:(et + 1) * 128],
                                      xn[:, dci, t0c:t0c + 512]) for dci in range(DC)]
                            self.mm_group(pb[0][...], pb[1], terms,
                                          reads=(wq_sb_b, xn_b))
                            self.act(qch[:, et, :], qch_b, pb[0][...], pb[1], AF.Identity)
                        for h in range(16):
                            b0 = (h % 2) * 64
                            dch = h // 2
                            pb = psf[2 + h % 2]
                            self.mm_group(
                                pb[0][b0:b0 + 64, 0:512], pb[1],
                                [(proj2[b0:b0 + 64, :], qch[b0:b0 + 64, dch, :])],
                                reads=(proj2_b, qch_b))
                            self.ts(qfd[:, h % 2, :][b0:b0 + 64, :], qfd_b,
                                    pb[0][b0:b0 + 64, 0:512], pb[1], EPS, EPS,
                                    ALU.add, ALU.max)
                            for ti in range(4):
                                pbn = psf[4]
                                self.mm_group(
                                    pbn[0][0:128, 0:64], pbn[1],
                                    [(qfd[:, h % 2, ti * 128:(ti + 1) * 128][b0:b0 + 64, :],
                                      kv_sb[b0:b0 + 64, (h // 2) * 64:(h // 2) * 64 + 64])],
                                    reads=(qfd_b, kv_sb_b))
                                pbd = psf[5]
                                self.mm_group(
                                    pbd[0][0:128, 0:1], pbd[1],
                                    [(qfd[:, h % 2, ti * 128:(ti + 1) * 128][b0:b0 + 64, :],
                                      ks_sb[b0:b0 + 64, h:h + 1])],
                                    reads=(qfd_b, ks_sb_b))
                                dc_ap = dcol[:, ti * 2:ti * 2 + 1]
                                self.ts(dc_ap, dcol_b, pbd[0][0:128, 0:1], pbd[1],
                                        EPS, None, ALU.add)
                                rc_ap = dcol[:, ti * 2 + 1:ti * 2 + 2]
                                self.recip(rc_ap, dcol_b, dc_ap, dcol_b)
                                self.ts(otm[:, ti, h * 64:(h + 1) * 64], otm_b,
                                        pbn[0][0:128, 0:64], pbn[1], rc_ap, None,
                                        ALU.mult, s_bufs=(dcol_b,))
                        for ti in range(4):
                            it = tc * 4 + ti
                            p = it % 2
                            self.dma("sync", xg[p][...],
                                     x_src_ap[it * 128:(it + 1) * 128, :],
                                     reads=(x_src_buf,), writes=(xg_b[p],))
                            self.dma("sync", xg[2 + p][...],
                                     g_scr[it * 128:(it + 1) * 128, :],
                                     reads=(g_scr_b,), writes=(xg_b[2 + p],))
                            for dc in range(DC):
                                pb2 = pst[dc % 2]
                                self.transpose(pb2[0][...], pb2[1],
                                               otm[:, ti, dc * 128:(dc + 1) * 128],
                                               otm_b, ident, ident_b)
                                self.vcopy(odm[:, dc, :], odm_b, pb2[0][...], pb2[1])
                            for dh in range(2):
                                pb = psf[dh % 2]
                                terms = [(odm[:, dci, :],
                                          wo_sb[:, dci, dh * 512:(dh + 1) * 512])
                                         for dci in range(DC)]
                                self.mm_group(pb[0][...], pb[1], terms,
                                              reads=(odm_b, wo_sb_b))
                                self.tt(so[p][:, dh * 512:(dh + 1) * 512], so_b[p],
                                        pb[0][...], pb[1],
                                        bot[:, dh * 512:(dh + 1) * 512], bot_b, ALU.add)
                            if debug:
                                self.dma("sync",
                                         self.dram["l2_s_dump"][it * 128:(it + 1) * 128, :],
                                         so[p][...], reads=(so_b[p],),
                                         writes=(self._sdump_buf2,))
                            self.tt(so[p][...], so_b[p], so[p][...], so_b[p],
                                    xg[2 + p][...], xg_b[2 + p], ALU.mult)
                            self.tt(so[p][...], so_b[p], so[p][...], so_b[p],
                                    xg[p][...], xg_b[p], ALU.add)
                            self.dma("sync", x_dst_ap[it * 128:(it + 1) * 128, :],
                                     so[p][...], reads=(so_b[p],), writes=(x_dst_buf,))
                self.close_scope(esA)
            self.close_scope(es_xn)




    def ln_pass_affine(self, x_src_ap, x_src_buf, out_dmaj, out_dmaj_buf,
                      gam_ap, gam_b, bet_ap, bet_b, ident, ident_b, pst, es_ph, tag):
        xt, xt_b, xc, xc_b, xnb, xnb_b = [], [], [], [], [], []
        for k in range(2):
            a, b = self.sb(f"{tag}_xt{k}", [128, 1024], F32, es_ph); xt.append(a); xt_b.append(b)
            a, b = self.sb(f"{tag}_xc{k}", [128, 1024], F32, es_ph); xc.append(a); xc_b.append(b)
            a, b = self.sb(f"{tag}_xnb{k}", [128, 1024], BF16, es_ph); xnb.append(a); xnb_b.append(b)
        colsL, colsL_b = [], []
        for k in range(2):
            a, b = self.sb(f"{tag}_cols{k}", [128, 8], F32, es_ph)
            colsL.append(a); colsL_b.append(b)
        for i in range(NT):
            p = i % 2
            cols, cols_b = colsL[p], colsL_b[p]
            self.dma("sync", xt[p][...], x_src_ap[i * 128:(i + 1) * 128, :],
                     reads=(x_src_buf,), writes=(xt_b[p],))
            c0 = 0
            s_ap = cols[:, c0:c0 + 1]
            self.vred(s_ap, cols_b, xt[p][...], xt_b[p], ALU.add)
            nm_ap = cols[:, c0 + 1:c0 + 2]
            self.ts(nm_ap, cols_b, s_ap, cols_b, -1.0 / D, None, ALU.mult)
            self.ts(xc[p][...], xc_b[p], xt[p][...], xt_b[p], nm_ap, None, ALU.add,
                    s_bufs=(cols_b,))
            v_ap = cols[:, c0 + 2:c0 + 3]
            self.act(xt[p][...], xt_b[p], xc[p][...], xc_b[p], AF.Square,
                     accum=v_ap, accum_buf=cols_b)
            r1 = cols[:, c0 + 3:c0 + 4]
            self.act(r1, cols_b, v_ap, cols_b, AF.Ln, bias=1.0e-5, scale=1.0 / D)
            self.act(r1, cols_b, r1, cols_b, AF.Exp, scale=-0.5)
            self.ts(xc[p][...], xc_b[p], xc[p][...], xc_b[p], r1, None, ALU.mult,
                    s_bufs=(cols_b,))
            self.tt(xc[p][...], xc_b[p], xc[p][...], xc_b[p], gam_ap, gam_b, ALU.mult)
            self.tt(xc[p][...], xc_b[p], xc[p][...], xc_b[p], bet_ap, bet_b, ALU.add)
            self.vcopy(xnb[p][...], xnb_b[p], xc[p][...], xc_b[p])
            for dc in range(DC):
                pb = dc % 2
                self.transpose(pst[pb][0][...], pst[pb][1],
                               xnb[p][:, dc * 128:(dc + 1) * 128], xnb_b[p],
                               ident, ident_b)
                self.vcopy(out_dmaj[:, dc, i * 128:(i + 1) * 128], out_dmaj_buf,
                           pst[pb][0][...], pst[pb][1])

    def build_layer3(self, W, x_src_ap, x_src_buf, x_dst_ap, x_dst_buf, consts,
                     debug=False):
        P = self.prog
        ident, ident_b = consts["ident"]
        identf, identf_b = consts["identf"]
        psf = consts["psf"]
        pst = consts["pst"]

        if debug:
            _, self._sdump_buf3 = self.dout("l3_s_dump", [T, D], F32)
        esL = contextlib.ExitStack()
        with esL:
            if debug:
                g_scr, g_scr_b = self.dout("l3_g_dump", [T, D], F32)
            else:
                g_scr, g_scr_b = self.dscratch("l3_g_scr", [T, D], F32)
            xn_scr, xn_scr_b = self.dscratch("l3_xn_scr", [T, D], F32)
            h1_scr, h1_scr_b = self.dscratch("l3_h1_scr", [DC, 128, T + 8], BF16)
            x2_scr, x2_scr_b = self.dscratch("l3_x2_scr", [T, D], F32)
            x3_scr, x3_scr_b = self.dscratch("l3_x3_scr", [T, D], F32)

            es_xn = contextlib.ExitStack()
            with es_xn:
                xn, xn_b = self.sb("l3_xn", [128, DC, T], BF16, es_xn)
                es_ln = contextlib.ExitStack()
                with es_ln:
                    zpad, zpad_b = self.sb("l3_zpad", [128, DC, 8], BF16, es_ln)
                    self.prog.emit("vector", lambda E: E.memset(zpad[...], 0.0),
                                   writes=(zpad_b,))
                    self.dma("sync", h1_scr[:, :, 0:8].rearrange("dc p t -> p dc t"),
                             zpad[...], reads=(zpad_b,), writes=(h1_scr_b,))
                    cg, cg_b = self.load_tile(W, "l3_cg_d", [128, 1024], F32, es_ln, "l3cgt")
                    cb, cb_b = self.load_tile(W, "l3_cb_d", [128, 1024], F32, es_ln, "l3cbt")
                    self.ln_pass_dual(x_src_ap, x_src_buf, xn, xn_b,
                                      xn_scr, xn_scr_b,
                                      h1_scr[:, :, 8:8 + T], h1_scr_b,
                                      cg[...], cg_b, cb[...], cb_b,
                                      ident, ident_b, pst, es_ln, "l3ln")
                self.close_scope(es_ln)

                esG = contextlib.ExitStack()
                gates, gates_b = self.sb("l3_gates", [128, NT, 8], F32, esL, side="left")
                sigc, sigc_b = self.sb("l3_sig", [128, NT], F32, esL, side="left")
                if True:
                    rw, rw_b = W["l3_rw"]
                    sgw, sgw_b = W["l3_sgw"]
                    gt_tmp, gt_tmp_b = self.sb("l3_gt", [128, 4, 8], F32, esG)
                    gcol, gcol_b = self.sb("l3_gcol", [128, 8], F32, esG)
                    for i in range(NT):
                        pb = psf[i % 2]
                        terms = [(xn[:, dc, i * 128:(i + 1) * 128], rw[:, dc, :])
                                 for dc in range(DC)]
                        self.mm_group(pb[0][:, 0:8], pb[1], terms, reads=(xn_b, rw_b))
                        t0 = gt_tmp[:, (i % 2) * 2, :]
                        t1 = gt_tmp[:, (i % 2) * 2 + 1, :]
                        c0 = (i % 2) * 4
                        m_ap = gcol[:, c0:c0 + 1]
                        self.vred(m_ap, gcol_b, pb[0][:, 0:8], pb[1], ALU.max)
                        nm = gcol[:, c0 + 1:c0 + 2]
                        self.ts(nm, gcol_b, m_ap, gcol_b, -1.0, None, ALU.mult)
                        l_ap = gcol[:, c0 + 2:c0 + 3]
                        self.act(t1, gt_tmp_b, pb[0][:, 0:8], pb[1], AF.Exp, bias=nm,
                                 scale=1.0, accum=l_ap, accum_buf=gcol_b,
                                 extra_reads=(gcol_b,))
                        r_ap = gcol[:, c0 + 3:c0 + 4]
                        self.recip(r_ap, gcol_b, l_ap, gcol_b)
                        self.ts(t1, gt_tmp_b, t1, gt_tmp_b, r_ap, None, ALU.mult,
                                s_bufs=(gcol_b,))
                        self.vred(m_ap, gcol_b, t1, gt_tmp_b, ALU.max)
                        self.ts(t0, gt_tmp_b, t1, gt_tmp_b, m_ap, None, ALU.is_equal,
                                s_bufs=(gcol_b,))
                        self.stt(t0, gt_tmp_b, t0, gt_tmp_b, NEG, None, t1,
                                 gt_tmp_b, ALU.mult, ALU.add)
                        self.vred(nm, gcol_b, t0, gt_tmp_b, ALU.max)
                        self.ts(gates[:, i, :], gates_b, t1, gt_tmp_b, nm, None,
                                ALU.is_ge, s_bufs=(gcol_b,))
                        self.tt(gates[:, i, :], gates_b, gates[:, i, :], gates_b, t1,
                                gt_tmp_b, ALU.mult)
                        pbs = psf[2]
                        terms = [(xn[:, dc, i * 128:(i + 1) * 128], sgw[:, dc:dc + 1])
                                 for dc in range(DC)]
                        self.mm_group(pbs[0][:, 0:1], pbs[1], terms, reads=(xn_b, sgw_b))
                        self.act(sigc[:, i:i + 1], sigc_b, pbs[0][:, 0:1], pbs[1],
                                 AF.Sigmoid)
                self.close_scope(esG)

                esS = contextlib.ExitStack()
                if True:
                    sgu_u, sgu_u_b = [], []
                    for k in range(2):
                        a, b = self.sb(f"l3_sgu{k}", [128, 8, 1024], BF16, esS)
                        sgu_u.append(a); sgu_u_b.append(b)
                    sdw, sdw_b = self.sb("l3_sdw", [128, 8, 1024], BF16, esS)
                    self.dma("gpsimd", sdw[...],
                             W["l3_sdown_dram"][0].rearrange("(fc p) d -> p fc d", p=128),
                             reads=(W["l3_sdown_dram"][1],), writes=(sdw_b,))
                    for k in range(2):
                        self.dma("gpsimd", sgu_u[k][...],
                                 W["l3_sgu_dram"][0][:, k, :, :]
                                 .rearrange("(dc p) g f -> p dc (g f)", p=128),
                                 reads=(W["l3_sgu_dram"][1],), writes=(sgu_u_b[k],))
                    sh1, sh1_b = self.sb("l3_sh1", [128, 2, 512], BF16, esS)
                    sh1d, sh1d_b = self.sb("l3_sh1d", [128, 8, 128], BF16, esS)
                    shg, shg_b = self.sb("l3_shg", [128, 512], F32, esS)
                    sht, sht_b = self.sb("l3_sht", [128, 2, 1024], F32, esS)
                    for it in range(NT):
                        tt0 = it * 128
                        for u in range(2):
                            for half in range(2):
                                pb = psf[u * 2 + half]
                                terms = [(xn[:, dc, tt0:tt0 + 128],
                                          sgu_u[u][:, dc, half * 512:(half + 1) * 512])
                                         for dc in range(DC)]
                                self.mm_group(pb[0][...], pb[1], terms,
                                              reads=(xn_b, sgu_u_b[u]))
                            self.act(shg[...], shg_b, psf[u * 2][0][...],
                                     psf[u * 2][1], AF.Silu)
                            self.tt(sh1[:, u, :], sh1_b, shg[...], shg_b,
                                    psf[u * 2 + 1][0][...], psf[u * 2 + 1][1],
                                    ALU.mult)
                        for fc in range(DC):
                            pb2 = pst[fc % 2]
                            u, off = fc // 4, (fc % 4) * 128
                            self.transpose(pb2[0][...], pb2[1],
                                           sh1[:, u, off:off + 128], sh1_b,
                                           ident, ident_b)
                            self.vcopy(sh1d[:, fc, :], sh1d_b, pb2[0][...], pb2[1])
                        for dh in range(2):
                            pb = psf[4 + dh]
                            terms = [(sh1d[:, fci, :],
                                      sdw[:, fci, dh * 512:(dh + 1) * 512])
                                     for fci in range(DC)]
                            self.mm_group(pb[0][...], pb[1], terms,
                                          reads=(sh1d_b, sdw_b))
                            self.ts(sht[:, it % 2, dh * 512:(dh + 1) * 512], sht_b,
                                    pb[0][...], pb[1], sigc[:, it:it + 1], None,
                                    ALU.mult, s_bufs=(sigc_b,))
                        self.dma("sync", g_scr[tt0:tt0 + 128, :], sht[:, it % 2, :],
                                 reads=(sht_b,), writes=(g_scr_b,))
                self.close_scope(esS)

                esF = contextlib.ExitStack()
                if True:
                    g_sb, g_sb_b = self.sb("l3_gq", [128, 16, 1024], F32, esF)
                    w1u, w1u_b, w2u, w2u_b, h_sb, h_sb_b = [], [], [], [], [], []
                    for k in range(2):
                        a, b = self.sb(f"l3_w1u{k}", [128, 8, 512], BF16, esF)
                        w1u.append(a); w1u_b.append(b)
                        a, b = self.sb(f"l3_w2u{k}", [128, 4, 1024], BF16, esF)
                        w2u.append(a); w2u_b.append(b)
                        a, b = self.sb(f"l3_h{k}", [128, 4, 512], BF16, esF)
                        h_sb.append(a); h_sb_b.append(b)
                    gsil3, gsil3_b = self.sb("l3_gsil", [128, 2, 1024], F32, esF)
                    W1d, W1d_b = W["l3_w1_dram"]
                    W2d, W2d_b = W["l3_w2_dram"]
                    for quarter in range(2):
                        tq0 = quarter * 16
                        for ti in range(16):
                            it = tq0 + ti
                            self.dma("sync", g_sb[:, ti, :],
                                     g_scr[it * 128:(it + 1) * 128, :],
                                     reads=(g_scr_b,), writes=(g_sb_b,))
                        for e in range(8):
                            for fb in range(2):
                                u = e * 2 + fb
                                pw = u % 2
                                self.dma("gpsimd", w1u[pw][...],
                                         W1d[e, :, fb * 512:(fb + 1) * 512]
                                         .rearrange("(dc p) f -> p dc f", p=128),
                                         reads=(W1d_b,), writes=(w1u_b[pw],))
                                self.dma("gpsimd", w2u[pw][...],
                                         W2d[e, fb * 512:(fb + 1) * 512, :]
                                         .rearrange("(fc p) d -> p fc d", p=128),
                                         reads=(W2d_b,), writes=(w2u_b[pw],))
                                for tcp in range(2):
                                    for tc in (2 * tcp, 2 * tcp + 1):
                                        t0c = tq0 * 128 + tc * 512
                                        ph = tc % 2
                                        for ft in range(4):
                                            pb = psf[ft % 4]
                                            terms = [(w1u[pw][:, dc, ft * 128:(ft + 1) * 128],
                                                      xn[:, dc, t0c:t0c + 512])
                                                     for dc in range(DC)]
                                            self.mm_group(pb[0][...], pb[1], terms,
                                                          reads=(w1u_b[pw], xn_b))
                                            self.act(h_sb[ph][:, ft, :], h_sb_b[ph],
                                                     pb[0][...], pb[1], AF.Silu)
                                    for tc in (2 * tcp, 2 * tcp + 1):
                                        t0c = tq0 * 128 + tc * 512
                                        ph = tc % 2
                                        for ti in range(4):
                                            itt = (t0c // 128) + ti
                                            gq_ti = itt - tq0
                                            for dh in range(2):
                                                pb = psf[4 + ((ti * 2 + dh) % 2)]
                                                terms = [(h_sb[ph][:, fc, ti * 128:(ti + 1) * 128],
                                                          w2u[pw][:, fc, dh * 512:(dh + 1) * 512])
                                                         for fc in range(4)]
                                                self.mm_group(pb[0][...], pb[1], terms,
                                                              reads=(h_sb_b[ph], w2u_b[pw]))
                                                self.stt(g_sb[:, gq_ti, dh * 512:(dh + 1) * 512],
                                                         g_sb_b, pb[0][...], pb[1],
                                                         gates[:, itt, e:e + 1], gates_b,
                                                         g_sb[:, gq_ti, dh * 512:(dh + 1) * 512],
                                                         g_sb_b, ALU.mult, ALU.add)
                        for ti in range(16):
                            it = tq0 + ti
                            self.act(gsil3[:, ti % 2, :], gsil3_b, g_sb[:, ti, :], g_sb_b,
                                     AF.Silu)
                            self.dma("sync", g_scr[it * 128:(it + 1) * 128, :],
                                     gsil3[:, ti % 2, :], reads=(gsil3_b,),
                                     writes=(g_scr_b,))
                self.close_scope(esF)
            self.close_scope(es_xn)

            esCV = contextlib.ExitStack()
            if True:
                wcv, wcv_b = self.sb("l3_wcv", [128, 64, 128], BF16, esCV)
                self.dma("gpsimd", wcv[...],
                         W["l3_convw_dram"][0].rearrange("k o c p f -> p (k o c) f"),
                         reads=(W["l3_convw_dram"][1],), writes=(wcv_b,))
                cbs, cbs_b = W["l3_cbias"]
                h1c, h1c_b = [], []
                for k in range(2):
                    a, b = self.sb(f"l3_h1c{k}", [128, DC, 520], BF16, esCV)
                    h1c.append(a); h1c_b.append(b)
                cvd, cvd_b = self.sb("l3_cvd", [128, DC, 512], F32, esCV)
                x2t, x2t_b = [], []
                for k in range(2):
                    a, b = self.sb(f"l3_x2t{k}", [128, 1024], F32, esCV)
                    x2t.append(a); x2t_b.append(b)
                for tc in range(T // 512):
                    t0c = tc * 512
                    p = tc % 2
                    self.dma("sync", h1c[p][:, :, 0:519],
                             h1_scr[:, :, t0c + 1:t0c + 520]
                             .rearrange("dc p t -> p dc t"),
                             reads=(h1_scr_b,), writes=(h1c_b[p],))
                    for oc8 in range(8):
                        g = oc8 // 2
                        pb = psf[oc8 % 4]
                        terms = []
                        for k in range(4):
                            for cic in range(2):
                                widx = k * 16 + oc8 * 2 + cic
                                terms.append((wcv[:, widx, :],
                                              h1c[p][:, g * 2 + cic, 4 + k:4 + k + 512]))
                        self.mm_group(pb[0][...], pb[1], terms, reads=(wcv_b, h1c_b[p]))
                        self.act(cvd[:, oc8, :], cvd_b, pb[0][...], pb[1], AF.Identity,
                                 bias=cbs[:, oc8:oc8 + 1], extra_reads=(cbs_b,))
                    for ti in range(4):
                        it = tc * 4 + ti
                        self.dma("sync", x2t[p][...],
                                 xn_scr[it * 128:(it + 1) * 128, :],
                                 reads=(xn_scr_b,), writes=(x2t_b[p],))
                        for oc8 in range(8):
                            pb2 = psf[4 + oc8 % 2]
                            self.transpose(pb2[0][:, 0:128], pb2[1],
                                           cvd[:, oc8, ti * 128:(ti + 1) * 128], cvd_b,
                                           identf, identf_b)
                            self.tt(x2t[p][:, oc8 * 128:(oc8 + 1) * 128], x2t_b[p],
                                    x2t[p][:, oc8 * 128:(oc8 + 1) * 128], x2t_b[p],
                                    pb2[0][:, 0:128], pb2[1], ALU.add)
                        self.dma("sync", x2_scr[it * 128:(it + 1) * 128, :],
                                 x2t[p][...], reads=(x2t_b[p],), writes=(x2_scr_b,))
            self.close_scope(esCV)

            es4 = contextlib.ExitStack()
            if True:
                h2, h2_b = self.sb("l3_h2", [128, DC, T], BF16, es4)
                es_ln2 = contextlib.ExitStack()
                with es_ln2:
                    ag, ag_b = self.load_tile(W, "l3_ag_d", [128, 1024], F32, es_ln2, "l3agt")
                    ab, ab_b = self.load_tile(W, "l3_ab_d", [128, 1024], F32, es_ln2, "l3abt")
                    self.ln_pass_affine(x2_scr, x2_scr_b, h2, h2_b,
                                        ag[...], ag_b, ab[...], ab_b,
                                        ident, ident_b, pst, es_ln2, "l3ln2")
                self.close_scope(es_ln2)
                ones_sc, ones_sc_b = self.sb("l3_ones", [128, T], F32, es4)
                self.prog.emit("vector", lambda E: E.memset(ones_sc[...], 1.0),
                               writes=(ones_sc_b,))
                isc, isc_b = self.sb("l3_isc", [128, T], F32, es4)
                self.dma("sync", isc[...], W["l3_iscale_dram"][0][...],
                         reads=(W["l3_iscale_dram"][1],), writes=(isc_b,))
                qb_, qb_b = W["l3_bq"]
                kb_, kb_b = W["l3_bk"]
                vb_, vb_b = W["l3_bv"]
                wq3, wq3_b = self.sb("l3_wq3", [128, DC, 128], BF16, es4)
                wk3, wk3_b = self.sb("l3_wk3", [128, DC, 128], BF16, es4)
                wv3, wv3_b = self.sb("l3_wv3", [128, DC, 128], BF16, es4)
                qs, qs_b = self.sb("l3_qs", [128, T], F32, es4)
                ks, ks_b = self.sb("l3_ks", [128, T], F32, es4)
                vs, vs_b = self.sb("l3_vs", [128, T], F32, es4)
                tmps, tmps_b = self.sb("l3_tmps", [128, T], F32, es4)
                x2cb, x2cb_b = self.sb("l3_x2cb", [128, 32, 128], F32, es4)
                for dc in range(DC):
                    self.dma("gpsimd", wq3[...],
                             W["l3_WqT_dram"][0][:, dc * 128:(dc + 1) * 128]
                             .rearrange("(dci p) e -> p dci e", p=128),
                             reads=(W["l3_WqT_dram"][1],), writes=(wq3_b,))
                    self.dma("gpsimd", wk3[...],
                             W["l3_WkT_dram"][0][:, dc * 128:(dc + 1) * 128]
                             .rearrange("(dci p) e -> p dci e", p=128),
                             reads=(W["l3_WkT_dram"][1],), writes=(wk3_b,))
                    self.dma("gpsimd", wv3[...],
                             W["l3_WvT_dram"][0][:, dc * 128:(dc + 1) * 128]
                             .rearrange("(dci p) e -> p dci e", p=128),
                             reads=(W["l3_WvT_dram"][1],), writes=(wv3_b,))
                    for tc in range(DC):
                        t0c = tc * 512
                        for which, wsb, wsb_b, dst, dst_b, bias_t, bias_tb in (
                                (0, wq3, wq3_b, qs, qs_b, qb_, qb_b),
                                (1, wk3, wk3_b, ks, ks_b, kb_, kb_b),
                                (2, wv3, wv3_b, vs, vs_b, vb_, vb_b)):
                            pb = psf[which % 4]
                            terms = [(wsb[:, dci, :], h2[:, dci, t0c:t0c + 512])
                                     for dci in range(DC)]
                            self.mm_group(pb[0][...], pb[1], terms, reads=(wsb_b, h2_b))
                            self.act(dst[:, t0c:t0c + 512], dst_b, pb[0][...], pb[1],
                                     AF.Identity, bias=bias_t[:, dc:dc + 1],
                                     extra_reads=(bias_tb,))
                    self.prog.emit("vector",
                                   lambda E, o=tmps[...], d0=ones_sc[...], d1=qs[...]:
                                   E.tensor_tensor_scan(o, d0, d1, 0.0, ALU.mult, ALU.add),
                                   reads=(ones_sc_b, qs_b), writes=(tmps_b,))
                    self.tt(tmps[...], tmps_b, tmps[...], tmps_b, isc[...], isc_b,
                            ALU.mult)
                    self.tt(tmps[...], tmps_b, tmps[...], tmps_b, ks[...], ks_b,
                            ALU.mult)
                    self.prog.emit("vector",
                                   lambda E, o=qs[...], d0=ones_sc[...], d1=tmps[...]:
                                   E.tensor_tensor_scan(o, d0, d1, 0.0, ALU.mult, ALU.add),
                                   reads=(ones_sc_b, tmps_b), writes=(qs_b,))
                    self.tt(qs[...], qs_b, qs[...], qs_b, isc[...], isc_b, ALU.mult)
                    self.tt(qs[...], qs_b, qs[...], qs_b, vs[...], vs_b, ALU.mult)
                    self.dma("sync", x2cb[...],
                             x2_scr[:, dc * 128:(dc + 1) * 128]
                             .rearrange("(tt p) o -> p tt o", p=128),
                             reads=(x2_scr_b,), writes=(x2cb_b,))
                    for ti in range(NT):
                        pb2 = psf[4 + ti % 2]
                        self.transpose(pb2[0][:, 0:128], pb2[1],
                                       qs[:, ti * 128:(ti + 1) * 128], qs_b,
                                       identf, identf_b)
                        self.tt(x2cb[:, ti, :], x2cb_b, x2cb[:, ti, :], x2cb_b,
                                pb2[0][:, 0:128], pb2[1], ALU.add)
                    self.dma("sync",
                             x3_scr[:, dc * 128:(dc + 1) * 128]
                             .rearrange("(tt p) o -> p tt o", p=128),
                             x2cb[...], reads=(x2cb_b,), writes=(x3_scr_b,))
            self.close_scope(es4)

            es5 = contextlib.ExitStack()
            if True:
                h3, h3_b = self.sb("l3_h3", [128, DC, T], BF16, es5)
                es_ln3 = contextlib.ExitStack()
                with es_ln3:
                    bg, bg_b = self.load_tile(W, "l3_bg_d", [128, 1024], F32, es_ln3, "l3bgt")
                    bb2, bb2_b = self.load_tile(W, "l3_bb_d", [128, 1024], F32, es_ln3, "l3bbt")
                    self.ln_pass_affine(x3_scr, x3_scr_b, h3, h3_b,
                                        bg[...], bg_b, bb2[...], bb2_b,
                                        ident, ident_b, pst, es_ln3, "l3ln3")
                self.close_scope(es_ln3)
                y_sb, y_sb_b = self.sb("l3_yq", [128, 16, 1024], F32, es5)
                w1u, w1u_b, w2u, w2u_b, h_sb, h_sb_b = [], [], [], [], [], []
                for k in range(2):
                    a, b = self.sb(f"l3_bw1u{k}", [128, 8, 512], BF16, es5)
                    w1u.append(a); w1u_b.append(b)
                    a, b = self.sb(f"l3_bw2u{k}", [128, 4, 1024], BF16, es5)
                    w2u.append(a); w2u_b.append(b)
                    a, b = self.sb(f"l3_bh{k}", [128, 4, 512], BF16, es5)
                    h_sb.append(a); h_sb_b.append(b)
                but, but_b = self.load_tile(W, "l3_bu_d", [128, 1024], F32, es5, "l3but")
                but = but[...]
                bbr, bbr_b = W["l3_bbias"]
                WbT, WbT_b = W["l3_WbT_dram"]
                WuT, WuT_b = W["l3_WuT_dram"]
                xg, xg_b = [], []
                for k in range(4):
                    a, b = self.sb(f"l3_xg{k}", [128, D], F32, es5)
                    xg.append(a); xg_b.append(b)
                so, so_b = [], []
                for k in range(2):
                    a, b = self.sb(f"l3_so{k}", [128, D], F32, es5)
                    so.append(a); so_b.append(b)
                NFB = 8
                for quarter in range(2):
                    tq0 = quarter * 16
                    for fb in range(NFB):
                        pw = fb % 2
                        self.dma("gpsimd", w1u[pw][...],
                                 WbT[:, fb * 512:(fb + 1) * 512]
                                 .rearrange("(dc p) f -> p dc f", p=128),
                                 reads=(WbT_b,), writes=(w1u_b[pw],))
                        self.dma("gpsimd", w2u[pw][...],
                                 WuT[fb * 512:(fb + 1) * 512, :]
                                 .rearrange("(fc p) d -> p fc d", p=128),
                                 reads=(WuT_b,), writes=(w2u_b[pw],))
                        for tcp in range(2):
                            for tc in (2 * tcp, 2 * tcp + 1):
                                t0c = tq0 * 128 + tc * 512
                                ph = tc % 2
                                for ft in range(4):
                                    pb = psf[ft % 4]
                                    terms = [(w1u[pw][:, dc, ft * 128:(ft + 1) * 128],
                                              h3[:, dc, t0c:t0c + 512])
                                             for dc in range(DC)]
                                    self.mm_group(pb[0][...], pb[1], terms,
                                                  reads=(w1u_b[pw], h3_b))
                                    self.act(h_sb[ph][:, ft, :], h_sb_b[ph],
                                             pb[0][...], pb[1], AF.Gelu,
                                             bias=bbr[:, fb * 4 + ft:fb * 4 + ft + 1],
                                             extra_reads=(bbr_b,))
                            for tc in (2 * tcp, 2 * tcp + 1):
                                t0c = tq0 * 128 + tc * 512
                                ph = tc % 2
                                for ti in range(4):
                                    itt = (t0c // 128) + ti
                                    yq_ti = itt - tq0
                                    for dh in range(2):
                                        pb = psf[4 + ((ti * 2 + dh) % 2)]
                                        terms = [(h_sb[ph][:, fc, ti * 128:(ti + 1) * 128],
                                                  w2u[pw][:, fc, dh * 512:(dh + 1) * 512])
                                                 for fc in range(4)]
                                        self.mm_group(pb[0][...], pb[1], terms,
                                                      reads=(h_sb_b[ph], w2u_b[pw]))
                                        dst = y_sb[:, yq_ti, dh * 512:(dh + 1) * 512]
                                        if fb == 0:
                                            self.act(dst, y_sb_b, pb[0][...], pb[1],
                                                     AF.Identity)
                                        else:
                                            self.tt(dst, y_sb_b, dst, y_sb_b,
                                                    pb[0][...], pb[1], ALU.add)
                    for ti in range(16):
                        it = tq0 + ti
                        p = it % 2
                        self.dma("sync", xg[p][...],
                                 x_src_ap[it * 128:(it + 1) * 128, :],
                                 reads=(x_src_buf,), writes=(xg_b[p],))
                        self.dma("sync", xg[2 + p][...],
                                 x3_scr[it * 128:(it + 1) * 128, :],
                                 reads=(x3_scr_b,), writes=(xg_b[2 + p],))
                        self.tt(so[p][...], so_b[p], y_sb[:, ti, :], y_sb_b,
                                but[...], but_b, ALU.add)
                        self.tt(so[p][...], so_b[p], so[p][...], so_b[p],
                                xg[2 + p][...], xg_b[2 + p], ALU.add)
                        if debug:
                            self.dma("sync",
                                     self.dram["l3_s_dump"][it * 128:(it + 1) * 128, :],
                                     so[p][...], reads=(so_b[p],),
                                     writes=(self._sdump_buf3,))
                        self.dma("sync", xg[2 + p][...],
                                 g_scr[it * 128:(it + 1) * 128, :],
                                 reads=(g_scr_b,), writes=(xg_b[2 + p],))
                        self.tt(so[p][...], so_b[p], so[p][...], so_b[p],
                                xg[2 + p][...], xg_b[2 + p], ALU.mult)
                        self.tt(so[p][...], so_b[p], so[p][...], so_b[p],
                                xg[p][...], xg_b[p], ALU.add)
                        self.dma("sync", x_dst_ap[it * 128:(it + 1) * 128, :],
                                 so[p][...], reads=(so_b[p],), writes=(x_dst_buf,))
            self.close_scope(es5)





def prep_l0_host(inp, cp, consts_np):
    W = {}
    nc = cp.nc

    def dram(name, arr, dtype=F32):
        ap, buf = cp.din(name, list(arr.shape), dtype)
        cp.inputs[name] = np.ascontiguousarray(arr, dtype=np.float32)
        return ap, buf

    def sbuf_const(name, arr, sb_dtype):
        ap, buf = dram(name, arr)
        s_ap, s_buf = cp.sb("s_" + name, list(arr.shape), sb_dtype)
        cp.dma("gpsimd", s_ap[...], ap[...], reads=(buf,), writes=(s_buf,))
        return s_ap, s_buf

    W["l0_wqT"] = dram("l0_wqT", inp["L0_attn_wq"].T)
    W["l0_wkT"] = dram("l0_wkT", inp["L0_attn_wk"].T)
    W["l0_wvT"] = dram("l0_wvT", inp["L0_attn_wv"].T)
    W["l0_W1_dram"] = dram("l0_W1", inp["L0_ffn_W1"])
    W["l0_W2_dram"] = dram("l0_W2", inp["L0_ffn_W2"])
    W["cos_d"] = dram("c_cos_d", consts_np["cos_d"])
    W["sin_d"] = dram("c_sin_d", consts_np["sin_d"])
    W["mask4"] = dram("c_mask4", consts_np["mask4"])
    rw = inp["L0_ffn_router_W"].reshape(8, 128, 8).transpose(1, 0, 2)
    W["l0_rw_d"] = dram("l0_rw", rw)
    rbt = np.tile(inp["L0_ffn_router_b"][None, :], (128, 1))
    W["l0_rbt_d"] = dram("l0_rbt", rbt)
    b1r = inp["L0_ffn_b1"].reshape(8, 32, 128).transpose(2, 0, 1)
    W["l0_b1_d"] = dram("l0_b1r", b1r)
    W["l0_b2_d"] = dram("l0_b2", inp["L0_ffn_b2"])
    return W

def prep_l1_host(inp, cp):
    W = {}

    def dram(name, arr, dtype=F32):
        ap, buf = cp.din(name, list(arr.shape), dtype)
        cp.inputs[name] = np.ascontiguousarray(arr, dtype=np.float32)
        return ap, buf

    def sbuf_const(name, arr, sb_dtype):
        ap, buf = dram(name, arr)
        s_ap, s_buf = cp.sb("s_" + name, list(arr.shape), sb_dtype)
        cp.dma("gpsimd", s_ap[...], ap[...], reads=(buf,), writes=(s_buf,))
        return s_ap, s_buf

    W["l1_WqT_dram"] = dram("l1_WqT", inp["L1_attn_Wq"].T)
    W["l1_WkT_dram"] = dram("l1_WkT", inp["L1_attn_Wk"].T)
    W["l1_WvT_dram"] = dram("l1_WvT", inp["L1_attn_Wv"].T)
    W["l1_WoT_dram"] = dram("l1_WoT", inp["L1_attn_Wo"].T)
    W["l1_WfcT_dram"] = dram("l1_WfcT", inp["L1_ffn_W_fc"].T)
    W["l1_WprT_dram"] = dram("l1_WprT", inp["L1_ffn_W_proj"].T)
    W["l1_gam_d"] = dram("l1_gam", np.tile(inp["L1_attn_ln_gamma"][None, :], (128, 1)))
    W["l1_bet_d"] = dram("l1_bet", np.tile(inp["L1_attn_ln_beta"][None, :], (128, 1)))
    rel = inp["L1_attn_rel"]
    ii = np.arange(128)[:, None]
    jj = np.arange(128)[None, :]
    idx0 = np.clip(0 + ii - jj, 0, 128)
    idx1 = np.clip(128 + ii - jj, 0, 128)
    B0 = rel[:, idx0]
    B1 = rel[:, idx1]
    W["l1_B0_dram"] = dram("l1_B0", B0)
    W["l1_B1_dram"] = dram("l1_B1", B1)
    far = np.tile(rel[:, 128][None, :], (128, 1))
    W["l1_far_dram"] = dram("l1_far", far)
    mskd = np.triu(np.full((128, 128), NEG, np.float32), k=1)
    W["l1_maskdiag_dram"] = dram("l1_maskdiag", mskd)
    return W


def prep_l2_host(inp, cp):
    W = {}

    def dram(name, arr, dtype=F32):
        ap, buf = cp.din(name, list(arr.shape), dtype)
        cp.inputs[name] = np.ascontiguousarray(arr, dtype=np.float32)
        return ap, buf

    def sbuf_const(name, arr, sb_dtype):
        ap, buf = dram(name, arr)
        s_ap, s_buf = cp.sb("s_" + name, list(arr.shape), sb_dtype)
        cp.dma("gpsimd", s_ap[...], ap[...], reads=(buf,), writes=(s_buf,))
        return s_ap, s_buf

    W["l2_Wq_dram"] = dram("l2_Wq", inp["L2_attn_Wq"])
    W["l2_Wk_dram"] = dram("l2_Wk", inp["L2_attn_Wk"])
    W["l2_Wv_dram"] = dram("l2_Wv", inp["L2_attn_Wv"])
    W["l2_Wo_dram"] = dram("l2_Wo", inp["L2_attn_Wo"])
    W["l2_W1_dram"] = dram("l2_W1", inp["L2_ffn_W1"])
    W["l2_W2_dram"] = dram("l2_W2", inp["L2_ffn_W2"])
    W["l2_proj2"] = sbuf_const("l2_proj2",
                               np.concatenate([inp["L2_attn_proj"]] * 2, axis=0), BF16)
    W["l2_ones"] = sbuf_const("l2_onesc", np.ones((128, 1), np.float32), BF16)
    rw = inp["L2_ffn_Wr"].reshape(8, 128, 8).transpose(1, 0, 2)
    W["l2_Wr"] = sbuf_const("l2_rw", rw, BF16)
    W["l2_br_t"] = sbuf_const("l2_rbt", np.tile(inp["L2_ffn_br"][None, :], (128, 1)), F32)
    b1r = inp["L2_ffn_b1"].reshape(8, 16, 128).transpose(2, 0, 1)
    W["l2_b1"] = sbuf_const("l2_b1r", b1r, F32)
    W["l2_b2"] = sbuf_const("l2_b2", inp["L2_ffn_b2"], BF16)
    W["l2_bo_d"] = dram("l2_bot", np.tile(inp["L2_attn_bo"][None, :], (128, 1)))
    return W


def prep_l3_host(inp, cp):
    W = {}

    def dram(name, arr, dtype=F32):
        ap, buf = cp.din(name, list(arr.shape), dtype)
        cp.inputs[name] = np.ascontiguousarray(arr, dtype=np.float32)
        return ap, buf

    def sbuf_const(name, arr, sb_dtype):
        ap, buf = dram(name, arr)
        s_ap, s_buf = cp.sb("s_" + name, list(arr.shape), sb_dtype)
        cp.dma("gpsimd", s_ap[...], ap[...], reads=(buf,), writes=(s_buf,))
        return s_ap, s_buf

    W["l3_WqT_dram"] = dram("l3_WqT", inp["L3_attn_Wq"].T)
    W["l3_WkT_dram"] = dram("l3_WkT", inp["L3_attn_Wk"].T)
    W["l3_WvT_dram"] = dram("l3_WvT", inp["L3_attn_Wv"].T)
    W["l3_WbT_dram"] = dram("l3_WbT", inp["L3_attn_Wb"].T)
    W["l3_WuT_dram"] = dram("l3_WuT", inp["L3_attn_Wu"].T)
    W["l3_w1_dram"] = dram("l3_w1", inp["L3_ffn_w1"])
    W["l3_w2_dram"] = dram("l3_w2", inp["L3_ffn_w2"])
    W["l3_sdown_dram"] = dram("l3_sdown", inp["L3_ffn_shared_down"])
    sgu = inp["L3_ffn_shared_gate_up"]
    sgu_u = np.stack([
        np.stack([sgu[:, u * 512:(u + 1) * 512], sgu[:, 1024 + u * 512:1024 + (u + 1) * 512]],
                 axis=1)
        for u in range(2)], axis=1)
    W["l3_sgu_dram"] = dram("l3_sgu", sgu_u)
    cw = inp["L3_attn_conv_w"]
    blocks = np.zeros((4, 8, 2, 128, 128), np.float32)
    for k in range(4):
        for oc8 in range(8):
            g = oc8 // 2
            for cic in range(2):
                blk = cw[oc8 * 128:(oc8 + 1) * 128, cic * 128:(cic + 1) * 128, k]
                blocks[k, oc8, cic] = blk.T
    W["l3_convw_dram"] = dram("l3_convw", blocks)
    W["l3_cg_d"] = dram("l3_cg", np.tile(inp["L3_attn_ln_conv_g"][None, :], (128, 1)))
    W["l3_cb_d"] = dram("l3_cb", np.tile(inp["L3_attn_ln_conv_b"][None, :], (128, 1)))
    W["l3_ag_d"] = dram("l3_ag", np.tile(inp["L3_attn_ln_attn_g"][None, :], (128, 1)))
    W["l3_ab_d"] = dram("l3_ab", np.tile(inp["L3_attn_ln_attn_b"][None, :], (128, 1)))
    W["l3_bg_d"] = dram("l3_bg", np.tile(inp["L3_attn_ln_boom_g"][None, :], (128, 1)))
    W["l3_bb_d"] = dram("l3_bb", np.tile(inp["L3_attn_ln_boom_b"][None, :], (128, 1)))
    W["l3_bu_d"] = dram("l3_bu", np.tile(inp["L3_attn_bu"][None, :], (128, 1)))
    W["l3_cbias"] = sbuf_const("l3_cbv", inp["L3_attn_conv_b"].reshape(8, 128).T, F32)
    W["l3_bq"] = sbuf_const("l3_bqv", inp["L3_attn_bq"].reshape(8, 128).T, F32)
    W["l3_bk"] = sbuf_const("l3_bkv", inp["L3_attn_bk"].reshape(8, 128).T, F32)
    W["l3_bv"] = sbuf_const("l3_bvv", inp["L3_attn_bv"].reshape(8, 128).T, F32)
    W["l3_bbias"] = sbuf_const("l3_bbv", inp["L3_attn_bb"].reshape(32, 128).T, F32)
    rw = inp["L3_ffn_router_w"].reshape(8, 128, 8).transpose(1, 0, 2)
    W["l3_rw"] = sbuf_const("l3_rwv", rw, BF16)
    W["l3_sgw"] = sbuf_const("l3_sgwv", inp["L3_ffn_shared_gate_w"].reshape(8, 128).T, BF16)
    isc = (1.0 / np.arange(1, T + 1, dtype=np.float64)).astype(np.float32)
    W["l3_iscale_dram"] = dram("l3_iscale", np.tile(isc[None, :], (128, 1)))
    return W



from concourse.bass_utils import run_bass_kernel_spmd


def _build_full(inp):
    cp = CoreProgram()
    consts, c_np = cp.setup_consts()
    W0 = prep_l0_host(inp, cp, c_np)
    W1 = prep_l1_host(inp, cp)
    W2 = prep_l2_host(inp, cp)
    W3 = prep_l3_host(inp, cp)
    x_in_ap, x_in_b = cp.din("x_in", [T, D], F32)
    xA_ap, xA_b = cp.dscratch("xA", [T, D], F32)
    xB_ap, xB_b = cp.dscratch("xB", [T, D], F32)
    xout_ap, xout_b = cp.dout("x_out", [T, D], F32)

    cp.build_layer0(W0, x_in_ap, x_in_b, xA_ap, xA_b, consts)
    cp.prog.global_barrier()
    cp.build_layer1(W1, xA_ap, xA_b, xB_ap, xB_b, consts)
    cp.prog.global_barrier()
    cp.build_layer2(W2, xB_ap, xB_b, xA_ap, xA_b, consts)
    cp.prog.global_barrier()
    cp.build_layer3(W3, xA_ap, xA_b, xout_ap, xout_b, consts)
    nc = cp.finalize()
    return nc, cp


def kernel(**inputs):
    inp = {k: np.asarray(v) for k, v in inputs.items()}
    nc, cp = _build_full(inp)
    x_full = np.ascontiguousarray(inp["L0_attn_x"], dtype=np.float32)
    in_maps = []
    for core in range(8):
        m = dict(cp.inputs)
        m["x_in"] = np.ascontiguousarray(x_full[core])
        in_maps.append(m)
    res = run_bass_kernel_spmd(nc, in_maps, core_ids=list(range(8)))
    out = np.stack([np.asarray(r["x_out"]) for r in res.results]).astype(np.float32)
    return out
